# version 01
# baseline (speedup 1.0000x reference)
import numpy as np
import ml_dtypes
import concourse.bass as bass
import concourse.mybir as mybir
from concourse.bass_utils import run_bass_kernel_spmd

F32 = mybir.dt.float32
BF16 = mybir.dt.bfloat16
AF = mybir.ActivationFunctionType
ALU = mybir.AluOpType
AX = mybir.AxisListType
BF = ml_dtypes.bfloat16

D = 1024
NCH = 8
B = 2
T = 2048
TOK = B * T
TT = 512
NTILE = TOK // TT
NCORES = 8
LNEPS = 1e-5


def host_consts():
    c = {}
    c["ONESB"] = np.ones((128, 512), dtype=BF)
    c["ONESF"] = np.ones((128, 512), dtype=np.float32)
    c["I128F"] = np.eye(128, dtype=np.float32)
    c["I16F"] = np.eye(16, dtype=np.float32)
    e2 = np.zeros((128, NCH, 16), dtype=BF)
    for cc in range(NCH):
        e2[0:64, cc, 2 * cc] = 1
        e2[64:128, cc, 2 * cc + 1] = 1
    c["E2SEL"] = e2
    e2b = np.zeros((16, NCH, 128), dtype=BF)
    for cc in range(NCH):
        e2b[2 * cc, cc, 0:64] = 1
        e2b[2 * cc + 1, cc, 64:128] = 1
    c["E2B"] = e2b
    e8 = np.zeros((128, 4, 8), dtype=BF)
    for cl in range(4):
        e8[0:64, cl, 2 * cl] = 1
        e8[64:128, cl, 2 * cl + 1] = 1
    c["E8SEL"] = e8
    m = np.zeros((128, 128), dtype=np.float32)
    m[np.triu_indices(128, 1)] = -1e30
    c["MASKTRI"] = m
    DIM_HEAD, WINDOW = 64, 256
    half = np.arange(0, DIM_HEAD, 2, dtype=np.float32)
    inv_freq = 1.0 / 10000.0 ** (half / DIM_HEAD)
    t = np.arange(T, dtype=np.float32)
    freqs = t[:, None] * inv_freq[None, :]
    freqs = np.concatenate([freqs, freqs], -1)
    base_scale = (half + 0.4 * DIM_HEAD) / (1.4 * DIM_HEAD)
    power = (t - T // 2) / (WINDOW // 2)
    scale = base_scale[None, :] ** power[:, None]
    scale = np.concatenate([scale, scale], -1)
    cos, sin = np.cos(freqs), np.sin(freqs)

    def u_of(ss):
        u = np.empty_like(ss)
        u[0:32] = ss[32:64]
        u[32:64] = ss[0:32]
        return u
    cs_q = (cos * scale).T.astype(np.float32)
    ss_q = (sin * scale).T.astype(np.float32)
    inv = (1.0 / scale)
    cs_k = (cos * inv).T.astype(np.float32)
    ss_k = (sin * inv).T.astype(np.float32)
    c["ROT_CSQ"] = np.vstack([cs_q, cs_q]).astype(BF)
    c["ROT_UQ"] = np.vstack([u_of(ss_q), u_of(ss_q)]).astype(BF)
    c["ROT_CSK"] = np.vstack([cs_k, cs_k]).astype(BF)
    c["ROT_UK"] = np.vstack([u_of(ss_k), u_of(ss_k)]).astype(BF)
    c["I128B"] = np.eye(128, dtype=BF)
    e8b = np.zeros((8, 4, 128), dtype=BF)
    for cl in range(4):
        e8b[2 * cl, cl, 0:64] = 1
        e8b[2 * cl + 1, cl, 64:128] = 1
    c["E8B"] = e8b
    R64 = np.zeros((64, 64), dtype=np.float32)
    for y in range(32):
        R64[y, y + 32] = -1.0
    for y in range(32, 64):
        R64[y, y - 32] = 1.0
    RT = np.zeros((128, 128), dtype=BF)
    RT[0:64, 0:64] = R64.T.astype(BF)
    RT[64:128, 64:128] = R64.T.astype(BF)
    c["R128T"] = RT

    def band_mask(iq0, jl0):
        mm = np.full((128, 384), -1e30, dtype=np.float32)
        for i in range(128):
            lo, hi = iq0 + i, iq0 + i + 256
            jl = jl0 + np.arange(384)
            mm[i, (jl >= lo) & (jl <= hi)] = 0.0
        return mm
    c["MSK_SUB0"] = band_mask(0, 0)
    c["MSK_SUB1"] = band_mask(128, 128)
    m0s0 = np.full((128, 128), -1e30, dtype=np.float32)
    m0s0[np.tril_indices(128)] = 0.0
    c["MSK_W0S0"] = m0s0
    m0s1 = np.full((128, 256), -1e30, dtype=np.float32)
    for i in range(128):
        m0s1[i, : 129 + i] = 0.0
    c["MSK_W0S1"] = m0s1
    return c


CONSTS = host_consts()


def new_nc():
    nc = bass.Bass("TRN2", target_bir_lowering=False)
    return nc


def declare_consts(nc, names=None):
    aps = {}
    for k, v in CONSTS.items():
        if names is not None and k not in names:
            continue
        dt = BF16 if v.dtype == BF else F32
        aps[k] = nc.dram_tensor(k, list(v.shape), dt, kind="ExternalInput").ap()
    return aps


def load_consts(nc, caps):
    sb = {}
    for k, ap in caps.items():
        v = CONSTS[k]
        dt = BF16 if v.dtype == BF else F32
        sb[k] = nc.alloc_sbuf_tensor(f"c_{k}", list(v.shape), dt)
    sb["EPS5"] = nc.alloc_sbuf_tensor("c_EPS5", [128, 1], F32)
    sb["EPS6"] = nc.alloc_sbuf_tensor("c_EPS6", [128, 1], F32)
    sem = nc.semaphore("c_ld").__enter__()
    with nc.Block() as blk:
        @blk.sync
        def _(sy):
            n = 0
            for k in caps:
                sy.dma_start(out=sb[k].ap(), in_=caps[k]).then_inc(sem, 16)
                n += 1
            sy.wait_ge(sem, 16 * n)
        @blk.vector
        def _(v):
            v.memset(sb["EPS5"].ap(), LNEPS)
            v.memset(sb["EPS6"].ap(), 1e-6)
    nc.const_aps.register(F32, LNEPS, sb["EPS5"].ap())
    nc.const_aps.register(F32, 1e-6, sb["EPS6"].ap())
    return sb


def bc_ap(h, nch, tt):
    return bass.AP(h, 0, [[tt, 128], [0, nch], [1, tt]])


def x_dram_tile(xap, t0, tt):
    return xap[:, :, t0:t0 + tt].rearrange("c p t -> p c t")


def emit_lnwrap(nc, csb, y_sb, x_sb, sq_sb, rowA, rowB, rowC,
                prS1, prS2, pbc0, pbc1, tmp_sb, extra_row=None):
    ONESB, ONESF = csb["ONESB"], csb["ONESF"]
    with nc.Block() as blk:
        @blk.scalar
        def _(s):
            s.activation(sq_sb.ap(), y_sb.ap(), AF.Square)
    with nc.Block() as blk:
        @blk.tensor
        def _(te):
            for c in range(NCH):
                te.matmul(prS1, ONESB.ap()[:, 0:1], y_sb.ap()[:, c, :],
                          start=(c == 0), stop=(c == NCH - 1))
            for c in range(NCH):
                te.matmul(prS2, ONESB.ap()[:, 0:1], sq_sb.ap()[:, c, :],
                          start=(c == 0), stop=(c == NCH - 1))
    with nc.Block() as blk:
        @blk.scalar
        def _(s):
            s.activation(rowA[:], prS1, AF.Copy, scale=1.0 / D)
            s.activation(rowB[:], rowA[:], AF.Square)
            s.activation(rowC[:], prS2, AF.Copy, scale=1.0 / D)
    with nc.Block() as blk:
        @blk.vector
        def _(v):
            v.tensor_sub(rowC[:], rowC[:], rowB[:])
    with nc.Block() as blk:
        @blk.scalar
        def _(s):
            s.activation(rowB[:], rowC[:], AF.Ln, bias=csb["EPS5"].ap()[0:1, :])
            s.activation(rowB[:], rowB[:], AF.Exp, scale=-0.5)
    with nc.Block() as blk:
        @blk.vector
        def _(v):
            v.tensor_mul(rowC[:], rowA[:], rowB[:])
            v.drain()
            v.tensor_scalar_mul(rowC[:], rowC[:], -1.0)
            if extra_row is not None:
                v.drain()
                v.tensor_add(rowC[:], rowC[:], extra_row)
    with nc.Block() as blk:
        @blk.tensor
        def _(te):
            te.matmul(pbc0[:], ONESF.ap()[0:1, 0:128], rowB[:], start=True, stop=True)
            te.matmul(pbc1[:], ONESF.ap()[0:1, 0:128], rowC[:], start=True, stop=True)
    with nc.Block() as blk:
        @blk.vector
        def _(v):
            v.tensor_tensor(tmp_sb.ap(), y_sb.ap(), bc_ap(pbc0, NCH, TT),
                            ALU.mult)
    with nc.Block() as blk:
        @blk.vector
        def _(v):
            v.tensor_tensor(tmp_sb.ap(), tmp_sb.ap(), bc_ap(pbc1, NCH, TT),
                            ALU.add)
    with nc.Block() as blk:
        @blk.vector
        def _(v):
            v.tensor_add(x_sb.ap(), tmp_sb.ap(), x_sb.ap())



def build_prep():
    nc = new_nc()
    caps = declare_consts(nc, ["ONESF"])
    xin = nc.dram_tensor("xin", [NCH, 128, TOK], F32, kind="ExternalInput").ap()
    xout = nc.dram_tensor("xout", [NCH, 128, TOK], F32, kind="ExternalOutput").ap()
    m0out = nc.dram_tensor("m0out", [1, TOK], F32, kind="ExternalOutput").ap()
    csb = load_consts(nc, caps)
    ONESF = csb["ONESF"]
    with (
        nc.sbuf_tensor("xt", [128, NCH, TT], F32) as xt,
        nc.sbuf_tensor("mrow", [1, TT], F32) as mrow,
        nc.sbuf_tensor("xo", [128, NCH, TT], F32) as xo,
        nc.psum_tensor("ps1", [1, TT], F32) as ps1,
        nc.psum_tensor("pmb", [128, TT], F32) as pmb,
        nc.semaphore("ld") as ld,
        nc.semaphore("st") as st,
    ):
        for i in range(NTILE):
            t0 = i * TT
            with nc.Block() as blk:
                @blk.sync
                def _(sy):
                    sy.dma_start(out=xt[:], in_=x_dram_tile(xin, t0, TT)).then_inc(ld, 16)
                    sy.wait_ge(ld, 16 * (i + 1))
            with nc.Block() as blk:
                @blk.tensor
                def _(te):
                    for c in range(NCH):
                        te.matmul(ps1[:], ONESF.ap()[:, 0:1], xt[:, c, :],
                                  start=(c == 0), stop=(c == NCH - 1))
            with nc.Block() as blk:
                @blk.scalar
                def _(s):
                    s.activation(mrow[:], ps1[:], AF.Copy, scale=-1.0 / D)
            with nc.Block() as blk:
                @blk.tensor
                def _(te):
                    te.matmul(pmb[:], ONESF.ap()[0:1, 0:128], mrow[:], start=True, stop=True)
            with nc.Block() as blk:
                @blk.vector
                def _(v):
                    v.tensor_tensor(xo[:], xt[:], bc_ap(pmb, NCH, TT), ALU.add)
            with nc.Block() as blk:
                @blk.scalar
                def _(s):
                    s.activation(mrow[:], mrow[:], AF.Copy, scale=-1.0)
            with nc.Block() as blk:
                @blk.sync
                def _(sy):
                    sy.dma_start(out=x_dram_tile(xout, t0, TT), in_=xo[:]).then_inc(st, 16)
                    sy.dma_start(out=m0out[:, t0:t0 + TT], in_=mrow[:]).then_inc(st, 16)
                    sy.wait_ge(st, 32 * (i + 1))
    return nc



def build_A0():
    nc = new_nc()
    caps = declare_consts(nc, ["ONESB", "ONESF"])
    xin = nc.dram_tensor("xin", [NCH, 128, TOK], F32, kind="ExternalInput").ap()
    xout = nc.dram_tensor("xout", [NCH, 128, TOK], F32, kind="ExternalOutput").ap()
    wnames = ["Wq", "Wk", "Wv", "Wo"]
    wd = {w: nc.dram_tensor(w, [D, D], F32, kind="ExternalInput").ap() for w in wnames}
    bnames = ["bq", "bk", "bv", "bo"]
    bd = {b: nc.dram_tensor(b, [1, D], F32, kind="ExternalInput").ap() for b in bnames}
    csb = load_consts(nc, caps)
    ONESB, ONESF = csb["ONESB"], csb["ONESF"]

    wsb = {w: nc.alloc_sbuf_tensor(f"w_{w}", [128, NCH, D], BF16) for w in wnames}
    bsb = {b: nc.alloc_sbuf_tensor(f"b_{b}", [1, D], BF16) for b in bnames}
    wldsem = nc.semaphore("wld").__enter__()
    with nc.Block() as blk:
        @blk.gpsimd
        def _(g):
            n = 0
            for w in wnames:
                g.dma_start(out=wsb[w].ap(),
                            in_=wd[w].rearrange("(c p) o -> p c o", p=128)
                            ).then_inc(wldsem, 16)
                n += 1
            for b in bnames:
                g.dma_start(out=bsb[b].ap(), in_=bd[b]).then_inc(wldsem, 16)
                n += 1
            g.wait_ge(wldsem, 16 * n)

    if True:
        xt = nc.alloc_sbuf_tensor("xt", [128, NCH, TT], F32)
        sqb = nc.alloc_sbuf_tensor("sqb", [128, NCH, TT], BF16)
        rrow = nc.alloc_sbuf_tensor("rrow", [1, TT], F32)
        rowB = nc.alloc_sbuf_tensor("rowB", [1, TT], F32)
        rowC = nc.alloc_sbuf_tensor("rowC", [1, TT], F32)
        xb = nc.alloc_sbuf_tensor("xb", [128, NCH, TT], BF16)
        qt = nc.alloc_sbuf_tensor("qt", [128, NCH, TT], BF16)
        qsw = nc.alloc_sbuf_tensor("qsw", [128, NCH, TT], BF16)
        kt = nc.alloc_sbuf_tensor("kt", [128, NCH, TT], BF16)
        vt = nc.alloc_sbuf_tensor("vt", [128, NCH, TT], BF16)
        pbuf = nc.alloc_sbuf_tensor("pbuf", [128, 2048], BF16)
        zpart = nc.alloc_sbuf_tensor("zpart", [128, 4], F32)
        ztot = nc.alloc_sbuf_tensor("ztot", [128, 1], F32)
        invzf = nc.alloc_sbuf_tensor("invzf", [128, 1], F32)
        invz = nc.alloc_sbuf_tensor("invz", [128, 1], BF16)
        r4row = nc.alloc_sbuf_tensor("r4row", [1, 2048], F32)
        outb = nc.alloc_sbuf_tensor("outb", [128, NCH, TT], BF16)
        yb = nc.alloc_sbuf_tensor("yb", [128, NCH, TT], BF16)
        tmp1 = nc.alloc_sbuf_tensor("tmp1", [128, NCH, TT], F32)
        pg0 = nc.alloc_psum_tensor("pg0", [128, TT], F32)
        pg1 = nc.alloc_psum_tensor("pg1", [128, TT], F32)
        pg2 = nc.alloc_psum_tensor("pg2", [128, TT], F32)
        pg3 = nc.alloc_psum_tensor("pg3", [128, TT], F32)
        pr = nc.alloc_psum_tensor("pr", [1, 2048], F32)
        ld = nc.semaphore("ld").__enter__()
        st = nc.semaphore("st").__enter__()
        mmS = nc.semaphore("mm").__enter__()
        evS = nc.semaphore("ev").__enter__()
        vzS = nc.semaphore("vz").__enter__()
        rrS = nc.semaphore("rr").__enter__()
        pgs = [pg0, pg1, pg2, pg3]
        mmc, evc, vzc, rrc = [0], [0], [0], [0]

        for i in range(NTILE):
            t0 = i * TT
            with nc.Block() as blk:
                @blk.sync
                def _(sy):
                    sy.dma_start(out=xt[:], in_=x_dram_tile(xin, t0, TT)).then_inc(ld, 16)
                    sy.wait_ge(ld, 48 * i + 16)
            with nc.Block() as blk:
                @blk.scalar
                def _(s):
                    s.activation(sqb[:], xt[:], AF.Square)
            with nc.Block() as blk:
                @blk.tensor
                def _(te):
                    for c in range(NCH):
                        te.matmul(pr[:, 0:TT], ONESB.ap()[:, 0:1], sqb[:, c, :],
                                  start=(c == 0), stop=(c == NCH - 1))
            with nc.Block() as blk:
                @blk.scalar
                def _(s):
                    s.activation(rrow[:], pr[:, 0:TT], AF.Ln,
                                 bias=csb["EPS5"].ap()[0:1, :], scale=1.0 / D)
                    s.activation(rrow[:], rrow[:], AF.Exp, scale=-0.5)
            with nc.Block() as blk:
                @blk.tensor
                def _(te):
                    te.matmul(pg0[:], ONESF.ap()[0:1, 0:128], rrow[:], start=True, stop=True)
            with nc.Block() as blk:
                @blk.vector
                def _(v):
                    v.tensor_tensor(xb[:], xt[:], bc_ap(pg0, NCH, TT), ALU.mult)
            sched = [(w, dst, b, oc) for (w, dst, b) in
                     (("Wq", qt, "bq"), ("Wk", kt, "bk"), ("Wv", vt, "bv"))
                     for oc in range(NCH)]
            MM0, EV0 = mmc[0], evc[0]
            with nc.Block() as blk:
                @blk.tensor
                def _(te):
                    for idx, (w, dst, b, oc) in enumerate(sched):
                        pb = pgs[idx % 2]
                        if idx >= 2:
                            te.wait_ge(evS, EV0 + idx - 1)
                        for c in range(NCH):
                            te.matmul(pb[:], wsb[w].ap()[:, c, oc * 128:(oc + 1) * 128],
                                      xb[:, c, :], start=(c == 0), stop=False)
                        te.matmul(pb[:], bsb[b].ap()[:, oc * 128:(oc + 1) * 128],
                                  ONESB.ap()[0:1, :], start=False, stop=True
                                  ).then_inc(mmS, 1)
                @blk.scalar
                def _(s):
                    for idx, (w, dst, b, oc) in enumerate(sched):
                        pb = pgs[idx % 2]
                        s.wait_ge(mmS, MM0 + idx + 1)
                        s.activation(dst.ap()[:, oc, :], pb[:], AF.Copy
                                     ).then_inc(evS, 1)
            mmc[0] += len(sched)
            evc[0] += len(sched)
            with nc.Block() as blk:
                @blk.sync
                def _(sy):
                    sy.dma_start(out=qsw.ap()[0:64, :, :],
                                 in_=qt.ap()[64:128, :, :]).then_inc(ld, 16)
                    sy.dma_start(out=qsw.ap()[64:128, :, :],
                                 in_=qt.ap()[0:64, :, :]).then_inc(ld, 16)
                    sy.wait_ge(ld, 48 * (i + 1))

            for k in range(4):
                tc0 = k * 128
                MM0, EV0, VZ0, RR0 = mmc[0], evc[0], vzc[0], rrc[0]
                with nc.Block() as blk:
                    @blk.tensor
                    def _(te):
                        for jp in range(17):
                            if jp < 16:
                                half, cc = jp % 2, jp // 2
                                lhs = kt.ap()[half * 64:(half + 1) * 64, cc,
                                              tc0:tc0 + 128]
                                if jp > 0:
                                    te.wait_ge(evS, EV0 + jp)
                                for g in range(4):
                                    gh, gc = g // 2, g % 2
                                    buf = qt if gh == half else qsw
                                    rhs = buf.ap()[half * 64:(half + 1) * 64,
                                                   gc * 4:(gc + 1) * 4, tc0:tc0 + 128]
                                    ins = te.matmul(pgs[g][:], lhs, rhs,
                                                    start=True, stop=True)
                                ins.then_inc(mmS, 1)
                            if jp > 0:
                                jr = jp - 1
                                te.wait_ge(vzS, VZ0 + jr + 1)
                                for g in range(4):
                                    ins = te.matmul(pr[:, g * TT:(g + 1) * TT],
                                                    invz.ap(),
                                                    pbuf[:, g * TT:(g + 1) * TT],
                                                    start=(jr == 0), stop=(jr == 15),
                                                    skip_group_check=True)
                                ins.then_inc(rrS, 1)
                    @blk.scalar
                    def _(s):
                        for jp in range(16):
                            s.wait_ge(mmS, MM0 + jp + 1)
                            if jp >= 1:
                                s.wait_ge(rrS, RR0 + jp)
                            if jp >= 1:
                                s.wait_ge(vzS, VZ0 + jp)
                            for g in range(4):
                                ins = s.activation(pbuf[:, g * TT:(g + 1) * TT],
                                                   pgs[g][:], AF.Exp,
                                                   scale=1.0 / 1024.0,
                                                   accum_out=zpart[:, g:g + 1])
                            ins.then_inc(evS, 1)
                    @blk.vector
                    def _(v):
                        for jp in range(16):
                            v.wait_ge(evS, EV0 + jp + 1)
                            if jp > 0:
                                v.wait_ge(rrS, RR0 + jp)
                            v.reduce_sum(ztot[:], zpart[:], axis=AX.X)
                            v.drain()
                            v.reciprocal(invzf[:], ztot[:])
                            v.drain()
                            v.tensor_copy(invz[:], invzf[:]).then_inc(vzS, 1)
                mmc[0] += 16
                evc[0] += 16
                vzc[0] += 16
                rrc[0] += 16
                with nc.Block() as blk:
                    @blk.scalar
                    def _(s):
                        for g in range(4):
                            s.activation(r4row[:, g * TT:(g + 1) * TT],
                                         pr[:, g * TT:(g + 1) * TT], AF.Copy)
                with nc.Block() as blk:
                    @blk.tensor
                    def _(te):
                        for cc in range(NCH):
                            pb = (pg0, pg1)[cc // 4]
                            for half in range(2):
                                a0 = half * 1024 + cc * 128
                                te.matmul(
                                    pb[half * 64:(half + 1) * 64,
                                       (cc % 4) * 128:(cc % 4 + 1) * 128],
                                    ONESF.ap()[0:1, 0:64],
                                    r4row[:, a0:a0 + 128], start=True, stop=True)
                with nc.Block() as blk:
                    @blk.vector
                    def _(v):
                        v.tensor_tensor(outb[:, 0:4, tc0:tc0 + 128],
                                        vt[:, 0:4, tc0:tc0 + 128],
                                        pg0.ap().rearrange("p (c t) -> p c t", c=4),
                                        ALU.mult)
                        v.tensor_tensor(outb[:, 4:8, tc0:tc0 + 128],
                                        vt[:, 4:8, tc0:tc0 + 128],
                                        pg1.ap().rearrange("p (c t) -> p c t", c=4),
                                        ALU.mult)

            MM0, EV0 = mmc[0], evc[0]
            with nc.Block() as blk:
                @blk.tensor
                def _(te):
                    for oc in range(NCH):
                        pb = pgs[oc % 2]
                        if oc >= 2:
                            te.wait_ge(evS, EV0 + oc - 1)
                        for c in range(NCH):
                            te.matmul(pb[:], wsb["Wo"].ap()[:, c, oc * 128:(oc + 1) * 128],
                                      outb[:, c, :], start=(c == 0), stop=False)
                        te.matmul(pb[:], bsb["bo"].ap()[:, oc * 128:(oc + 1) * 128],
                                  ONESB.ap()[0:1, :], start=False, stop=True
                                  ).then_inc(mmS, 1)
                @blk.scalar
                def _(s):
                    for oc in range(NCH):
                        pb = pgs[oc % 2]
                        s.wait_ge(mmS, MM0 + oc + 1)
                        s.activation(yb[:, oc, :], pb[:], AF.Copy).then_inc(evS, 1)
            mmc[0] += NCH
            evc[0] += NCH

            emit_lnwrap(nc, csb, yb, xt, sqb, rrow, rowB, rowC,
                        pr[:, 0:TT], pr[:, TT:2 * TT], pg0, pg1, tmp1)
            with nc.Block() as blk:
                @blk.sync
                def _(sy):
                    sy.dma_start(out=x_dram_tile(xout, t0, TT), in_=xt[:]).then_inc(st, 16)
                    sy.wait_ge(st, 16 * (i + 1))
    return nc





def build_F0():
    nc = new_nc()
    caps = declare_consts(nc, ["ONESB", "ONESF"])
    xin = nc.dram_tensor("xin", [NCH, 128, TOK], F32, kind="ExternalInput").ap()
    xout = nc.dram_tensor("xout", [NCH, 128, TOK], F32, kind="ExternalOutput").ap()
    wnames = ["WrT", "WkT", "WvT"]
    wd = {w: nc.dram_tensor(w, [D, D], F32, kind="ExternalInput").ap() for w in wnames}
    mur = nc.dram_tensor("mur", [1, D], F32, kind="ExternalInput").ap()
    muk = nc.dram_tensor("muk", [1, D], F32, kind="ExternalInput").ap()
    csb = load_consts(nc, caps)
    ONESB, ONESF = csb["ONESB"], csb["ONESF"]

    wsb = {w: nc.alloc_sbuf_tensor(f"w_{w}", [128, NCH, D], BF16) for w in wnames}
    murs = nc.alloc_sbuf_tensor("murs", [128, NCH], F32)
    muks = nc.alloc_sbuf_tensor("muks", [128, NCH], F32)
    wldsem = nc.semaphore("wld").__enter__()
    with nc.Block() as blk:
        @blk.gpsimd
        def _(g):
            n = 0
            for w in wnames:
                g.dma_start(out=wsb[w].ap(),
                            in_=wd[w].rearrange("(c p) o -> p c o", p=128)
                            ).then_inc(wldsem, 16)
                n += 1
            for dst, src in ((murs, mur), (muks, muk)):
                g.dma_start(out=dst.ap(),
                            in_=src.rearrange("x (c p) -> (x p) c", p=128),
                            allow_slow_non_contiguous=True).then_inc(wldsem, 16)
                n += 1
            g.wait_ge(wldsem, 16 * n)

    xt513 = nc.alloc_sbuf_tensor("xt513", [128, NCH, TT + 1], F32)
    sqb = nc.alloc_sbuf_tensor("sqb", [128, NCH, TT], BF16)
    rrow = nc.alloc_sbuf_tensor("rrow", [1, TT], F32)
    rowB = nc.alloc_sbuf_tensor("rowB", [1, TT], F32)
    rowC = nc.alloc_sbuf_tensor("rowC", [1, TT], F32)
    rghost = nc.alloc_sbuf_tensor("rghost", [1, 1], F32)
    h1 = nc.alloc_sbuf_tensor("h1", [128, NCH, TT + 1], BF16)
    tmix = nc.alloc_sbuf_tensor("tmix", [128, NCH, TT], BF16)
    xrb = nc.alloc_sbuf_tensor("xrb", [128, NCH, TT], BF16)
    xkb = nc.alloc_sbuf_tensor("xkb", [128, NCH, TT], BF16)
    rs = nc.alloc_sbuf_tensor("rs", [128, NCH, TT], BF16)
    kk = nc.alloc_sbuf_tensor("kk", [128, NCH, TT], BF16)
    ka = nc.alloc_sbuf_tensor("ka", [128, NCH, TT], BF16)
    yb = nc.alloc_sbuf_tensor("yb", [128, NCH, TT], BF16)
    tmp1 = nc.alloc_sbuf_tensor("tmp1", [128, NCH, TT], F32)
    pg0 = nc.alloc_psum_tensor("pg0", [128, TT], F32)
    pg1 = nc.alloc_psum_tensor("pg1", [128, TT], F32)
    pg2 = nc.alloc_psum_tensor("pg2", [128, TT], F32)
    pr = nc.alloc_psum_tensor("pr", [1, 2048], F32)
    ld = nc.semaphore("ld").__enter__()
    st = nc.semaphore("st").__enter__()
    mmS = nc.semaphore("mm").__enter__()
    evS = nc.semaphore("ev").__enter__()
    pgs = [pg0, pg1, pg2]
    mmc, evc = [0], [0]
    nld = [0]

    def proj_block(nc, src, wname, evac):
        MM0, EV0 = mmc[0], evc[0]
        with nc.Block() as blk:
            @blk.tensor
            def _(te):
                for oc in range(NCH):
                    pb = pgs[oc % 2]
                    if oc >= 2:
                        te.wait_ge(evS, EV0 + oc - 1)
                    for c in range(NCH):
                        ins = te.matmul(pb[:],
                                        wsb[wname].ap()[:, c, oc * 128:(oc + 1) * 128],
                                        src.ap()[:, c, :], start=(c == 0),
                                        stop=(c == NCH - 1))
                    ins.then_inc(mmS, 1)
            evac(blk, MM0)
        mmc[0] += NCH
        evc[0] += NCH

    for i in range(NTILE):
        t0 = i * TT
        bstart = (t0 % T) == 0
        with nc.Block() as blk:
            @blk.sync
            def _(sy):
                if bstart:
                    sy.dma_start(out=xt513[:, :, 1:TT + 1],
                                 in_=x_dram_tile(xin, t0, TT)).then_inc(ld, 16)
                else:
                    sy.dma_start(out=xt513[:],
                                 in_=xin[:, :, t0 - 1:t0 + TT]
                                 .rearrange("c p t -> p c t")).then_inc(ld, 16)
                nld[0] += 16
                sy.wait_ge(ld, nld[0])
            @blk.vector
            def _(v):
                if bstart:
                    v.memset(xt513.ap()[:, :, 0:1], 0.0)
        xt = xt513.ap()[:, :, 1:TT + 1]
        with nc.Block() as blk:
            @blk.scalar
            def _(s):
                s.activation(sqb.ap(), xt, AF.Square)
        with nc.Block() as blk:
            @blk.tensor
            def _(te):
                for c in range(NCH):
                    te.matmul(pr[:, 0:TT], ONESB.ap()[:, 0:1], sqb.ap()[:, c, :],
                              start=(c == 0), stop=(c == NCH - 1))
        with nc.Block() as blk:
            @blk.scalar
            def _(s):
                s.activation(rrow.ap(), pr[:, 0:TT], AF.Ln,
                             bias=csb["EPS5"].ap()[0:1, :], scale=1.0 / D)
                s.activation(rrow.ap(), rrow.ap(), AF.Exp, scale=-0.5)
        with nc.Block() as blk:
            @blk.tensor
            def _(te):
                te.matmul(pg0[:], ONESF.ap()[0:1, 0:128], rrow.ap(),
                          start=True, stop=True)
                te.matmul(pg1[0:128, 0:1], ONESF.ap()[0:1, 0:128],
                          rghost.ap(), start=True, stop=True)
        with nc.Block() as blk:
            @blk.vector
            def _(v):
                v.tensor_tensor(h1.ap()[:, :, 1:TT + 1], xt, bc_ap(pg0, NCH, TT),
                                ALU.mult)
        with nc.Block() as blk:
            @blk.vector
            def _(v):
                if bstart:
                    v.memset(h1.ap()[:, :, 0:1], 0.0)
                else:
                    v.tensor_tensor(h1.ap()[:, :, 0:1], xt513.ap()[:, :, 0:1],
                                    bass.AP(pg1, 0, [[TT, 128], [0, NCH], [1, 1]]),
                                    ALU.mult)
                v.tensor_copy(rghost.ap(), rrow.ap()[:, TT - 1:TT])
        with nc.Block() as blk:
            @blk.vector
            def _(v):
                v.tensor_sub(tmix.ap(), h1.ap()[:, :, 1:TT + 1],
                             h1.ap()[:, :, 0:TT])
        with nc.Block() as blk:
            @blk.vector
            def _(v):
                for c in range(NCH):
                    v.scalar_tensor_tensor(xrb.ap()[:, c, :], tmix.ap()[:, c, :],
                                           murs.ap()[:, c:c + 1],
                                           h1.ap()[:, c, 0:TT], ALU.mult, ALU.add)
                for c in range(NCH):
                    v.scalar_tensor_tensor(xkb.ap()[:, c, :], tmix.ap()[:, c, :],
                                           muks.ap()[:, c:c + 1],
                                           h1.ap()[:, c, 0:TT], ALU.mult, ALU.add)
        def evac_sig(blk, MM0):
            @blk.scalar
            def _(s):
                for oc in range(NCH):
                    pb = pgs[oc % 2]
                    s.wait_ge(mmS, MM0 + oc + 1)
                    s.activation(rs.ap()[:, oc, :], pb[:], AF.Sigmoid
                                 ).then_inc(evS, 1)
        proj_block(nc, xrb, "WrT", evac_sig)
        def evac_cp(blk, MM0):
            @blk.scalar
            def _(s):
                for oc in range(NCH):
                    pb = pgs[oc % 2]
                    s.wait_ge(mmS, MM0 + oc + 1)
                    s.activation(kk.ap()[:, oc, :], pb[:], AF.Copy
                                 ).then_inc(evS, 1)
        proj_block(nc, xkb, "WkT", evac_cp)
        with nc.Block() as blk:
            @blk.vector
            def _(v):
                v.scalar_tensor_tensor(ka.ap(), kk.ap(), 0.0, kk.ap(),
                                       ALU.max, ALU.mult)
        def evac_gate(blk, MM0):
            @blk.vector
            def _(v):
                for oc in range(NCH):
                    pb = pgs[oc % 2]
                    v.wait_ge(mmS, MM0 + oc + 1)
                    v.tensor_tensor(yb.ap()[:, oc, :], rs.ap()[:, oc, :], pb[:],
                                    ALU.mult).then_inc(evS, 1)
        proj_block(nc, ka, "WvT", evac_gate)
        emit_lnwrap(nc, csb, yb, xt513_x(xt513), sqb, rrow, rowB, rowC,
                    pr[:, 0:TT], pr[:, TT:2 * TT], pg0, pg1, tmp1)
        with nc.Block() as blk:
            @blk.sync
            def _(sy):
                sy.dma_start(out=x_dram_tile(xout, t0, TT),
                             in_=xt513.ap()[:, :, 1:TT + 1]).then_inc(st, 16)
                sy.wait_ge(st, 16 * (i + 1))
    return nc


class _XtView:
    def __init__(self, h):
        self.h = h

    def ap(self):
        return self.h.ap()[:, :, 1:TT + 1]


def xt513_x(h):
    return _XtView(h)





def a1_kgroups(qi):
    nk = (qi + 1) * 128
    gs, off = [], 0
    while off < nk:
        w = min(512, nk - off)
        gs.append((off, w))
        off += w
    return gs


A1_DEBUG = False


def build_A1():
    nc = new_nc()
    caps = declare_consts(nc, ["ONESB", "ONESF", "I16F", "MASKTRI", "E8SEL"])
    dbg = {}
    if A1_DEBUG:
        for nm, shp in (("d_qh", [128, 4, 2048]), ("d_kn", [128, 4, 2048]),
                        ("d_vtm", [128, 16, 512]), ("d_qsqs", [8, 2048]),
                        ("d_ksqs", [8, 2048]), ("d_Ob", [128, NCH, 2048]),
                        ("d_P", [128, 2048]), ("d_s", [128, 2048])):
            dt = F32 if nm in ("d_qsqs", "d_ksqs", "d_s") else BF16
            dbg[nm] = nc.dram_tensor(nm, shp, dt, kind="ExternalOutput").ap()
    xin = nc.dram_tensor("xin", [NCH, 128, TOK], F32, kind="ExternalInput").ap()
    xout = nc.dram_tensor("xout", [NCH, 128, TOK], F32, kind="ExternalOutput").ap()
    wqkv = nc.dram_tensor("wqkv", [D, 3 * D], F32, kind="ExternalInput").ap()
    bqkv = nc.dram_tensor("bqkv", [1, 3 * D], F32, kind="ExternalInput").ap()
    wout = nc.dram_tensor("wout", [D, D], F32, kind="ExternalInput").ap()
    bout = nc.dram_tensor("bout", [1, D], F32, kind="ExternalInput").ap()
    csb = load_consts(nc, caps)
    ONESB, ONESF = csb["ONESB"], csb["ONESF"]
    I16F, MASKTRI, E8 = csb["I16F"], csb["MASKTRI"], csb["E8SEL"]

    qh = nc.alloc_sbuf_tensor("qh", [128, 4, 2048], BF16)
    kn = nc.alloc_sbuf_tensor("kn", [128, 4, 2048], BF16)
    vtm = nc.alloc_sbuf_tensor("vtm", [128, 16, 512], BF16)
    wbuf = nc.alloc_sbuf_tensor("wbuf", [128, NCH, 1024], BF16)
    bqkv_sb = nc.alloc_sbuf_tensor("bqkv_sb", [1, 3 * D], BF16)
    bout_sb = nc.alloc_sbuf_tensor("bout_sb", [1, D], BF16)
    Ob = nc.alloc_sbuf_tensor("Ob", [128, NCH, 2048], BF16)
    xt = nc.alloc_sbuf_tensor("xt", [128, NCH, TT], F32)
    xb = nc.alloc_sbuf_tensor("xb", [128, NCH, TT], BF16)
    sqs0 = nc.alloc_sbuf_tensor("sqs0", [128, TT], BF16)
    sqs1 = nc.alloc_sbuf_tensor("sqs1", [128, TT], BF16)
    qsqs = nc.alloc_sbuf_tensor("qsqs", [8, 2048], F32)
    ksqs = nc.alloc_sbuf_tensor("ksqs", [8, 2048], F32)
    sS = nc.alloc_sbuf_tensor("sS", [128, 2048], F32)
    Pb = nc.alloc_sbuf_tensor("Pb", [128, 2048], BF16)
    PbN = nc.alloc_sbuf_tensor("PbN", [128, 2048], BF16)
    PT = nc.alloc_sbuf_tensor("PT", [128, 16, 128], BF16)
    rec0 = nc.alloc_sbuf_tensor("rec0", [128, TT], F32)
    rec1 = nc.alloc_sbuf_tensor("rec1", [128, TT], F32)
    sqf0 = nc.alloc_sbuf_tensor("sqf0", [128, TT], F32)
    sqf1 = nc.alloc_sbuf_tensor("sqf1", [128, TT], F32)
    negm = nc.alloc_sbuf_tensor("negm", [128, 1], F32)
    junk1 = nc.alloc_sbuf_tensor("junk1", [128, 1], F32)
    zrow = nc.alloc_sbuf_tensor("zrow", [128, 1], F32)
    invzq = nc.alloc_sbuf_tensor("invzq", [128, 1], F32)
    rrowB = nc.alloc_sbuf_tensor("rrowB", [1, 2048], F32)
    rowA = nc.alloc_sbuf_tensor("rowA", [1, TT], F32)
    rowB = nc.alloc_sbuf_tensor("rowB", [1, TT], F32)
    rowC = nc.alloc_sbuf_tensor("rowC", [1, TT], F32)
    yb = nc.alloc_sbuf_tensor("yb", [128, NCH, TT], BF16)
    tmpb = nc.alloc_sbuf_tensor("tmpb", [128, NCH, TT], BF16)

    pg0 = nc.alloc_psum_tensor("pg0", [128, TT], F32)
    pg1 = nc.alloc_psum_tensor("pg1", [128, TT], F32)
    pg2 = nc.alloc_psum_tensor("pg2", [128, TT], F32)
    pg3 = nc.alloc_psum_tensor("pg3", [128, TT], F32)
    pav = nc.alloc_psum_tensor("pav", [64, 128], F32)
    pq8 = nc.alloc_psum_tensor("pq8", [8, TT], F32)
    pl1 = nc.alloc_psum_tensor("pl1", [1, TT], F32)
    pl2 = nc.alloc_psum_tensor("pl2", [1, TT], F32)
    pgs = [pg0, pg1, pg2]

    sems = {}
    for nm in ["ld", "st", "mm", "ev", "tt", "mx", "ex", "pn", "tr", "av",
               "oe", "wld"]:
        sems[nm] = nc.semaphore(nm).__enter__()
    C = {k: 0 for k in sems}
    ld, st, mmS, evS, ttS = (sems[k] for k in ["ld", "st", "mm", "ev", "tt"])
    mxS, exS, pnS, trS, avS, oeS, wldsem = (
        sems[k] for k in ["mx", "ex", "pn", "tr", "av", "oe", "wld"])

    def wload(g, dst_ap, src_ap, slow=False):
        g.dma_start(out=dst_ap, in_=src_ap,
                    allow_slow_non_contiguous=slow).then_inc(wldsem, 16)
        C["wld"] += 16

    def ind8(hl, n):
        return bass.AP(I16F, hl, [[16, 8], [0, n]])

    with nc.Block() as blk:
        @blk.gpsimd
        def _(g):
            wload(g, bqkv_sb.ap(), bqkv)
            wload(g, bout_sb.ap(), bout)
            g.wait_ge(wldsem, C["wld"])

    for b in range(B):
        tb0 = b * T
        for hg in range(2):
            with nc.Block() as blk:
                @blk.gpsimd
                def _(g):
                    wload(g, wbuf.ap()[:, :, 0:512],
                          wqkv[:, hg * 512:(hg + 1) * 512]
                          .rearrange("(c p) o -> p c o", p=128))
                    wload(g, wbuf.ap()[:, :, 512:1024],
                          wqkv[:, D + hg * 512:D + (hg + 1) * 512]
                          .rearrange("(c p) o -> p c o", p=128))
                    g.wait_ge(wldsem, C["wld"])
            for it in range(4):
                t0 = tb0 + it * TT
                rsl = rrowB.ap()[:, it * TT:(it + 1) * TT]
                with nc.Block() as blk:
                    @blk.sync
                    def _(sy):
                        sy.dma_start(out=xt.ap(), in_=x_dram_tile(xin, t0, TT)
                                     ).then_inc(ld, 16)
                        C["ld"] += 16
                        sy.wait_ge(ld, C["ld"])
                if hg == 0:
                    with nc.Block() as blk:
                        @blk.scalar
                        def _(s):
                            s.activation(xb.ap(), xt.ap(), AF.Square)
                    with nc.Block() as blk:
                        @blk.tensor
                        def _(te):
                            for c in range(NCH):
                                te.matmul(pl1[:], ONESB.ap()[:, 0:1],
                                          xb.ap()[:, c, :],
                                          start=(c == 0), stop=(c == NCH - 1))
                    with nc.Block() as blk:
                        @blk.scalar
                        def _(s):
                            s.activation(rsl, pl1[:], AF.Ln,
                                         bias=csb["EPS5"].ap()[0:1, :],
                                         scale=1.0 / D)
                            s.activation(rsl, rsl, AF.Exp, scale=-0.5)
                with nc.Block() as blk:
                    @blk.tensor
                    def _(te):
                        te.matmul(pg3[:], ONESF.ap()[0:1, 0:128], rsl,
                                  start=True, stop=True)
                with nc.Block() as blk:
                    @blk.vector
                    def _(v):
                        v.tensor_tensor(xb.ap(), xt.ap(), bc_ap(pg3, NCH, TT),
                                        ALU.mult)
                MM0, EV0 = C["mm"], C["ev"]
                qkocs = [("q", oc) for oc in range(4)] + \
                        [("k", oc) for oc in range(4)]
                with nc.Block() as blk:
                    @blk.tensor
                    def _(te):
                        for idx, (which, oc) in enumerate(qkocs):
                            pb = pgs[idx % 2]
                            if idx >= 2:
                                te.wait_ge(evS, EV0 + idx - 1)
                            wcol = (0 if which == "q" else 512) + oc * 128
                            gcol = ((hg * 512 if which == "q"
                                     else D + hg * 512) + oc * 128)
                            for c in range(NCH):
                                te.matmul(pb[:], wbuf.ap()[:, c, wcol:wcol + 128],
                                          xb.ap()[:, c, :], start=(c == 0),
                                          stop=False)
                            te.matmul(pb[:], bqkv_sb.ap()[:, gcol:gcol + 128],
                                      ONESB.ap()[0:1, :], start=False, stop=True
                                      ).then_inc(mmS, 1)
                    @blk.scalar
                    def _(s):
                        for idx, (which, oc) in enumerate(qkocs):
                            pb = pgs[idx % 2]
                            s.wait_ge(mmS, MM0 + idx + 1)
                            dst = qh if which == "q" else kn
                            scl = 1.0 if which == "q" else -1.0
                            s.activation(dst.ap()[:, oc, it * TT:(it + 1) * TT],
                                         pb[:], AF.Copy, scale=scl
                                         ).then_inc(evS, 1)
                C["mm"] += len(qkocs)
                C["ev"] += len(qkocs)
            with nc.Block() as blk:
                @blk.gpsimd
                def _(g):
                    wload(g, wbuf.ap()[:, :, 0:512],
                          wqkv[:, 2 * D + hg * 512:2 * D + (hg + 1) * 512]
                          .rearrange("(c p) o -> p c o", p=128))
                    g.wait_ge(wldsem, C["wld"])
            for it in range(4):
                t0 = tb0 + it * TT
                with nc.Block() as blk:
                    @blk.sync
                    def _(sy):
                        sy.dma_start(out=xt.ap(), in_=x_dram_tile(xin, t0, TT)
                                     ).then_inc(ld, 16)
                        C["ld"] += 16
                        sy.wait_ge(ld, C["ld"])
                with nc.Block() as blk:
                    @blk.tensor
                    def _(te):
                        te.matmul(pg3[:], ONESF.ap()[0:1, 0:128],
                                  rrowB.ap()[:, it * TT:(it + 1) * TT],
                                  start=True, stop=True)
                with nc.Block() as blk:
                    @blk.vector
                    def _(v):
                        v.tensor_tensor(xb.ap(), xt.ap(), bc_ap(pg3, NCH, TT),
                                        ALU.mult)
                MM0, EV0 = C["mm"], C["ev"]
                with nc.Block() as blk:
                    @blk.tensor
                    def _(te):
                        for tb in range(4):
                            pb = pgs[tb % 2]
                            if tb >= 2:
                                te.wait_ge(evS, EV0 + tb - 1)
                            for c in range(NCH):
                                te.matmul(pb[:],
                                          xb.ap()[:, c, tb * 128:(tb + 1) * 128],
                                          wbuf.ap()[:, c, 0:512],
                                          start=(c == 0), stop=False)
                            te.matmul(pb[:], ONESB.ap()[0:1, 0:128],
                                      bqkv_sb.ap()[:, 2 * D + hg * 512:
                                                   2 * D + (hg + 1) * 512],
                                      start=False, stop=True).then_inc(mmS, 1)
                    @blk.scalar
                    def _(s):
                        for tb in range(4):
                            pb = pgs[tb % 2]
                            s.wait_ge(mmS, MM0 + tb + 1)
                            s.activation(vtm.ap()[:, it * 4 + tb, :], pb[:],
                                         AF.Copy).then_inc(evS, 1)
                C["mm"] += 4
                C["ev"] += 4
            for src, dstrows, epsb in ((qh, qsqs, 0.0), (kn, ksqs, 1e-6)):
                for it in range(4):
                    MM0, EV0 = C["mm"], C["ev"]
                    with nc.Block() as blk:
                        @blk.scalar
                        def _(s):
                            for oc in range(4):
                                sqd = (sqs0, sqs1)[oc % 2]
                                if oc >= 2:
                                    s.wait_ge(mmS, MM0 + oc - 1)
                                s.activation(sqd.ap(), src.ap()[:, oc,
                                             it * TT:(it + 1) * TT],
                                             AF.Square).then_inc(evS, 1)
                        @blk.tensor
                        def _(te):
                            for oc in range(4):
                                sqd = (sqs0, sqs1)[oc % 2]
                                te.wait_ge(evS, EV0 + oc + 1)
                                te.matmul(pq8[:], E8.ap()[:, oc, :], sqd.ap(),
                                          start=(oc == 0), stop=(oc == 3)
                                          ).then_inc(mmS, 1)
                    C["mm"] += 4
                    C["ev"] += 4
                    with nc.Block() as blk:
                        @blk.scalar
                        def _(s):
                            s.activation(dstrows.ap()[:, it * TT:(it + 1) * TT],
                                         pq8[:], AF.Copy, bias=epsb)
            if A1_DEBUG and b == 0 and hg == 0:
                with nc.Block() as blk:
                    @blk.sync
                    def _(sy):
                        for ap_, h_ in ((dbg["d_qh"], qh), (dbg["d_kn"], kn),
                                        (dbg["d_vtm"], vtm), (dbg["d_qsqs"], qsqs),
                                        (dbg["d_ksqs"], ksqs)):
                            sy.dma_start(out=ap_, in_=h_.ap()).then_inc(st, 16)
                            C["st"] += 16
                        sy.wait_ge(st, C["st"])
            for hl in range(8):
                c_l, half = hl // 2, hl % 2
                gch = (hg * 8 + hl) // 2
                MM0, EV0, TT0 = C["mm"], C["ev"], C["tt"]
                MX0, EX0, PN0, TR0 = C["mx"], C["ex"], C["pn"], C["tr"]
                AV0, OE0 = C["av"], C["oe"]
                groups = [(qi, kk0, kw) for qi in range(16)
                          for (kk0, kw) in a1_kgroups(qi)]
                with nc.Block() as blk:
                    @blk.tensor
                    def _(te):
                        gidx = 0
                        for qi in range(16):
                            nk = (qi + 1) * 128
                            lhs = qh.ap()[half * 64:(half + 1) * 64, c_l,
                                          qi * 128:(qi + 1) * 128]
                            for (kk0, kw) in a1_kgroups(qi):
                                bank = pgs[gidx % 3]
                                if gidx >= 3:
                                    te.wait_ge(ttS, TT0 + gidx - 2)
                                rhs = kn.ap()[half * 64:(half + 1) * 64, c_l,
                                              kk0:kk0 + kw]
                                te.matmul(bank[:, 0:kw], lhs, rhs,
                                          start=True, stop=False
                                          ).then_inc(mmS, 1)
                                te.wait_ge(evS, EV0 + gidx + 1)
                                te.matmul(bank[:, 0:kw], lhs, rhs,
                                          start=False, stop=False,
                                          skip_group_check=True)
                                te.matmul(bank[:, 0:kw], ind8(hl, 128),
                                          ksqs.ap()[:, kk0:kk0 + kw],
                                          start=False, stop=False,
                                          skip_group_check=True)
                                te.matmul(bank[:, 0:kw],
                                          qsqs.ap()[:, qi * 128:(qi + 1) * 128],
                                          ind8(hl, kw),
                                          start=False, stop=True,
                                          skip_group_check=True).then_inc(mmS, 1)
                                gidx += 1
                            te.wait_ge(trS, TR0 + 16 * (qi + 1))
                            te.wait_ge(oeS, OE0 + qi)
                            for kc in range(qi + 1):
                                ins = te.matmul(pav[:],
                                                vtm.ap()[:, kc,
                                                         hl * 64:(hl + 1) * 64],
                                                PT.ap()[:, kc, :],
                                                start=(kc == 0), stop=(kc == qi))
                            ins.then_inc(avS, 1)
                    @blk.scalar
                    def _(s):
                        gidx = 0
                        for qi in range(16):
                            nk = (qi + 1) * 128
                            for (kk0, kw) in a1_kgroups(qi):
                                bank = pgs[gidx % 3]
                                sqd = (sqf0, sqf1)[gidx % 2]
                                s.wait_ge(mmS, MM0 + 2 * gidx + 1)
                                if gidx >= 2:
                                    s.wait_ge(ttS, TT0 + gidx - 1)
                                s.activation(sqd.ap()[:, 0:kw], bank[:, 0:kw],
                                             AF.Square).then_inc(evS, 1)
                                gidx += 1
                            s.wait_ge(mxS, MX0 + qi + 1)
                            if qi >= 1:
                                s.wait_ge(pnS, PN0 + qi)
                            s.activation(Pb.ap()[:, 0:nk], sS.ap()[:, 0:nk],
                                         AF.Exp, bias=negm.ap())
                            s.activation(junk1.ap(), negm.ap(), AF.Copy
                                         ).then_inc(exS, 1)
                            s.wait_ge(avS, AV0 + qi + 1)
                            s.activation(Ob.ap()[half * 64:(half + 1) * 64, gch,
                                                 qi * 128:(qi + 1) * 128],
                                         pav[:], AF.Copy).then_inc(oeS, 1)
                    @blk.vector
                    def _(v):
                        gidx = 0
                        for qi in range(16):
                            nk = (qi + 1) * 128
                            for (kk0, kw) in a1_kgroups(qi):
                                bank = pgs[gidx % 3]
                                sqd = (sqf0, sqf1)[gidx % 2]
                                rcd = (rec0, rec1)[gidx % 2]
                                v.wait_ge(mmS, MM0 + 2 * gidx + 2)
                                v.reciprocal(rcd.ap()[:, 0:kw], bank[:, 0:kw])
                                v.tensor_tensor(sS.ap()[:, kk0:kk0 + kw],
                                                sqd.ap()[:, 0:kw],
                                                rcd.ap()[:, 0:kw],
                                                ALU.mult).then_inc(ttS, 1)
                                gidx += 1
                            v.tensor_tensor(sS.ap()[:, qi * 128:nk],
                                            sS.ap()[:, qi * 128:nk],
                                            MASKTRI.ap(), ALU.add)
                            v.reduce_max(negm.ap(), sS.ap()[:, 0:nk], axis=AX.X,
                                         negate=True)
                            v.tensor_copy(junk1.ap(), negm.ap()).then_inc(mxS, 1)
                            v.wait_ge(exS, EX0 + qi + 1)
                            v.reduce_sum(zrow.ap(), Pb.ap()[:, 0:nk], axis=AX.X)
                            v.drain()
                            v.reciprocal(invzq.ap(), zrow.ap())
                            v.drain()
                            v.tensor_scalar(PbN.ap()[:, 0:nk], Pb.ap()[:, 0:nk],
                                            invzq.ap(), None, ALU.mult)
                            v.tensor_copy(junk1.ap(), invzq.ap()).then_inc(pnS, 1)
                    @blk.sync
                    def _(sy):
                        for qi in range(16):
                            nk = (qi + 1) * 128
                            sy.wait_ge(pnS, PN0 + qi + 1)
                            sy.dma_start_transpose(PT.ap()[:, 0:qi + 1, :],
                                                   PbN.ap()[:, 0:nk]
                                                   ).then_inc(trS, 16)
                if A1_DEBUG and b == 0 and hg == 0 and hl == 0:
                    with nc.Block() as blk:
                        @blk.sync
                        def _(sy):
                            sy.dma_start(out=dbg["d_s"], in_=sS.ap()).then_inc(st, 16)
                            sy.dma_start(out=dbg["d_P"], in_=Pb.ap()).then_inc(st, 16)
                            C["st"] += 32
                            sy.wait_ge(st, C["st"])
                ngroups = len(groups)
                C["mm"] += 2 * ngroups
                C["ev"] += ngroups
                C["tt"] += ngroups
                C["mx"] += 16
                C["ex"] += 16
                C["pn"] += 16
                C["tr"] += 16 * 16
                C["av"] += 16
                C["oe"] += 16
        if A1_DEBUG and b == 0:
            with nc.Block() as blk:
                @blk.sync
                def _(sy):
                    sy.dma_start(out=dbg["d_Ob"], in_=Ob.ap()).then_inc(st, 16)
                    C["st"] += 16
                    sy.wait_ge(st, C["st"])
        with nc.Block() as blk:
            @blk.gpsimd
            def _(g):
                wload(g, wbuf.ap()[:, :, 0:1024],
                      wout.rearrange("(c p) o -> p c o", p=128))
                g.wait_ge(wldsem, C["wld"])
        for it in range(4):
            t0 = tb0 + it * TT
            with nc.Block() as blk:
                @blk.sync
                def _(sy):
                    sy.dma_start(out=xt.ap(), in_=x_dram_tile(xin, t0, TT)
                                 ).then_inc(ld, 16)
                    C["ld"] += 16
                    sy.wait_ge(ld, C["ld"])
            MM0, EV0 = C["mm"], C["ev"]
            with nc.Block() as blk:
                @blk.tensor
                def _(te):
                    for oc in range(NCH):
                        pb = pgs[oc % 2]
                        if oc >= 2:
                            te.wait_ge(evS, EV0 + oc - 1)
                        for c in range(NCH):
                            te.matmul(pb[:], wbuf.ap()[:, c, oc * 128:(oc + 1) * 128],
                                      Ob.ap()[:, c, it * TT:(it + 1) * TT],
                                      start=(c == 0), stop=False)
                        te.matmul(pb[:], bout_sb.ap()[:, oc * 128:(oc + 1) * 128],
                                  ONESB.ap()[0:1, :], start=False, stop=True
                                  ).then_inc(mmS, 1)
                @blk.scalar
                def _(s):
                    for oc in range(NCH):
                        pb = pgs[oc % 2]
                        s.wait_ge(mmS, MM0 + oc + 1)
                        s.activation(yb.ap()[:, oc, :], pb[:], AF.Copy
                                     ).then_inc(evS, 1)
            C["mm"] += NCH
            C["ev"] += NCH
            emit_lnwrap(nc, csb, yb, xt, xb, rowA, rowB, rowC,
                        pl1[:], pl2[:], pg0, pg1, tmpb)
            with nc.Block() as blk:
                @blk.sync
                def _(sy):
                    sy.dma_start(out=x_dram_tile(xout, t0, TT), in_=xt.ap()
                                 ).then_inc(st, 16)
                    C["st"] += 16
                    sy.wait_ge(st, C["st"])
    return nc







def build_hprep():
    nc = new_nc()
    caps = declare_consts(nc, ["ONESB", "ONESF"])
    xin = nc.dram_tensor("xin", [NCH, 128, TOK], F32, kind="ExternalInput").ap()
    xiout = nc.dram_tensor("xiout", [NCH, 128, TOK], F32, kind="ExternalOutput").ap()
    csb = load_consts(nc, caps)
    ONESB, ONESF = csb["ONESB"], csb["ONESF"]
    xt = nc.alloc_sbuf_tensor("xt", [128, NCH, TT], F32)
    sqb = nc.alloc_sbuf_tensor("sqb", [128, NCH, TT], BF16)
    rrow = nc.alloc_sbuf_tensor("rrow", [1, TT], F32)
    xo = nc.alloc_sbuf_tensor("xo", [128, NCH, TT], F32)
    pl1 = nc.alloc_psum_tensor("pl1", [1, TT], F32)
    pbc = nc.alloc_psum_tensor("pbc", [128, TT], F32)
    ld = nc.semaphore("ld").__enter__()
    st = nc.semaphore("st").__enter__()
    for i in range(NTILE):
        t0 = i * TT
        with nc.Block() as blk:
            @blk.sync
            def _(sy):
                sy.dma_start(out=xt.ap(), in_=x_dram_tile(xin, t0, TT)).then_inc(ld, 16)
                sy.wait_ge(ld, 16 * (i + 1))
        with nc.Block() as blk:
            @blk.scalar
            def _(s):
                s.activation(sqb.ap(), xt.ap(), AF.Square)
        with nc.Block() as blk:
            @blk.tensor
            def _(te):
                for c in range(NCH):
                    te.matmul(pl1[:], ONESB.ap()[:, 0:1], sqb.ap()[:, c, :],
                              start=(c == 0), stop=(c == NCH - 1))
        with nc.Block() as blk:
            @blk.scalar
            def _(s):
                s.activation(rrow.ap(), pl1[:], AF.Ln,
                             bias=csb["EPS5"].ap()[0:1, :], scale=1.0 / D)
                s.activation(rrow.ap(), rrow.ap(), AF.Exp, scale=-0.5)
        with nc.Block() as blk:
            @blk.tensor
            def _(te):
                te.matmul(pbc[:], ONESF.ap()[0:1, 0:128], rrow.ap(),
                          start=True, stop=True)
        with nc.Block() as blk:
            @blk.vector
            def _(v):
                v.tensor_tensor(xo.ap(), xt.ap(), bc_ap(pbc, NCH, TT), ALU.mult)
        with nc.Block() as blk:
            @blk.sync
            def _(sy):
                sy.dma_start(out=x_dram_tile(xiout, t0, TT), in_=xo.ap()
                             ).then_inc(st, 16)
                sy.wait_ge(st, 16 * (i + 1))
    return nc



HID = 2730
NHP = 22
SUP = 1024
NSUP = TOK // SUP


def hp_width(hp):
    return 42 if hp == NHP - 1 else 128


def build_iffn():
    nc = new_nc()
    caps = declare_consts(nc, ["ONESB", "ONESF"])
    xin = nc.dram_tensor("xin", [NCH, 128, TOK], F32, kind="ExternalInput").ap()
    xout = nc.dram_tensor("xout", [NCH, 128, TOK], F32, kind="ExternalOutput").ap()
    w1a = nc.dram_tensor("w1a", [D, HID], F32, kind="ExternalInput").ap()
    w1g = nc.dram_tensor("w1g", [D, HID], F32, kind="ExternalInput").ap()
    b1a = nc.dram_tensor("b1a", [128, NHP], F32, kind="ExternalInput").ap()
    b1g = nc.dram_tensor("b1g", [128, NHP], F32, kind="ExternalInput").ap()
    w2p = nc.dram_tensor("w2p", [NHP * 128, D], F32, kind="ExternalInput").ap()
    b2 = nc.dram_tensor("b2", [128, NCH], F32, kind="ExternalInput").ap()
    lng = nc.dram_tensor("lng", [128, NCH], F32, kind="ExternalInput").ap()
    lnb = nc.dram_tensor("lnb", [128, NCH], F32, kind="ExternalInput").ap()
    csb = load_consts(nc, caps)
    ONESB, ONESF = csb["ONESB"], csb["ONESF"]

    xt = nc.alloc_sbuf_tensor("xt", [128, NCH, SUP + 1], F32)
    sqb = nc.alloc_sbuf_tensor("sqb", [128, NCH, SUP], BF16)
    tln = nc.alloc_sbuf_tensor("tln", [128, NCH, SUP], F32)
    xb = nc.alloc_sbuf_tensor("xb", [128, NCH, SUP], BF16)
    th = nc.alloc_sbuf_tensor("th", [128, NHP, SUP], BF16)
    gsb = nc.alloc_sbuf_tensor("gsb", [128, TT], BF16)
    w1ab = nc.alloc_sbuf_tensor("w1ab", [128, NCH, 128], BF16)
    w1gb = nc.alloc_sbuf_tensor("w1gb", [128, NCH, 128], BF16)
    w2b = nc.alloc_sbuf_tensor("w2b", [128, NHP, 128], BF16)
    b1as = nc.alloc_sbuf_tensor("b1as", [128, NHP], F32)
    b1gs = nc.alloc_sbuf_tensor("b1gs", [128, NHP], F32)
    b2s = nc.alloc_sbuf_tensor("b2s", [128, NCH], F32)
    lngs = nc.alloc_sbuf_tensor("lngs", [128, NCH], F32)
    lnbs = nc.alloc_sbuf_tensor("lnbs", [128, NCH], F32)
    rowM = nc.alloc_sbuf_tensor("rowM", [1, SUP], F32)
    rowR = nc.alloc_sbuf_tensor("rowR", [1, SUP], F32)
    rowC = nc.alloc_sbuf_tensor("rowC", [1, SUP], F32)

    pg0 = nc.alloc_psum_tensor("pg0", [128, TT], F32)
    pg1 = nc.alloc_psum_tensor("pg1", [128, TT], F32)
    pg2 = nc.alloc_psum_tensor("pg2", [128, TT], F32)
    pg3 = nc.alloc_psum_tensor("pg3", [128, TT], F32)
    pbc0 = nc.alloc_psum_tensor("pbc0", [128, TT], F32)
    pbc1 = nc.alloc_psum_tensor("pbc1", [128, TT], F32)
    pl1 = nc.alloc_psum_tensor("pl1", [1, TT], F32)
    pl2 = nc.alloc_psum_tensor("pl2", [1, TT], F32)

    sems = {}
    for nm in ["ld", "st", "mm", "ev", "wld"]:
        sems[nm] = nc.semaphore(nm).__enter__()
    C = {k: 0 for k in sems}
    ld, st, mmS, evS, wldsem = (sems[k] for k in ["ld", "st", "mm", "ev", "wld"])

    def wload(g, dst_ap, src_ap, slow=False):
        g.dma_start(out=dst_ap, in_=src_ap,
                    allow_slow_non_contiguous=slow).then_inc(wldsem, 16)
        C["wld"] += 16

    with nc.Block() as blk:
        @blk.gpsimd
        def _(g):
            wload(g, b1as.ap(), b1a)
            wload(g, b1gs.ap(), b1g)
            wload(g, b2s.ap(), b2)
            wload(g, lngs.ap(), lng)
            wload(g, lnbs.ap(), lnb)
            g.wait_ge(wldsem, C["wld"])

    for sup in range(NSUP):
        t0 = sup * SUP
        bstart = (t0 % T) == 0
        with nc.Block() as blk:
            @blk.sync
            def _(sy):
                if bstart:
                    sy.dma_start(out=xt.ap()[:, :, 1:SUP + 1],
                                 in_=x_dram_tile(xin, t0, SUP)).then_inc(ld, 16)
                else:
                    sy.dma_start(out=xt.ap(),
                                 in_=xin[:, :, t0 - 1:t0 + SUP]
                                 .rearrange("c p t -> p c t")).then_inc(ld, 16)
                C["ld"] += 16
                sy.wait_ge(ld, C["ld"])
            @blk.vector
            def _(v):
                if bstart:
                    v.memset(xt.ap()[:, :, 0:1], 0.0)
        xsh_lo = xt.ap()[:, 0:4, 1:SUP + 1]
        xsh_hi = xt.ap()[:, 4:8, 0:SUP]
        with nc.Block() as blk:
            @blk.scalar
            def _(s):
                s.activation(sqb.ap()[:, 0:4, :], xsh_lo, AF.Square)
                s.activation(sqb.ap()[:, 4:8, :], xsh_hi, AF.Square)
        for half in range(2):
            hsl = slice(half * TT, (half + 1) * TT)
            with nc.Block() as blk:
                @blk.tensor
                def _(te):
                    for c in range(NCH):
                        off = 1 if c < 4 else 0
                        te.matmul(pl1[:], ONESF.ap()[:, 0:1],
                                  xt.ap()[:, c, off + half * TT:off + (half + 1) * TT],
                                  start=(c == 0), stop=(c == NCH - 1))
                    for c in range(NCH):
                        te.matmul(pl2[:], ONESB.ap()[:, 0:1],
                                  sqb.ap()[:, c, hsl],
                                  start=(c == 0), stop=(c == NCH - 1))
            with nc.Block() as blk:
                @blk.scalar
                def _(s):
                    s.activation(rowM.ap()[:, hsl], pl1[:], AF.Copy, scale=1.0 / D)
                    s.activation(rowR.ap()[:, hsl], pl1[:], AF.Square,
                                 scale=1.0 / D)
                    s.activation(rowC.ap()[:, hsl], pl2[:], AF.Copy, scale=1.0 / D)
            with nc.Block() as blk:
                @blk.vector
                def _(v):
                    v.tensor_sub(rowC.ap()[:, hsl], rowC.ap()[:, hsl],
                                 rowR.ap()[:, hsl])
            with nc.Block() as blk:
                @blk.scalar
                def _(s):
                    s.activation(rowR.ap()[:, hsl], rowC.ap()[:, hsl], AF.Ln,
                                 bias=csb["EPS5"].ap()[0:1, :])
                    s.activation(rowR.ap()[:, hsl], rowR.ap()[:, hsl],
                                 AF.Exp, scale=-0.5)
            with nc.Block() as blk:
                @blk.vector
                def _(v):
                    v.tensor_mul(rowC.ap()[:, hsl], rowM.ap()[:, hsl],
                                 rowR.ap()[:, hsl])
                    v.drain()
                    v.tensor_scalar_mul(rowC.ap()[:, hsl], rowC.ap()[:, hsl],
                                        -1.0)
            with nc.Block() as blk:
                @blk.tensor
                def _(te):
                    te.matmul(pbc0[:], ONESF.ap()[0:1, 0:128], rowR.ap()[:, hsl],
                              start=True, stop=True)
                    te.matmul(pbc1[:], ONESF.ap()[0:1, 0:128], rowC.ap()[:, hsl],
                              start=True, stop=True)
            with nc.Block() as blk:
                @blk.vector
                def _(v):
                    v.tensor_tensor(tln.ap()[:, 0:4, hsl],
                                    xt.ap()[:, 0:4, 1 + half * TT:1 + (half + 1) * TT],
                                    bc_ap(pbc0, 4, TT), ALU.mult)
                    v.tensor_tensor(tln.ap()[:, 4:8, hsl],
                                    xt.ap()[:, 4:8, half * TT:(half + 1) * TT],
                                    bc_ap(pbc0, 4, TT), ALU.mult)
            with nc.Block() as blk:
                @blk.vector
                def _(v):
                    v.tensor_tensor(tln.ap()[:, :, hsl], tln.ap()[:, :, hsl],
                                    bc_ap(pbc1, NCH, TT), ALU.add)
        with nc.Block() as blk:
            @blk.vector
            def _(v):
                for c in range(NCH):
                    v.tensor_scalar(xb.ap()[:, c, :], tln.ap()[:, c, :],
                                    lngs.ap()[:, c:c + 1], lnbs.ap()[:, c:c + 1],
                                    ALU.mult, ALU.add)
        with nc.Block() as blk:
            @blk.vector
            def _(v):
                v.memset(th.ap()[:, NHP - 1, :], 0.0)
        for hp in range(NHP):
            wdt = hp_width(hp)
            with nc.Block() as blk:
                @blk.gpsimd
                def _(g):
                    wload(g, w1ab.ap()[:, :, 0:wdt],
                          w1a[:, hp * 128:hp * 128 + wdt]
                          .rearrange("(c p) o -> p c o", p=128))
                    wload(g, w1gb.ap()[:, :, 0:wdt],
                          w1g[:, hp * 128:hp * 128 + wdt]
                          .rearrange("(c p) o -> p c o", p=128))
                    g.wait_ge(wldsem, C["wld"])
            MM0, EV0 = C["mm"], C["ev"]
            with nc.Block() as blk:
                @blk.tensor
                def _(te):
                    for tt2 in range(2):
                        hsl = slice(tt2 * TT, (tt2 + 1) * TT)
                        if tt2 >= 1:
                            te.wait_ge(evS, EV0 + 2 * tt2 - 1)
                        for c in range(NCH):
                            te.matmul(pg0[0:wdt, :] if tt2 == 0 else pg2[0:wdt, :],
                                      w1ab.ap()[:, c, 0:wdt], xb.ap()[:, c, hsl],
                                      start=(c == 0), stop=(c == NCH - 1)
                                      ).then_inc(mmS, 1) if c == NCH - 1 else \
                            te.matmul(pg0[0:wdt, :] if tt2 == 0 else pg2[0:wdt, :],
                                      w1ab.ap()[:, c, 0:wdt], xb.ap()[:, c, hsl],
                                      start=(c == 0), stop=False)
                        for c in range(NCH):
                            te.matmul(pg1[0:wdt, :] if tt2 == 0 else pg3[0:wdt, :],
                                      w1gb.ap()[:, c, 0:wdt], xb.ap()[:, c, hsl],
                                      start=(c == 0), stop=(c == NCH - 1)
                                      ).then_inc(mmS, 1) if c == NCH - 1 else \
                            te.matmul(pg1[0:wdt, :] if tt2 == 0 else pg3[0:wdt, :],
                                      w1gb.ap()[:, c, 0:wdt], xb.ap()[:, c, hsl],
                                      start=(c == 0), stop=False)
                @blk.scalar
                def _(s):
                    for tt2 in range(2):
                        pB = pg1 if tt2 == 0 else pg3
                        s.wait_ge(mmS, MM0 + 2 * tt2 + 2)
                        if tt2 == 1:
                            s.wait_ge(evS, EV0 + 2)
                        s.activation(gsb.ap()[0:wdt, :], pB[0:wdt, :], AF.Gelu,
                                     bias=b1gs.ap()[0:wdt, hp:hp + 1]
                                     ).then_inc(evS, 1)
                @blk.vector
                def _(v):
                    for tt2 in range(2):
                        pA = pg0 if tt2 == 0 else pg2
                        hsl = slice(tt2 * TT, (tt2 + 1) * TT)
                        v.wait_ge(evS, EV0 + 2 * tt2 + 1)
                        v.scalar_tensor_tensor(th.ap()[0:wdt, hp, hsl],
                                               pA[0:wdt, :],
                                               b1as.ap()[0:wdt, hp:hp + 1],
                                               gsb.ap()[0:wdt, :],
                                               ALU.add, ALU.mult).then_inc(evS, 1)
            C["mm"] += 4
            C["ev"] += 4
        for oc in range(NCH):
            with nc.Block() as blk:
                @blk.gpsimd
                def _(g):
                    wload(g, w2b.ap(),
                          w2p[:, oc * 128:(oc + 1) * 128]
                          .rearrange("(c p) o -> p c o", p=128))
                    g.wait_ge(wldsem, C["wld"])
            MM0 = C["mm"]
            with nc.Block() as blk:
                @blk.tensor
                def _(te):
                    for tt2 in range(2):
                        hsl = slice(tt2 * TT, (tt2 + 1) * TT)
                        pb = (pg0, pg1)[tt2]
                        for hp in range(NHP):
                            ins = te.matmul(pb[:], w2b.ap()[:, hp, :],
                                            th.ap()[:, hp, hsl],
                                            start=(hp == 0), stop=(hp == NHP - 1))
                        ins.then_inc(mmS, 1)
                @blk.vector
                def _(v):
                    for tt2 in range(2):
                        hsl1 = slice(1 + tt2 * TT, 1 + (tt2 + 1) * TT)
                        pb = (pg0, pg1)[tt2]
                        v.wait_ge(mmS, MM0 + tt2 + 1)
                        v.scalar_tensor_tensor(xt.ap()[:, oc, hsl1], pb[:],
                                               b2s.ap()[:, oc:oc + 1],
                                               xt.ap()[:, oc, hsl1],
                                               ALU.add, ALU.add)
            C["mm"] += 2
        with nc.Block() as blk:
            @blk.sync
            def _(sy):
                sy.dma_start(out=x_dram_tile(xout, t0, SUP),
                             in_=xt.ap()[:, :, 1:SUP + 1]).then_inc(st, 16)
                C["st"] += 16
                sy.wait_ge(st, C["st"])
    return nc





def iattn_windows():
    out = []
    for w in range(8):
        for sub in range(2):
            qt0 = w * 256 + sub * 128
            if w == 0:
                kw0, kwid = 0, 128 if sub == 0 else 256
                mn = "MSK_W0S0" if sub == 0 else "MSK_W0S1"
            else:
                kw0, kwid = (w - 1) * 256 + sub * 128, 384
                mn = "MSK_SUB0" if sub == 0 else "MSK_SUB1"
            out.append((qt0, kw0, kwid, mn))
    return out


IATTN_PH = 99
IATTN_NOE8 = False
IATTN_DEBUG = False


def build_iattn():
    PH = IATTN_PH
    nc = new_nc()
    dbg = {}
    cnames = ["I128B", "R128T", "E8SEL", "E8B",
              "MSK_SUB0", "MSK_SUB1", "MSK_W0S0", "MSK_W0S1"]
    caps = declare_consts(nc, cnames)
    if IATTN_DEBUG:
        for nm, shp, dt in (("d_qf", [128, 4, 2048], BF16),
                            ("d_kf", [128, 4, 2048], BF16),
                            ("d_vtm", [128, 16, 512], BF16),
                            ("d_Ob", [128, NCH, 2048], BF16),
                            ("d_xs", [128, NCH, 2048], BF16),
                            ("d_nq", [8, TT], BF16)):
            dbg[nm] = nc.dram_tensor(nm, shp, dt, kind="ExternalOutput").ap()
    rotaps = declare_consts(nc, ["ROT_CSQ", "ROT_UQ", "ROT_CSK", "ROT_UK"])
    xin = nc.dram_tensor("xin", [NCH, 128, TOK], F32, kind="ExternalInput").ap()
    xout = nc.dram_tensor("xout", [NCH, 128, TOK], F32, kind="ExternalOutput").ap()
    wqkv = nc.dram_tensor("wqkv", [D, 3 * D], F32, kind="ExternalInput").ap()
    wo = nc.dram_tensor("wo", [D, D], F32, kind="ExternalInput").ap()
    qs2 = nc.dram_tensor("qs2", [128, 1], F32, kind="ExternalInput").ap()
    ks2 = nc.dram_tensor("ks2", [128, 1], F32, kind="ExternalInput").ap()
    csb = load_consts(nc, caps)
    I128B, R128T, E8, E8B = csb["I128B"], csb["R128T"], csb["E8SEL"], csb["E8B"]

    xs = nc.alloc_sbuf_tensor("xs", [128, NCH, 2048], BF16)
    Ob = nc.alloc_sbuf_tensor("Ob", [128, NCH, 2048], BF16)
    qf = nc.alloc_sbuf_tensor("qf", [128, 4, 2048], BF16)
    kf = nc.alloc_sbuf_tensor("kf", [128, 4, 2048], BF16)
    vtm = nc.alloc_sbuf_tensor("vtm", [128, 16, 512], BF16)
    wbuf = nc.alloc_sbuf_tensor("wbuf", [128, NCH, 1024], BF16)
    tabq1 = nc.alloc_sbuf_tensor("tabq1", [128, 2048], BF16)
    tabq2 = nc.alloc_sbuf_tensor("tabq2", [128, 2048], BF16)
    tabk1 = nc.alloc_sbuf_tensor("tabk1", [128, 2048], BF16)
    tabk2 = nc.alloc_sbuf_tensor("tabk2", [128, 2048], BF16)
    qraw = nc.alloc_sbuf_tensor("qraw", [128, 4, TT], BF16)
    kraw = nc.alloc_sbuf_tensor("kraw", [128, 4, TT], BF16)
    tqb = nc.alloc_sbuf_tensor("tqb", [128, TT], BF16)
    csqb = nc.alloc_sbuf_tensor("csqb", [128, TT], BF16)
    usqb = nc.alloc_sbuf_tensor("usqb", [128, TT], BF16)
    sqs0 = nc.alloc_sbuf_tensor("sqs0", [128, TT], BF16)
    sqs1 = nc.alloc_sbuf_tensor("sqs1", [128, TT], BF16)
    nqf = nc.alloc_sbuf_tensor("nqf", [8, TT], F32)
    nq8 = nc.alloc_sbuf_tensor("nq8", [8, TT], BF16)
    nk8 = nc.alloc_sbuf_tensor("nk8", [8, TT], BF16)
    sS0 = nc.alloc_sbuf_tensor("sS0", [128, TT], F32)
    sS1 = nc.alloc_sbuf_tensor("sS1", [128, TT], F32)
    Pw = nc.alloc_sbuf_tensor("Pw", [128, 8, 384], BF16)
    PwN = nc.alloc_sbuf_tensor("PwN", [128, 8, 384], BF16)
    PTw = nc.alloc_sbuf_tensor("PTw", [128, 24, 128], BF16)
    negm0 = nc.alloc_sbuf_tensor("negm0", [128, 1], F32)
    negm1 = nc.alloc_sbuf_tensor("negm1", [128, 1], F32)
    zrow = nc.alloc_sbuf_tensor("zrow", [128, 1], F32)
    invzq = nc.alloc_sbuf_tensor("invzq", [128, 1], F32)
    junk1 = nc.alloc_sbuf_tensor("junk1", [128, 1], F32)
    qs2s = nc.alloc_sbuf_tensor("qs2s", [128, 1], F32)
    ks2s = nc.alloc_sbuf_tensor("ks2s", [128, 1], F32)
    xt = nc.alloc_sbuf_tensor("xt", [128, NCH, TT], F32)

    pg0 = nc.alloc_psum_tensor("pg0", [128, TT], F32)
    pg1 = nc.alloc_psum_tensor("pg1", [128, TT], F32)
    pg2 = nc.alloc_psum_tensor("pg2", [128, TT], F32)
    pg3 = nc.alloc_psum_tensor("pg3", [128, TT], F32)
    pq8 = nc.alloc_psum_tensor("pq8", [8, TT], F32)
    pk8 = nc.alloc_psum_tensor("pk8", [8, TT], F32)
    pav = nc.alloc_psum_tensor("pav", [64, 128], F32)

    sems = {}
    for nm in ["ld", "st", "mm", "ev", "tt", "ex", "pn", "tr", "av", "oe", "wld"]:
        sems[nm] = nc.semaphore(nm).__enter__()
    C = {k: 0 for k in sems}
    ld, st, mmS, evS, ttS = (sems[k] for k in ["ld", "st", "mm", "ev", "tt"])
    exS, pnS, trS, avS, oeS, wldsem = (
        sems[k] for k in ["ex", "pn", "tr", "av", "oe", "wld"])

    def wload(g, dst_ap, src_ap, slow=False):
        g.dma_start(out=dst_ap, in_=src_ap,
                    allow_slow_non_contiguous=slow).then_inc(wldsem, 16)
        C["wld"] += 16

    with nc.Block() as blk:
        @blk.gpsimd
        def _(g):
            wload(g, qs2s.ap(), qs2)
            wload(g, ks2s.ap(), ks2)
            g.wait_ge(wldsem, C["wld"])
    for tab, rot, col in ((tabq1, "ROT_CSQ", qs2s), (tabq2, "ROT_UQ", qs2s),
                          (tabk1, "ROT_CSK", ks2s), (tabk2, "ROT_UK", ks2s)):
        with nc.Block() as blk:
            @blk.sync
            def _(sy):
                sy.dma_start(out=xs.ap()[:, 0, :], in_=rotaps[rot]
                             ).then_inc(ld, 16)
                C["ld"] += 16
                sy.wait_ge(ld, C["ld"])
        with nc.Block() as blk:
            @blk.vector
            def _(v):
                v.tensor_scalar(tab.ap(), xs.ap()[:, 0, :], col.ap(), None,
                                ALU.mult)

    for b in range(B):
        tb0 = b * T
        with nc.Block() as blk:
            @blk.gpsimd
            def _(g):
                wload(g, xs.ap()[:, 0:4, :],
                      xin[0:4, :, tb0:tb0 + T].rearrange("c p t -> p c t"))
                wload(g, xs.ap()[:, 4:8, 1:T],
                      xin[4:8, :, tb0:tb0 + T - 1].rearrange("c p t -> p c t"))
                g.wait_ge(wldsem, C["wld"])
            @blk.vector
            def _(v):
                pass
        with nc.Block() as blk:
            @blk.vector
            def _(v):
                v.memset(xs.ap()[:, 4:8, 0:1], 0.0)
        for hg in range(2 if PH >= 9 else 1):
            if b > 0 and PH < 9:
                break
            with nc.Block() as blk:
                @blk.gpsimd
                def _(g):
                    wload(g, wbuf.ap()[:, :, 0:512],
                          wqkv[:, hg * 512:(hg + 1) * 512]
                          .rearrange("(c p) o -> p c o", p=128))
                    wload(g, wbuf.ap()[:, :, 512:1024],
                          wqkv[:, D + hg * 512:D + (hg + 1) * 512]
                          .rearrange("(c p) o -> p c o", p=128))
                    g.wait_ge(wldsem, C["wld"])
            for it in range(4 if PH >= 3 else 0):
                tsl = slice(it * TT, (it + 1) * TT)
                MM0, EV0 = C["mm"], C["ev"]
                qkocs = [("q", oc) for oc in range(4)] + \
                        [("k", oc) for oc in range(4)]
                MM0, EV0 = C["mm"], C["ev"]
                with nc.Block() as blk:
                    @blk.tensor
                    def _(te):
                        for idx, (which, oc) in enumerate(qkocs):
                            pb = (pg0, pg1)[idx % 2]
                            if idx >= 2:
                                te.wait_ge(evS, EV0 + idx - 1)
                            wcol = (0 if which == "q" else 512) + oc * 128
                            for c in range(NCH):
                                ins = te.matmul(pb[:],
                                                wbuf.ap()[:, c, wcol:wcol + 128],
                                                xs.ap()[:, c, tsl],
                                                start=(c == 0), stop=(c == NCH - 1))
                            ins.then_inc(mmS, 1)
                    @blk.scalar
                    def _(s):
                        for idx, (which, oc) in enumerate(qkocs):
                            pb = (pg0, pg1)[idx % 2]
                            dst = qraw if which == "q" else kraw
                            s.wait_ge(mmS, MM0 + idx + 1)
                            s.activation(dst.ap()[:, oc, :], pb[:], AF.Copy
                                         ).then_inc(evS, 1)
                C["mm"] += 8
                C["ev"] += 8
                for which, src, pq in (("q", qraw, pq8), ("k", kraw, pk8)):
                    MM0, EV0 = C["mm"], C["ev"]
                    with nc.Block() as blk:
                        @blk.scalar
                        def _(s):
                            for oc in range(4):
                                sqd = (sqs0, sqs1)[oc % 2]
                                if oc >= 2:
                                    s.wait_ge(mmS, MM0 + oc - 1)
                                s.activation(sqd.ap(), src.ap()[:, oc, :],
                                             AF.Square).then_inc(evS, 1)
                        @blk.tensor
                        def _(te):
                            for oc in range(4):
                                sqd = (sqs0, sqs1)[oc % 2]
                                te.wait_ge(evS, EV0 + oc + 1)
                                te.matmul(pq[:], E8.ap()[:, oc, :], sqd.ap(),
                                          start=(oc == 0), stop=(oc == 3)
                                          ).then_inc(mmS, 1)
                    C["mm"] += 4
                    C["ev"] += 4
                if PH < 4:
                    continue
                with nc.Block() as blk:
                    @blk.scalar
                    def _(s):
                        s.activation(nqf.ap(), pq8[:], AF.Sqrt)
                with nc.Block() as blk:
                    @blk.vector
                    def _(v):
                        v.tensor_scalar_max(nqf.ap(), nqf.ap(), 1e-12)
                        v.drain()
                        with nc.allow_low_precision(
                                reason="l2norm scale in bf16 is within bar"):
                            v.reciprocal(nq8.ap(), nqf.ap())
                with nc.Block() as blk:
                    @blk.scalar
                    def _(s):
                        s.activation(nqf.ap(), pk8[:], AF.Sqrt)
                with nc.Block() as blk:
                    @blk.vector
                    def _(v):
                        v.tensor_scalar_max(nqf.ap(), nqf.ap(), 1e-12)
                        v.drain()
                        with nc.allow_low_precision(
                                reason="l2norm scale in bf16 is within bar"):
                            v.reciprocal(nk8.ap(), nqf.ap())
                MM0, EV0, TT0 = C["mm"], C["ev"], C["tt"]
                rots = ([("q", cl) for cl in range(4)] +
                        [("k", cl) for cl in range(4)]) if PH >= 5 else []
                with nc.Block() as blk:
                    @blk.tensor
                    def _(te):
                        for idx, (which, cl) in enumerate(rots):
                            n8 = nq8 if which == "q" else nk8
                            if idx > 0:
                                te.wait_ge(ttS, TT0 + idx)
                            te.matmul(pg3[:], E8B.ap()[:, cl, :], n8.ap(),
                                      start=True, stop=True).then_inc(mmS, 1)
                            te.wait_ge(ttS, TT0 + idx + 1)
                            if idx > 0:
                                te.wait_ge(evS, EV0 + idx)
                            te.matmul(pg2[:], I128B.ap(), csqb.ap(),
                                      start=True, stop=False)
                            te.matmul(pg2[:], R128T.ap(), usqb.ap(),
                                      start=False, stop=True).then_inc(mmS, 1)
                    @blk.vector
                    def _(v):
                        for idx, (which, cl) in enumerate(rots):
                            src = qraw if which == "q" else kraw
                            tb1 = tabq1 if which == "q" else tabk1
                            tb2 = tabq2 if which == "q" else tabk2
                            v.wait_ge(mmS, MM0 + 2 * idx + 1)
                            v.tensor_tensor(tqb.ap(), src.ap()[:, cl, :], pg3[:],
                                            ALU.mult)
                            v.tensor_tensor(csqb.ap(), tqb.ap(),
                                            tb1.ap()[:, tsl], ALU.mult)
                            v.tensor_tensor(usqb.ap(), tqb.ap(),
                                            tb2.ap()[:, tsl], ALU.mult
                                            ).then_inc(ttS, 1)
                    @blk.scalar
                    def _(s):
                        for idx, (which, cl) in enumerate(rots):
                            dst = qf if which == "q" else kf
                            s.wait_ge(mmS, MM0 + 2 * idx + 2)
                            s.activation(dst.ap()[:, cl, tsl], pg2[:], AF.Copy
                                         ).then_inc(evS, 1)
                C["mm"] += 16
                C["ev"] += 8
                C["tt"] += 8
            if PH < 6:
                continue
            with nc.Block() as blk:
                @blk.gpsimd
                def _(g):
                    wload(g, wbuf.ap()[:, :, 0:512],
                          wqkv[:, 2 * D + hg * 512:2 * D + (hg + 1) * 512]
                          .rearrange("(c p) o -> p c o", p=128))
                    g.wait_ge(wldsem, C["wld"])
            for it in range(4):
                MM0, EV0 = C["mm"], C["ev"]
                with nc.Block() as blk:
                    @blk.tensor
                    def _(te):
                        for tb in range(4):
                            pb = (pg0, pg1)[tb % 2]
                            if tb >= 2:
                                te.wait_ge(evS, EV0 + tb - 1)
                            for c in range(NCH):
                                ins = te.matmul(
                                    pb[:],
                                    xs.ap()[:, c, it * TT + tb * 128:
                                            it * TT + (tb + 1) * 128],
                                    wbuf.ap()[:, c, 0:512],
                                    start=(c == 0), stop=(c == NCH - 1))
                            ins.then_inc(mmS, 1)
                    @blk.scalar
                    def _(s):
                        for tb in range(4):
                            pb = (pg0, pg1)[tb % 2]
                            s.wait_ge(mmS, MM0 + tb + 1)
                            s.activation(vtm.ap()[:, it * 4 + tb, :], pb[:],
                                         AF.Copy).then_inc(evS, 1)
                C["mm"] += 4
                C["ev"] += 4
            if IATTN_DEBUG and b == 0 and hg == 0:
                with nc.Block() as blk:
                    @blk.sync
                    def _(sy):
                        for nm, h in (("d_qf", qf), ("d_kf", kf), ("d_vtm", vtm),
                                      ("d_xs", xs), ("d_nq", nq8)):
                            sy.dma_start(out=dbg[nm], in_=h.ap()).then_inc(st, 16)
                            C["st"] += 16
                        sy.wait_ge(st, C["st"])
            for (qt0, kw0, kwid, mn) in (iattn_windows() if PH >= 7 else []):
                mask = csb[mn]
                nkc = kwid // 128
                MM0, TT0, EX0, PN0 = C["mm"], C["tt"], C["ex"], C["pn"]
                with nc.Block() as blk:
                    @blk.tensor
                    def _(te):
                        for hl in range(8):
                            cl, half = hl // 2, hl % 2
                            pb = (pg0, pg1)[hl % 2]
                            if hl >= 2:
                                te.wait_ge(ttS, TT0 + hl - 1)
                            te.matmul(pb[0:128, 0:kwid],
                                      qf.ap()[half * 64:(half + 1) * 64, cl,
                                              qt0:qt0 + 128],
                                      kf.ap()[half * 64:(half + 1) * 64, cl,
                                              kw0:kw0 + kwid],
                                      start=True, stop=True).then_inc(mmS, 1)
                    @blk.vector
                    def _(v):
                        for hl in range(8):
                            pb = (pg0, pg1)[hl % 2]
                            ss = (sS0, sS1)[hl % 2]
                            ng = (negm0, negm1)[hl % 2]
                            v.wait_ge(mmS, MM0 + hl + 1)
                            v.scalar_tensor_tensor(ss.ap()[:, 0:kwid],
                                                   pb[0:128, 0:kwid], 8.0,
                                                   mask.ap()[:, 0:kwid],
                                                   ALU.mult, ALU.add)
                            v.reduce_max(ng.ap(), ss.ap()[:, 0:kwid],
                                         axis=AX.X, negate=True)
                            v.tensor_copy(junk1.ap(), ng.ap()).then_inc(ttS, 1)
                            if hl > 0:
                                v.wait_ge(exS, EX0 + hl)
                                v.reduce_sum(zrow.ap(),
                                             Pw.ap()[:, hl - 1, 0:kwid], axis=AX.X)
                                v.drain()
                                v.reciprocal(invzq.ap(), zrow.ap())
                                v.drain()
                                v.tensor_scalar(PwN.ap()[:, hl - 1, 0:kwid],
                                                Pw.ap()[:, hl - 1, 0:kwid],
                                                invzq.ap(), None, ALU.mult
                                                ).then_inc(pnS, 1)
                        v.wait_ge(exS, EX0 + 8)
                        v.reduce_sum(zrow.ap(), Pw.ap()[:, 7, 0:kwid], axis=AX.X)
                        v.drain()
                        v.reciprocal(invzq.ap(), zrow.ap())
                        v.drain()
                        v.tensor_scalar(PwN.ap()[:, 7, 0:kwid],
                                        Pw.ap()[:, 7, 0:kwid],
                                        invzq.ap(), None, ALU.mult
                                        ).then_inc(pnS, 1)
                    @blk.scalar
                    def _(s):
                        for hl in range(8):
                            ss = (sS0, sS1)[hl % 2]
                            ng = (negm0, negm1)[hl % 2]
                            s.wait_ge(ttS, TT0 + hl + 1)
                            s.activation(Pw.ap()[:, hl, 0:kwid],
                                         ss.ap()[:, 0:kwid], AF.Exp,
                                         bias=ng.ap()).then_inc(exS, 1)
                C["mm"] += 8
                C["tt"] += 8
                C["ex"] += 8
                C["pn"] += 8
                if PH < 8:
                    continue
                with nc.Block() as blk:
                    @blk.sync
                    def _(sy):
                        if kwid == 384:
                            sy.dma_start_transpose(
                                PTw.ap()[:, 0:8 * nkc, :],
                                PwN.ap().rearrange("p h k -> p (h k)")
                                ).then_inc(trS, 16)
                            C["tr"] += 16
                        else:
                            for hl in range(8):
                                sy.dma_start_transpose(
                                    PTw.ap()[:, hl * nkc:(hl + 1) * nkc, :],
                                    PwN.ap()[:, hl, 0:kwid]).then_inc(trS, 16)
                                C["tr"] += 16
                        sy.wait_ge(trS, C["tr"])
                if PH < 9:
                    continue
                AV0, OE0 = C["av"], C["oe"]
                with nc.Block() as blk:
                    @blk.tensor
                    def _(te):
                        for hl in range(8):
                            cl, half = hl // 2, hl % 2
                            if hl > 0:
                                te.wait_ge(oeS, OE0 + hl)
                            for kc in range(nkc):
                                ins = te.matmul(
                                    pav[:],
                                    vtm.ap()[:, kw0 // 128 + kc,
                                             hl * 64:(hl + 1) * 64],
                                    PTw.ap()[:, hl * nkc + kc, :],
                                    start=(kc == 0), stop=(kc == nkc - 1))
                            ins.then_inc(avS, 1)
                    @blk.scalar
                    def _(s):
                        for hl in range(8):
                            gch = (hg * 8 + hl) // 2
                            half = hl % 2
                            s.wait_ge(avS, AV0 + hl + 1)
                            s.activation(Ob.ap()[half * 64:(half + 1) * 64, gch,
                                                 qt0:qt0 + 128],
                                         pav[:], AF.Copy).then_inc(oeS, 1)
                C["av"] += 8
                C["oe"] += 8
        if IATTN_DEBUG and b == 0:
            with nc.Block() as blk:
                @blk.sync
                def _(sy):
                    sy.dma_start(out=dbg["d_Ob"], in_=Ob.ap()).then_inc(st, 16)
                    C["st"] += 16
                    sy.wait_ge(st, C["st"])
        if PH < 9:
            continue
        with nc.Block() as blk:
            @blk.gpsimd
            def _(g):
                wload(g, wbuf.ap()[:, :, 0:1024],
                      wo.rearrange("(c p) o -> p c o", p=128))
                g.wait_ge(wldsem, C["wld"])
        for it in range(4):
            t0 = tb0 + it * TT
            with nc.Block() as blk:
                @blk.sync
                def _(sy):
                    sy.dma_start(out=xt.ap(), in_=x_dram_tile(xin, t0, TT)
                                 ).then_inc(ld, 16)
                    C["ld"] += 16
                    sy.wait_ge(ld, C["ld"])
            MM0, EV0 = C["mm"], C["ev"]
            with nc.Block() as blk:
                @blk.tensor
                def _(te):
                    for oc in range(NCH):
                        pb = (pg0, pg1)[oc % 2]
                        if oc >= 2:
                            te.wait_ge(evS, EV0 + oc - 1)
                        for c in range(NCH):
                            ins = te.matmul(pb[:],
                                            wbuf.ap()[:, c, oc * 128:(oc + 1) * 128],
                                            Ob.ap()[:, c, it * TT:(it + 1) * TT],
                                            start=(c == 0), stop=(c == NCH - 1))
                        ins.then_inc(mmS, 1)
                @blk.vector
                def _(v):
                    for oc in range(NCH):
                        pb = (pg0, pg1)[oc % 2]
                        v.wait_ge(mmS, MM0 + oc + 1)
                        v.tensor_tensor(xt.ap()[:, oc, :], pb[:],
                                        xt.ap()[:, oc, :], ALU.add
                                        ).then_inc(evS, 1)
            C["mm"] += NCH
            C["ev"] += NCH
            with nc.Block() as blk:
                @blk.sync
                def _(sy):
                    sy.dma_start(out=x_dram_tile(xout, t0, TT), in_=xt.ap()
                                 ).then_inc(st, 16)
                    C["st"] += 16
                    sy.wait_ge(st, C["st"])
    return nc





def build_final():
    nc = new_nc()
    caps = declare_consts(nc, ["ONESB", "ONESF"])
    xin = nc.dram_tensor("xin", [NCH, 128, TOK], F32, kind="ExternalInput").ap()
    xiin = nc.dram_tensor("xiin", [NCH, 128, TOK], F32, kind="ExternalInput").ap()
    m0in = nc.dram_tensor("m0in", [1, TOK], F32, kind="ExternalInput").ap()
    xout = nc.dram_tensor("xout", [NCH, 128, TOK], F32, kind="ExternalOutput").ap()
    csb = load_consts(nc, caps)
    xt = nc.alloc_sbuf_tensor("xt", [128, NCH, TT], F32)
    fit = nc.alloc_sbuf_tensor("fit", [128, NCH, TT], F32)
    yb = nc.alloc_sbuf_tensor("yb", [128, NCH, TT], BF16)
    sqb = nc.alloc_sbuf_tensor("sqb", [128, NCH, TT], BF16)
    tmpb = nc.alloc_sbuf_tensor("tmpb", [128, NCH, TT], F32)
    m0sb = nc.alloc_sbuf_tensor("m0sb", [1, TOK], F32)
    rowA = nc.alloc_sbuf_tensor("rowA", [1, TT], F32)
    rowB = nc.alloc_sbuf_tensor("rowB", [1, TT], F32)
    rowC = nc.alloc_sbuf_tensor("rowC", [1, TT], F32)
    pg0 = nc.alloc_psum_tensor("pg0", [128, TT], F32)
    pg1 = nc.alloc_psum_tensor("pg1", [128, TT], F32)
    pl1 = nc.alloc_psum_tensor("pl1", [1, TT], F32)
    pl2 = nc.alloc_psum_tensor("pl2", [1, TT], F32)
    ld = nc.semaphore("ld").__enter__()
    st = nc.semaphore("st").__enter__()
    with nc.Block() as blk:
        @blk.sync
        def _(sy):
            sy.dma_start(out=m0sb.ap(), in_=m0in).then_inc(ld, 16)
            sy.wait_ge(ld, 16)
    for i in range(NTILE):
        t0 = i * TT
        with nc.Block() as blk:
            @blk.sync
            def _(sy):
                sy.dma_start(out=xt.ap(), in_=x_dram_tile(xin, t0, TT)
                             ).then_inc(ld, 16)
                sy.dma_start(out=fit.ap(), in_=x_dram_tile(xiin, t0, TT)
                             ).then_inc(ld, 16)
                sy.wait_ge(ld, 16 + 32 * (i + 1))
        with nc.Block() as blk:
            @blk.vector
            def _(v):
                v.tensor_copy(yb.ap(), fit.ap())
        emit_lnwrap(nc, csb, yb, xt, sqb, rowA, rowB, rowC,
                    pl1[:], pl2[:], pg0, pg1, tmpb,
                    extra_row=m0sb.ap()[:, t0:t0 + TT])
        with nc.Block() as blk:
            @blk.sync
            def _(sy):
                sy.dma_start(out=x_dram_tile(xout, t0, TT), in_=xt.ap()
                             ).then_inc(st, 16)
                sy.wait_ge(st, 16 * (i + 1))
    return nc



def fm_shards(x):
    shards = []
    for c in range(NCORES):
        xc = np.asarray(x[c * B:(c + 1) * B]).reshape(TOK, D).T
        shards.append(np.ascontiguousarray(xc).reshape(NCH, 128, TOK)
                      .astype(np.float32))
    return shards


def unshard_out(shards):
    out = np.empty((NCORES * B, T, D), dtype=np.float32)
    for c in range(NCORES):
        out[c * B:(c + 1) * B] = shards[c].reshape(D, TOK).T.reshape(B, T, D)
    return out


_STAGE_CACHE = {}

STAGE_CONSTS = {
    "prep": ["ONESF"],
    "A0": ["ONESB", "ONESF"],
    "F0": ["ONESB", "ONESF"],
    "A1": ["ONESB", "ONESF", "I16F", "MASKTRI", "E8SEL"],
    "hprep": ["ONESB", "ONESF"],
    "iffn": ["ONESB", "ONESF"],
    "final": ["ONESB", "ONESF"],
    "iattn": ["I128B", "R128T", "E8SEL", "E8B",
              "ROT_CSQ", "ROT_UQ", "ROT_CSK", "ROT_UK",
              "MSK_SUB0", "MSK_SUB1", "MSK_W0S0", "MSK_W0S1"],
}


def get_stage(name):
    if name not in _STAGE_CACHE:
        _STAGE_CACHE[name] = globals()[f"build_{name}"]()
    return _STAGE_CACHE[name]


def run_stage(name, per_core_maps):
    nc = get_stage(name)
    cmaps = []
    for m in per_core_maps:
        mm = dict(m)
        for k in STAGE_CONSTS[name]:
            mm[k] = CONSTS[k]
        cmaps.append(mm)
    res = run_bass_kernel_spmd(nc, cmaps, core_ids=list(range(NCORES)))
    return res.results


def _iattn_maps(inputs, lyr):
    return {
        "wqkv": np.ascontiguousarray(inputs["L1_ffn_Wqkv"][lyr]),
        "wo": np.ascontiguousarray(inputs["L1_ffn_Wo"][lyr]),
        "qs2": np.ascontiguousarray(
            np.tile(np.asarray(inputs["L1_ffn_q_scale"][lyr]), 2).reshape(128, 1)),
        "ks2": np.ascontiguousarray(
            np.tile(np.asarray(inputs["L1_ffn_k_scale"][lyr]), 2).reshape(128, 1)),
    }


def _iffn_maps(inputs, lyr):
    W1 = np.asarray(inputs["L1_ffn_W1"][lyr])
    W2 = np.asarray(inputs["L1_ffn_W2"][lyr])
    b1 = np.asarray(inputs["L1_ffn_b1"][lyr])

    def padbias(v):
        vp = np.zeros(NHP * 128, np.float32)
        vp[:HID] = v
        return np.ascontiguousarray(vp.reshape(NHP, 128).T)
    w2p = np.zeros((NHP * 128, D), np.float32)
    w2p[:HID] = W2
    col = lambda v: np.ascontiguousarray(np.asarray(v).reshape(NCH, 128).T)
    return {
        "w1a": np.ascontiguousarray(W1[:, :HID]),
        "w1g": np.ascontiguousarray(W1[:, HID:]),
        "b1a": padbias(b1[:HID]), "b1g": padbias(b1[HID:]),
        "w2p": w2p,
        "b2": col(inputs["L1_ffn_b2"][lyr]),
        "lng": col(inputs["L1_ffn_ln_g"][lyr]),
        "lnb": col(inputs["L1_ffn_ln_b"][lyr]),
    }


def kernel(**inputs):
    inputs = {k: np.asarray(v) for k, v in inputs.items()}
    X = fm_shards(inputs["L0_attn_X_embed"])

    res = run_stage("prep", [{"xin": X[c]} for c in range(NCORES)])
    X = [res[c]["xout"] for c in range(NCORES)]
    m0 = [res[c]["m0out"] for c in range(NCORES)]

    a0m = {"Wq": inputs["L0_attn_Wq"], "Wk": inputs["L0_attn_Wk"],
           "Wv": inputs["L0_attn_Wv"], "Wo": inputs["L0_attn_Wo"],
           "bq": inputs["L0_attn_bq"].reshape(1, -1),
           "bk": inputs["L0_attn_bk"].reshape(1, -1),
           "bv": inputs["L0_attn_bv"].reshape(1, -1),
           "bo": inputs["L0_attn_bo"].reshape(1, -1)}
    res = run_stage("A0", [{"xin": X[c], **a0m} for c in range(NCORES)])
    X = [res[c]["xout"] for c in range(NCORES)]

    f0m = {"WrT": np.ascontiguousarray(inputs["L0_ffn_W_r"].T),
           "WkT": np.ascontiguousarray(inputs["L0_ffn_W_k"].T),
           "WvT": np.ascontiguousarray(inputs["L0_ffn_W_v"].T),
           "mur": inputs["L0_ffn_mu_r"].reshape(1, -1),
           "muk": inputs["L0_ffn_mu_k"].reshape(1, -1)}
    res = run_stage("F0", [{"xin": X[c], **f0m} for c in range(NCORES)])
    X = [res[c]["xout"] for c in range(NCORES)]

    a1m = {"wqkv": inputs["L1_attn_w_qkv"],
           "bqkv": inputs["L1_attn_b_qkv"].reshape(1, -1),
           "wout": inputs["L1_attn_w_out"],
           "bout": inputs["L1_attn_b_out"].reshape(1, -1)}
    res = run_stage("A1", [{"xin": X[c], **a1m} for c in range(NCORES)])
    X = [res[c]["xout"] for c in range(NCORES)]

    res = run_stage("hprep", [{"xin": X[c]} for c in range(NCORES)])
    XI = [res[c]["xiout"] for c in range(NCORES)]

    for lyr in range(6):
        am = _iattn_maps(inputs, lyr)
        res = run_stage("iattn", [{"xin": XI[c], **am} for c in range(NCORES)])
        XI = [res[c]["xout"] for c in range(NCORES)]
        fm = _iffn_maps(inputs, lyr)
        res = run_stage("iffn", [{"xin": XI[c], **fm} for c in range(NCORES)])
        XI = [res[c]["xout"] for c in range(NCORES)]

    res = run_stage("final", [{"xin": X[c], "xiin": XI[c], "m0in": m0[c]}
                              for c in range(NCORES)])
    out = [res[c]["xout"] for c in range(NCORES)]
    return unshard_out(out)


# version 04 (final)
# speedup vs baseline: 1.0044x; 1.0044x over previous
import numpy as np
import ml_dtypes
import concourse.bass as bass
import concourse.mybir as mybir
from concourse.bass_utils import run_bass_kernel_spmd

F32 = mybir.dt.float32
BF16 = mybir.dt.bfloat16
AF = mybir.ActivationFunctionType
ALU = mybir.AluOpType
AX = mybir.AxisListType
BF = ml_dtypes.bfloat16

D = 1024
NCH = 8
B = 2
T = 2048
TOK = B * T
TT = 512
NTILE = TOK // TT
NCORES = 8
LNEPS = 1e-5


def host_consts():
    c = {}
    c["ONESB"] = np.ones((128, 512), dtype=BF)
    c["ONESF"] = np.ones((128, 512), dtype=np.float32)
    c["I128F"] = np.eye(128, dtype=np.float32)
    c["I16F"] = np.eye(16, dtype=np.float32)
    e2 = np.zeros((128, NCH, 16), dtype=BF)
    for cc in range(NCH):
        e2[0:64, cc, 2 * cc] = 1
        e2[64:128, cc, 2 * cc + 1] = 1
    c["E2SEL"] = e2
    e2b = np.zeros((16, NCH, 128), dtype=BF)
    for cc in range(NCH):
        e2b[2 * cc, cc, 0:64] = 1
        e2b[2 * cc + 1, cc, 64:128] = 1
    c["E2B"] = e2b
    e8 = np.zeros((128, 4, 8), dtype=BF)
    for cl in range(4):
        e8[0:64, cl, 2 * cl] = 1
        e8[64:128, cl, 2 * cl + 1] = 1
    c["E8SEL"] = e8
    m = np.zeros((128, 128), dtype=np.float32)
    m[np.triu_indices(128, 1)] = -1e30
    c["MASKTRI"] = m
    DIM_HEAD, WINDOW = 64, 256
    half = np.arange(0, DIM_HEAD, 2, dtype=np.float32)
    inv_freq = 1.0 / 10000.0 ** (half / DIM_HEAD)
    t = np.arange(T, dtype=np.float32)
    freqs = t[:, None] * inv_freq[None, :]
    freqs = np.concatenate([freqs, freqs], -1)
    base_scale = (half + 0.4 * DIM_HEAD) / (1.4 * DIM_HEAD)
    power = (t - T // 2) / (WINDOW // 2)
    scale = base_scale[None, :] ** power[:, None]
    scale = np.concatenate([scale, scale], -1)
    cos, sin = np.cos(freqs), np.sin(freqs)

    def u_of(ss):
        u = np.empty_like(ss)
        u[0:32] = ss[32:64]
        u[32:64] = ss[0:32]
        return u
    cs_q = (cos * scale).T.astype(np.float32)
    ss_q = (sin * scale).T.astype(np.float32)
    inv = (1.0 / scale)
    cs_k = (cos * inv).T.astype(np.float32)
    ss_k = (sin * inv).T.astype(np.float32)
    c["ROT_CSQ"] = np.vstack([cs_q, cs_q]).astype(BF)
    c["ROT_UQ"] = np.vstack([u_of(ss_q), u_of(ss_q)]).astype(BF)
    c["ROT_CSK"] = np.vstack([cs_k, cs_k]).astype(BF)
    c["ROT_UK"] = np.vstack([u_of(ss_k), u_of(ss_k)]).astype(BF)
    c["I128B"] = np.eye(128, dtype=BF)
    e8b = np.zeros((8, 4, 128), dtype=BF)
    for cl in range(4):
        e8b[2 * cl, cl, 0:64] = 1
        e8b[2 * cl + 1, cl, 64:128] = 1
    c["E8B"] = e8b
    R64 = np.zeros((64, 64), dtype=np.float32)
    for y in range(32):
        R64[y, y + 32] = -1.0
    for y in range(32, 64):
        R64[y, y - 32] = 1.0
    RT = np.zeros((128, 128), dtype=BF)
    RT[0:64, 0:64] = R64.T.astype(BF)
    RT[64:128, 64:128] = R64.T.astype(BF)
    c["R128T"] = RT

    def band_mask(iq0, jl0):
        mm = np.full((128, 384), -1e30, dtype=np.float32)
        for i in range(128):
            lo, hi = iq0 + i, iq0 + i + 256
            jl = jl0 + np.arange(384)
            mm[i, (jl >= lo) & (jl <= hi)] = 0.0
        return mm
    c["MSK_SUB0"] = band_mask(0, 0).astype(BF)
    c["MSK_SUB1"] = band_mask(128, 128).astype(BF)
    m0s0 = np.full((128, 128), -1e30, dtype=np.float32)
    m0s0[np.tril_indices(128)] = 0.0
    c["MSK_W0S0"] = m0s0.astype(BF)
    m0s1 = np.full((128, 256), -1e30, dtype=np.float32)
    for i in range(128):
        m0s1[i, : 129 + i] = 0.0
    c["MSK_W0S1"] = m0s1.astype(BF)
    return c


CONSTS = host_consts()


def new_nc():
    nc = bass.Bass("TRN2", target_bir_lowering=False)
    return nc


def declare_consts(nc, names=None):
    aps = {}
    for k, v in CONSTS.items():
        if names is not None and k not in names:
            continue
        dt = BF16 if v.dtype == BF else F32
        aps[k] = nc.dram_tensor(k, list(v.shape), dt, kind="ExternalInput").ap()
    return aps


def load_consts(nc, caps):
    sb = {}
    for k, ap in caps.items():
        v = CONSTS[k]
        dt = BF16 if v.dtype == BF else F32
        sb[k] = nc.alloc_sbuf_tensor(f"c_{k}", list(v.shape), dt)
    sb["EPS5"] = nc.alloc_sbuf_tensor("c_EPS5", [128, 1], F32)
    sb["EPS6"] = nc.alloc_sbuf_tensor("c_EPS6", [128, 1], F32)
    sem = nc.semaphore("c_ld").__enter__()
    with nc.Block() as blk:
        @blk.sync
        def _(sy):
            n = 0
            for k in caps:
                sy.dma_start(out=sb[k].ap(), in_=caps[k]).then_inc(sem, 16)
                n += 1
            sy.wait_ge(sem, 16 * n)
        @blk.vector
        def _(v):
            v.memset(sb["EPS5"].ap(), LNEPS)
            v.memset(sb["EPS6"].ap(), 1e-6)
    nc.const_aps.register(F32, LNEPS, sb["EPS5"].ap())
    nc.const_aps.register(F32, 1e-6, sb["EPS6"].ap())
    return sb


import contextlib


def _stage_allocs(nc, P):
    stk = contextlib.ExitStack()
    sems = []

    def SB(nm, shp, dt):
        return stk.enter_context(nc.sbuf_tensor(P + nm, shp, dt))

    def PS(nm, shp, dt=F32):
        return stk.enter_context(nc.psum_tensor(P + nm, shp, dt))

    def SEM(nm):
        h = stk.enter_context(nc.semaphore(P + nm))
        sems.append(h)
        return h

    def finish():
        with nc.Block() as blk:
            @blk.gpsimd
            def _(g):
                for h in sems:
                    g.sem_clear(h)
        stk.close()
    return stk, SB, PS, SEM, finish


def bc_ap(h, nch, tt):
    return bass.AP(h, 0, [[tt, 128], [0, nch], [1, tt]])


def x_dram_tile(xap, t0, tt):
    return xap[:, :, t0:t0 + tt].rearrange("c p t -> p c t")


def emit_lnwrap(nc, csb, y_sb, x_sb, sq_sb, rowA, rowB, rowC,
                prS1, prS2, pbc0, pbc1, tmp_sb, extra_row=None):
    ONESB, ONESF = csb["ONESB"], csb["ONESF"]
    with nc.Block() as blk:
        @blk.scalar
        def _(s):
            s.activation(sq_sb.ap(), y_sb.ap(), AF.Square)
    with nc.Block() as blk:
        @blk.tensor
        def _(te):
            for c in range(NCH):
                te.matmul(prS1, ONESB.ap()[:, 0:1], y_sb.ap()[:, c, :],
                          start=(c == 0), stop=(c == NCH - 1))
            for c in range(NCH):
                te.matmul(prS2, ONESB.ap()[:, 0:1], sq_sb.ap()[:, c, :],
                          start=(c == 0), stop=(c == NCH - 1))
    with nc.Block() as blk:
        @blk.scalar
        def _(s):
            s.activation(rowA[:], prS1, AF.Copy, scale=1.0 / D)
            s.activation(rowB[:], rowA[:], AF.Square)
            s.activation(rowC[:], prS2, AF.Copy, scale=1.0 / D)
    with nc.Block() as blk:
        @blk.vector
        def _(v):
            v.tensor_sub(rowC[:], rowC[:], rowB[:])
    with nc.Block() as blk:
        @blk.scalar
        def _(s):
            s.activation(rowB[:], rowC[:], AF.Ln, bias=csb["EPS5"].ap()[0:1, :])
            s.activation(rowB[:], rowB[:], AF.Exp, scale=-0.5)
    with nc.Block() as blk:
        @blk.vector
        def _(v):
            v.tensor_mul(rowC[:], rowA[:], rowB[:])
            v.drain()
            v.tensor_scalar_mul(rowC[:], rowC[:], -1.0)
            if extra_row is not None:
                v.drain()
                v.tensor_add(rowC[:], rowC[:], extra_row)
    with nc.Block() as blk:
        @blk.tensor
        def _(te):
            te.matmul(pbc0[:], ONESF.ap()[0:1, 0:128], rowB[:], start=True, stop=True)
            te.matmul(pbc1[:], ONESF.ap()[0:1, 0:128], rowC[:], start=True, stop=True)
    with nc.Block() as blk:
        @blk.vector
        def _(v):
            v.tensor_tensor(tmp_sb.ap(), y_sb.ap(), bc_ap(pbc0, NCH, TT),
                            ALU.mult)
    with nc.Block() as blk:
        @blk.vector
        def _(v):
            v.tensor_tensor(tmp_sb.ap(), tmp_sb.ap(), bc_ap(pbc1, NCH, TT),
                            ALU.add)
    with nc.Block() as blk:
        @blk.vector
        def _(v):
            v.tensor_add(x_sb.ap(), tmp_sb.ap(), x_sb.ap())



def emit_prep(nc, csb, P, io):
    stk, SB, PS, SEM, _finish = _stage_allocs(nc, P)
    xin = io["xin"]
    xout = io["xout"]
    m0out = io["m0out"]
    ONESF = csb["ONESF"]
    with (
        nc.sbuf_tensor("xt", [128, NCH, TT], F32) as xt,
        nc.sbuf_tensor("mrow", [1, TT], F32) as mrow,
        nc.sbuf_tensor("xo", [128, NCH, TT], F32) as xo,
        nc.psum_tensor("ps1", [1, TT], F32) as ps1,
        nc.psum_tensor("pmb", [128, TT], F32) as pmb,
        nc.semaphore("ld") as ld,
        nc.semaphore("st") as st,
    ):
        for i in range(NTILE):
            t0 = i * TT
            with nc.Block() as blk:
                @blk.sync
                def _(sy):
                    sy.dma_start(out=xt[:], in_=x_dram_tile(xin, t0, TT)).then_inc(ld, 16)
                    sy.wait_ge(ld, 16 * (i + 1))
            with nc.Block() as blk:
                @blk.tensor
                def _(te):
                    for c in range(NCH):
                        te.matmul(ps1[:], ONESF.ap()[:, 0:1], xt[:, c, :],
                                  start=(c == 0), stop=(c == NCH - 1))
            with nc.Block() as blk:
                @blk.scalar
                def _(s):
                    s.activation(mrow[:], ps1[:], AF.Copy, scale=-1.0 / D)
            with nc.Block() as blk:
                @blk.tensor
                def _(te):
                    te.matmul(pmb[:], ONESF.ap()[0:1, 0:128], mrow[:], start=True, stop=True)
            with nc.Block() as blk:
                @blk.vector
                def _(v):
                    v.tensor_tensor(xo[:], xt[:], bc_ap(pmb, NCH, TT), ALU.add)
            with nc.Block() as blk:
                @blk.scalar
                def _(s):
                    s.activation(mrow[:], mrow[:], AF.Copy, scale=-1.0)
            with nc.Block() as blk:
                @blk.sync
                def _(sy):
                    sy.dma_start(out=x_dram_tile(xout, t0, TT), in_=xo[:]).then_inc(st, 16)
                    sy.dma_start(out=m0out[:, t0:t0 + TT], in_=mrow[:]).then_inc(st, 16)
                    sy.wait_ge(st, 32 * (i + 1))
    _finish()



def emit_A0(nc, csb, P, io):
    stk, SB, PS, SEM, _finish = _stage_allocs(nc, P)
    xin = io["xin"]
    xout = io["xout"]
    wnames = ["Wq", "Wk", "Wv", "Wo"]
    wd = {w: io[w] for w in wnames}
    bnames = ["bq", "bk", "bv", "bo"]
    bd = {b: io[b] for b in bnames}
    ONESB, ONESF = csb["ONESB"], csb["ONESF"]

    wsb = {w: SB(f"w_{w}", [128, NCH, D], BF16) for w in wnames}
    bsb = {b: SB(f"b_{b}", [1, D], BF16) for b in bnames}
    wldsem = SEM("wld")
    with nc.Block() as blk:
        @blk.gpsimd
        def _(g):
            n = 0
            for w in wnames:
                g.dma_start(out=wsb[w].ap(),
                            in_=wd[w].rearrange("(c p) o -> p c o", p=128)
                            ).then_inc(wldsem, 16)
                n += 1
            for b in bnames:
                g.dma_start(out=bsb[b].ap(), in_=bd[b]).then_inc(wldsem, 16)
                n += 1
            g.wait_ge(wldsem, 16 * n)

    if True:
        xt = SB("xt", [128, NCH, TT], F32)
        sqb = SB("sqb", [128, NCH, TT], BF16)
        rrow = SB("rrow", [1, TT], F32)
        rowB = SB("rowB", [1, TT], F32)
        rowC = SB("rowC", [1, TT], F32)
        xb = SB("xb", [128, NCH, TT], BF16)
        qt = SB("qt", [128, NCH, TT], BF16)
        qsw = SB("qsw", [128, NCH, TT], BF16)
        kt = SB("kt", [128, NCH, TT], BF16)
        vt = SB("vt", [128, NCH, TT], BF16)
        pbuf = SB("pbuf", [128, 2048], BF16)
        zpart = SB("zpart", [128, 4], F32)
        ztot = SB("ztot", [128, 1], F32)
        invzf = SB("invzf", [128, 1], F32)
        invz = SB("invz", [128, 1], BF16)
        r4row = SB("r4row", [1, 2048], F32)
        outb = SB("outb", [128, NCH, TT], BF16)
        yb = SB("yb", [128, NCH, TT], BF16)
        tmp1 = SB("tmp1", [128, NCH, TT], F32)
        pg0 = PS("pg0", [128, TT], F32)
        pg1 = PS("pg1", [128, TT], F32)
        pg2 = PS("pg2", [128, TT], F32)
        pg3 = PS("pg3", [128, TT], F32)
        pr = PS("pr", [1, 2048], F32)
        ld = SEM("ld")
        st = SEM("st")
        mmS = SEM("mm")
        evS = SEM("ev")
        vzS = SEM("vz")
        rrS = SEM("rr")
        pgs = [pg0, pg1, pg2, pg3]
        mmc, evc, vzc, rrc = [0], [0], [0], [0]

        for i in range(NTILE):
            t0 = i * TT
            with nc.Block() as blk:
                @blk.sync
                def _(sy):
                    sy.dma_start(out=xt[:], in_=x_dram_tile(xin, t0, TT)).then_inc(ld, 16)
                    sy.wait_ge(ld, 48 * i + 16)
            with nc.Block() as blk:
                @blk.scalar
                def _(s):
                    s.activation(sqb[:], xt[:], AF.Square)
            with nc.Block() as blk:
                @blk.tensor
                def _(te):
                    for c in range(NCH):
                        te.matmul(pr[:, 0:TT], ONESB.ap()[:, 0:1], sqb[:, c, :],
                                  start=(c == 0), stop=(c == NCH - 1))
            with nc.Block() as blk:
                @blk.scalar
                def _(s):
                    s.activation(rrow[:], pr[:, 0:TT], AF.Ln,
                                 bias=csb["EPS5"].ap()[0:1, :], scale=1.0 / D)
                    s.activation(rrow[:], rrow[:], AF.Exp, scale=-0.5)
            with nc.Block() as blk:
                @blk.tensor
                def _(te):
                    te.matmul(pg0[:], ONESF.ap()[0:1, 0:128], rrow[:], start=True, stop=True)
            with nc.Block() as blk:
                @blk.vector
                def _(v):
                    v.tensor_tensor(xb[:], xt[:], bc_ap(pg0, NCH, TT), ALU.mult)
            sched = [(w, dst, b, oc) for (w, dst, b) in
                     (("Wq", qt, "bq"), ("Wk", kt, "bk"), ("Wv", vt, "bv"))
                     for oc in range(NCH)]
            MM0, EV0 = mmc[0], evc[0]
            with nc.Block() as blk:
                @blk.tensor
                def _(te):
                    for idx, (w, dst, b, oc) in enumerate(sched):
                        pb = pgs[idx % 2]
                        if idx >= 2:
                            te.wait_ge(evS, EV0 + idx - 1)
                        for c in range(NCH):
                            te.matmul(pb[:], wsb[w].ap()[:, c, oc * 128:(oc + 1) * 128],
                                      xb[:, c, :], start=(c == 0), stop=False)
                        te.matmul(pb[:], bsb[b].ap()[:, oc * 128:(oc + 1) * 128],
                                  ONESB.ap()[0:1, :], start=False, stop=True
                                  ).then_inc(mmS, 1)
                @blk.scalar
                def _(s):
                    for idx, (w, dst, b, oc) in enumerate(sched):
                        pb = pgs[idx % 2]
                        s.wait_ge(mmS, MM0 + idx + 1)
                        s.activation(dst.ap()[:, oc, :], pb[:], AF.Copy
                                     ).then_inc(evS, 1)
            mmc[0] += len(sched)
            evc[0] += len(sched)
            with nc.Block() as blk:
                @blk.sync
                def _(sy):
                    sy.dma_start(out=qsw.ap()[0:64, :, :],
                                 in_=qt.ap()[64:128, :, :]).then_inc(ld, 16)
                    sy.dma_start(out=qsw.ap()[64:128, :, :],
                                 in_=qt.ap()[0:64, :, :]).then_inc(ld, 16)
                    sy.wait_ge(ld, 48 * (i + 1))

            for k in range(4):
                tc0 = k * 128
                MM0, EV0, VZ0, RR0 = mmc[0], evc[0], vzc[0], rrc[0]
                with nc.Block() as blk:
                    @blk.tensor
                    def _(te):
                        for jp in range(17):
                            if jp < 16:
                                half, cc = jp % 2, jp // 2
                                lhs = kt.ap()[half * 64:(half + 1) * 64, cc,
                                              tc0:tc0 + 128]
                                if jp > 0:
                                    te.wait_ge(evS, EV0 + jp)
                                for g in range(4):
                                    gh, gc = g // 2, g % 2
                                    buf = qt if gh == half else qsw
                                    rhs = buf.ap()[half * 64:(half + 1) * 64,
                                                   gc * 4:(gc + 1) * 4, tc0:tc0 + 128]
                                    ins = te.matmul(pgs[g][:], lhs, rhs,
                                                    start=True, stop=True)
                                ins.then_inc(mmS, 1)
                            if jp > 0:
                                jr = jp - 1
                                te.wait_ge(vzS, VZ0 + jr + 1)
                                for g in range(4):
                                    ins = te.matmul(pr[:, g * TT:(g + 1) * TT],
                                                    invz.ap(),
                                                    pbuf[:, g * TT:(g + 1) * TT],
                                                    start=(jr == 0), stop=(jr == 15),
                                                    skip_group_check=True)
                                ins.then_inc(rrS, 1)
                    @blk.scalar
                    def _(s):
                        for jp in range(16):
                            s.wait_ge(mmS, MM0 + jp + 1)
                            if jp >= 1:
                                s.wait_ge(rrS, RR0 + jp)
                            if jp >= 1:
                                s.wait_ge(vzS, VZ0 + jp)
                            for g in range(4):
                                ins = s.activation(pbuf[:, g * TT:(g + 1) * TT],
                                                   pgs[g][:], AF.Exp,
                                                   scale=1.0 / 1024.0,
                                                   accum_out=zpart[:, g:g + 1])
                            ins.then_inc(evS, 1)
                    @blk.vector
                    def _(v):
                        for jp in range(16):
                            v.wait_ge(evS, EV0 + jp + 1)
                            if jp > 0:
                                v.wait_ge(rrS, RR0 + jp)
                            v.reduce_sum(ztot[:], zpart[:], axis=AX.X)
                            v.drain()
                            v.reciprocal(invzf[:], ztot[:])
                            v.drain()
                            v.tensor_copy(invz[:], invzf[:]).then_inc(vzS, 1)
                mmc[0] += 16
                evc[0] += 16
                vzc[0] += 16
                rrc[0] += 16
                with nc.Block() as blk:
                    @blk.scalar
                    def _(s):
                        for g in range(4):
                            s.activation(r4row[:, g * TT:(g + 1) * TT],
                                         pr[:, g * TT:(g + 1) * TT], AF.Copy)
                with nc.Block() as blk:
                    @blk.tensor
                    def _(te):
                        for cc in range(NCH):
                            pb = (pg0, pg1)[cc // 4]
                            for half in range(2):
                                a0 = half * 1024 + cc * 128
                                te.matmul(
                                    pb[half * 64:(half + 1) * 64,
                                       (cc % 4) * 128:(cc % 4 + 1) * 128],
                                    ONESF.ap()[0:1, 0:64],
                                    r4row[:, a0:a0 + 128], start=True, stop=True)
                with nc.Block() as blk:
                    @blk.vector
                    def _(v):
                        v.tensor_tensor(outb[:, 0:4, tc0:tc0 + 128],
                                        vt[:, 0:4, tc0:tc0 + 128],
                                        pg0.ap().rearrange("p (c t) -> p c t", c=4),
                                        ALU.mult)
                        v.tensor_tensor(outb[:, 4:8, tc0:tc0 + 128],
                                        vt[:, 4:8, tc0:tc0 + 128],
                                        pg1.ap().rearrange("p (c t) -> p c t", c=4),
                                        ALU.mult)

            MM0, EV0 = mmc[0], evc[0]
            with nc.Block() as blk:
                @blk.tensor
                def _(te):
                    for oc in range(NCH):
                        pb = pgs[oc % 2]
                        if oc >= 2:
                            te.wait_ge(evS, EV0 + oc - 1)
                        for c in range(NCH):
                            te.matmul(pb[:], wsb["Wo"].ap()[:, c, oc * 128:(oc + 1) * 128],
                                      outb[:, c, :], start=(c == 0), stop=False)
                        te.matmul(pb[:], bsb["bo"].ap()[:, oc * 128:(oc + 1) * 128],
                                  ONESB.ap()[0:1, :], start=False, stop=True
                                  ).then_inc(mmS, 1)
                @blk.scalar
                def _(s):
                    for oc in range(NCH):
                        pb = pgs[oc % 2]
                        s.wait_ge(mmS, MM0 + oc + 1)
                        s.activation(yb[:, oc, :], pb[:], AF.Copy).then_inc(evS, 1)
            mmc[0] += NCH
            evc[0] += NCH

            emit_lnwrap(nc, csb, yb, xt, sqb, rrow, rowB, rowC,
                        pr[:, 0:TT], pr[:, TT:2 * TT], pg0, pg1, tmp1)
            with nc.Block() as blk:
                @blk.sync
                def _(sy):
                    sy.dma_start(out=x_dram_tile(xout, t0, TT), in_=xt[:]).then_inc(st, 16)
                    sy.wait_ge(st, 16 * (i + 1))
    _finish()





def emit_F0(nc, csb, P, io):
    stk, SB, PS, SEM, _finish = _stage_allocs(nc, P)
    xin = io["xin"]
    xout = io["xout"]
    wnames = ["WrT", "WkT", "WvT"]
    wd = {w: io[w] for w in wnames}
    mur = io["mur"]
    muk = io["muk"]
    ONESB, ONESF = csb["ONESB"], csb["ONESF"]

    wsb = {w: SB(f"w_{w}", [128, NCH, D], BF16) for w in wnames}
    murs = SB("murs", [128, NCH], F32)
    muks = SB("muks", [128, NCH], F32)
    wldsem = SEM("wld")
    with nc.Block() as blk:
        @blk.gpsimd
        def _(g):
            n = 0
            for w in wnames:
                g.dma_start(out=wsb[w].ap(),
                            in_=wd[w].rearrange("(c p) o -> p c o", p=128)
                            ).then_inc(wldsem, 16)
                n += 1
            for dst, src in ((murs, mur), (muks, muk)):
                g.dma_start(out=dst.ap(),
                            in_=src.rearrange("x (c p) -> (x p) c", p=128),
                            allow_slow_non_contiguous=True).then_inc(wldsem, 16)
                n += 1
            g.wait_ge(wldsem, 16 * n)

    xt513 = SB("xt513", [128, NCH, TT + 1], F32)
    sqb = SB("sqb", [128, NCH, TT], BF16)
    rrow = SB("rrow", [1, TT], F32)
    rowB = SB("rowB", [1, TT], F32)
    rowC = SB("rowC", [1, TT], F32)
    rghost = SB("rghost", [1, 1], F32)
    h1 = SB("h1", [128, NCH, TT + 1], BF16)
    tmix = SB("tmix", [128, NCH, TT], BF16)
    xrb = SB("xrb", [128, NCH, TT], BF16)
    xkb = SB("xkb", [128, NCH, TT], BF16)
    rs = SB("rs", [128, NCH, TT], BF16)
    kk = SB("kk", [128, NCH, TT], BF16)
    ka = SB("ka", [128, NCH, TT], BF16)
    yb = SB("yb", [128, NCH, TT], BF16)
    tmp1 = SB("tmp1", [128, NCH, TT], F32)
    pg0 = PS("pg0", [128, TT], F32)
    pg1 = PS("pg1", [128, TT], F32)
    pg2 = PS("pg2", [128, TT], F32)
    pr = PS("pr", [1, 2048], F32)
    ld = SEM("ld")
    st = SEM("st")
    mmS = SEM("mm")
    evS = SEM("ev")
    pgs = [pg0, pg1, pg2]
    mmc, evc = [0], [0]
    nld = [0]

    def proj_block(nc, src, wname, evac):
        MM0, EV0 = mmc[0], evc[0]
        with nc.Block() as blk:
            @blk.tensor
            def _(te):
                for oc in range(NCH):
                    pb = pgs[oc % 2]
                    if oc >= 2:
                        te.wait_ge(evS, EV0 + oc - 1)
                    for c in range(NCH):
                        ins = te.matmul(pb[:],
                                        wsb[wname].ap()[:, c, oc * 128:(oc + 1) * 128],
                                        src.ap()[:, c, :], start=(c == 0),
                                        stop=(c == NCH - 1))
                    ins.then_inc(mmS, 1)
            evac(blk, MM0)
        mmc[0] += NCH
        evc[0] += NCH

    for i in range(NTILE):
        t0 = i * TT
        bstart = (t0 % T) == 0
        with nc.Block() as blk:
            @blk.sync
            def _(sy):
                if bstart:
                    sy.dma_start(out=xt513[:, :, 1:TT + 1],
                                 in_=x_dram_tile(xin, t0, TT)).then_inc(ld, 16)
                else:
                    sy.dma_start(out=xt513[:],
                                 in_=xin[:, :, t0 - 1:t0 + TT]
                                 .rearrange("c p t -> p c t")).then_inc(ld, 16)
                nld[0] += 16
                sy.wait_ge(ld, nld[0])
            @blk.vector
            def _(v):
                if bstart:
                    v.memset(xt513.ap()[:, :, 0:1], 0.0)
        xt = xt513.ap()[:, :, 1:TT + 1]
        with nc.Block() as blk:
            @blk.scalar
            def _(s):
                s.activation(sqb.ap(), xt, AF.Square)
        with nc.Block() as blk:
            @blk.tensor
            def _(te):
                for c in range(NCH):
                    te.matmul(pr[:, 0:TT], ONESB.ap()[:, 0:1], sqb.ap()[:, c, :],
                              start=(c == 0), stop=(c == NCH - 1))
        with nc.Block() as blk:
            @blk.scalar
            def _(s):
                s.activation(rrow.ap(), pr[:, 0:TT], AF.Ln,
                             bias=csb["EPS5"].ap()[0:1, :], scale=1.0 / D)
                s.activation(rrow.ap(), rrow.ap(), AF.Exp, scale=-0.5)
        with nc.Block() as blk:
            @blk.tensor
            def _(te):
                te.matmul(pg0[:], ONESF.ap()[0:1, 0:128], rrow.ap(),
                          start=True, stop=True)
                te.matmul(pg1[0:128, 0:1], ONESF.ap()[0:1, 0:128],
                          rghost.ap(), start=True, stop=True)
        with nc.Block() as blk:
            @blk.vector
            def _(v):
                v.tensor_tensor(h1.ap()[:, :, 1:TT + 1], xt, bc_ap(pg0, NCH, TT),
                                ALU.mult)
        with nc.Block() as blk:
            @blk.vector
            def _(v):
                if bstart:
                    v.memset(h1.ap()[:, :, 0:1], 0.0)
                else:
                    v.tensor_tensor(h1.ap()[:, :, 0:1], xt513.ap()[:, :, 0:1],
                                    bass.AP(pg1, 0, [[TT, 128], [0, NCH], [1, 1]]),
                                    ALU.mult)
                v.tensor_copy(rghost.ap(), rrow.ap()[:, TT - 1:TT])
        with nc.Block() as blk:
            @blk.vector
            def _(v):
                v.tensor_sub(tmix.ap(), h1.ap()[:, :, 1:TT + 1],
                             h1.ap()[:, :, 0:TT])
        with nc.Block() as blk:
            @blk.vector
            def _(v):
                for c in range(NCH):
                    v.scalar_tensor_tensor(xrb.ap()[:, c, :], tmix.ap()[:, c, :],
                                           murs.ap()[:, c:c + 1],
                                           h1.ap()[:, c, 0:TT], ALU.mult, ALU.add)
                for c in range(NCH):
                    v.scalar_tensor_tensor(xkb.ap()[:, c, :], tmix.ap()[:, c, :],
                                           muks.ap()[:, c:c + 1],
                                           h1.ap()[:, c, 0:TT], ALU.mult, ALU.add)
        def evac_sig(blk, MM0):
            @blk.scalar
            def _(s):
                for oc in range(NCH):
                    pb = pgs[oc % 2]
                    s.wait_ge(mmS, MM0 + oc + 1)
                    s.activation(rs.ap()[:, oc, :], pb[:], AF.Sigmoid
                                 ).then_inc(evS, 1)
        proj_block(nc, xrb, "WrT", evac_sig)
        def evac_cp(blk, MM0):
            @blk.scalar
            def _(s):
                for oc in range(NCH):
                    pb = pgs[oc % 2]
                    s.wait_ge(mmS, MM0 + oc + 1)
                    s.activation(kk.ap()[:, oc, :], pb[:], AF.Copy
                                 ).then_inc(evS, 1)
        proj_block(nc, xkb, "WkT", evac_cp)
        with nc.Block() as blk:
            @blk.vector
            def _(v):
                v.scalar_tensor_tensor(ka.ap(), kk.ap(), 0.0, kk.ap(),
                                       ALU.max, ALU.mult)
        def evac_gate(blk, MM0):
            @blk.vector
            def _(v):
                for oc in range(NCH):
                    pb = pgs[oc % 2]
                    v.wait_ge(mmS, MM0 + oc + 1)
                    v.tensor_tensor(yb.ap()[:, oc, :], rs.ap()[:, oc, :], pb[:],
                                    ALU.mult).then_inc(evS, 1)
        proj_block(nc, ka, "WvT", evac_gate)
        emit_lnwrap(nc, csb, yb, xt513_x(xt513), sqb, rrow, rowB, rowC,
                    pr[:, 0:TT], pr[:, TT:2 * TT], pg0, pg1, tmp1)
        with nc.Block() as blk:
            @blk.sync
            def _(sy):
                sy.dma_start(out=x_dram_tile(xout, t0, TT),
                             in_=xt513.ap()[:, :, 1:TT + 1]).then_inc(st, 16)
                sy.wait_ge(st, 16 * (i + 1))
    _finish()


class _APView:
    def __init__(self, ap_):
        self._ap = ap_

    def ap(self):
        return self._ap


class _XtView:
    def __init__(self, h):
        self.h = h

    def ap(self):
        return self.h.ap()[:, :, 1:TT + 1]


def xt513_x(h):
    return _XtView(h)





def a1_kgroups(qi):
    nk = (qi + 1) * 128
    gs, off = [], 0
    while off < nk:
        w = min(512, nk - off)
        gs.append((off, w))
        off += w
    return gs


A1_DEBUG = False


def emit_A1(nc, csb, P, io):
    stk, SB, PS, SEM, _finish = _stage_allocs(nc, P)
    dbg = {}
    assert not A1_DEBUG, "debug dumps removed in emitter form"
    xin = io["xin"]
    xout = io["xout"]
    wqkv = io["wqkv"]
    bqkv = io["bqkv"]
    wout = io["wout"]
    bout = io["bout"]
    ONESB, ONESF = csb["ONESB"], csb["ONESF"]
    I16F, MASKTRI, E8 = csb["I16F"], csb["MASKTRI"], csb["E8SEL"]

    qh = SB("qh", [128, 4, 2048], BF16)
    kn = SB("kn", [128, 4, 2048], BF16)
    vtm = SB("vtm", [128, 16, 512], BF16)
    wbuf = SB("wbuf", [128, NCH, 1024], BF16)
    bqkv_sb = SB("bqkv_sb", [1, 3 * D], BF16)
    bout_sb = SB("bout_sb", [1, D], BF16)
    Ob = SB("Ob", [128, NCH, 2048], BF16)
    xt = SB("xt", [128, NCH, TT], F32)
    xb = SB("xb", [128, NCH, TT], BF16)
    sqs0 = SB("sqs0", [128, TT], BF16)
    sqs1 = SB("sqs1", [128, TT], BF16)
    qsqs = SB("qsqs", [8, 2048], F32)
    ksqs = SB("ksqs", [8, 2048], F32)
    sS = SB("sS", [128, 2048], F32)
    Pb = SB("Pb", [128, 2048], BF16)
    PbN = SB("PbN", [128, 2048], BF16)
    PT = SB("PT", [128, 16, 128], BF16)
    rec0 = SB("rec0", [128, TT], F32)
    rec1 = SB("rec1", [128, TT], F32)
    sqf0 = SB("sqf0", [128, TT], F32)
    sqf1 = SB("sqf1", [128, TT], F32)
    negm = SB("negm", [128, 1], F32)
    junk1 = SB("junk1", [128, 1], F32)
    zrow = SB("zrow", [128, 1], F32)
    invzq = SB("invzq", [128, 1], F32)
    rrowB = SB("rrowB", [1, 2048], F32)
    rowA = SB("rowA", [1, TT], F32)
    rowB = SB("rowB", [1, TT], F32)
    rowC = SB("rowC", [1, TT], F32)
    yb = SB("yb", [128, NCH, TT], BF16)
    tmpb = _APView(qh.ap()[:, 0:2, :]
                   .rearrange("p c (d t) -> p (c d) t", d=4, t=TT))

    pg0 = PS("pg0", [128, TT], F32)
    pg1 = PS("pg1", [128, TT], F32)
    pg2 = PS("pg2", [128, TT], F32)
    pg3 = PS("pg3", [128, TT], F32)
    pav = PS("pav", [64, 128], F32)
    pq8 = PS("pq8", [8, TT], F32)
    pl1 = PS("pl1", [1, TT], F32)
    pl2 = PS("pl2", [1, TT], F32)
    pgs = [pg0, pg1, pg2]

    sems = {}
    for nm in ["ld", "st", "mm", "ev", "tt", "mx", "ex", "pn", "tr", "av",
               "oe", "wld"]:
        sems[nm] = SEM(nm)
    C = {k: 0 for k in sems}
    ld, st, mmS, evS, ttS = (sems[k] for k in ["ld", "st", "mm", "ev", "tt"])
    mxS, exS, pnS, trS, avS, oeS, wldsem = (
        sems[k] for k in ["mx", "ex", "pn", "tr", "av", "oe", "wld"])

    def wload(g, dst_ap, src_ap, slow=False):
        g.dma_start(out=dst_ap, in_=src_ap,
                    allow_slow_non_contiguous=slow).then_inc(wldsem, 16)
        C["wld"] += 16

    def ind8(hl, n):
        return bass.AP(I16F, hl, [[16, 8], [0, n]])

    with nc.Block() as blk:
        @blk.gpsimd
        def _(g):
            wload(g, bqkv_sb.ap(), bqkv)
            wload(g, bout_sb.ap(), bout)
            g.wait_ge(wldsem, C["wld"])

    for b in range(B):
        tb0 = b * T
        for hg in range(2):
            with nc.Block() as blk:
                @blk.gpsimd
                def _(g):
                    wload(g, wbuf.ap()[:, :, 0:512],
                          wqkv[:, hg * 512:(hg + 1) * 512]
                          .rearrange("(c p) o -> p c o", p=128))
                    wload(g, wbuf.ap()[:, :, 512:1024],
                          wqkv[:, D + hg * 512:D + (hg + 1) * 512]
                          .rearrange("(c p) o -> p c o", p=128))
                    g.wait_ge(wldsem, C["wld"])
            for it in range(4):
                t0 = tb0 + it * TT
                rsl = rrowB.ap()[:, it * TT:(it + 1) * TT]
                with nc.Block() as blk:
                    @blk.sync
                    def _(sy):
                        sy.dma_start(out=xt.ap(), in_=x_dram_tile(xin, t0, TT)
                                     ).then_inc(ld, 16)
                        C["ld"] += 16
                        sy.wait_ge(ld, C["ld"])
                if hg == 0:
                    with nc.Block() as blk:
                        @blk.scalar
                        def _(s):
                            s.activation(xb.ap(), xt.ap(), AF.Square)
                    with nc.Block() as blk:
                        @blk.tensor
                        def _(te):
                            for c in range(NCH):
                                te.matmul(pl1[:], ONESB.ap()[:, 0:1],
                                          xb.ap()[:, c, :],
                                          start=(c == 0), stop=(c == NCH - 1))
                    with nc.Block() as blk:
                        @blk.scalar
                        def _(s):
                            s.activation(rsl, pl1[:], AF.Ln,
                                         bias=csb["EPS5"].ap()[0:1, :],
                                         scale=1.0 / D)
                            s.activation(rsl, rsl, AF.Exp, scale=-0.5)
                with nc.Block() as blk:
                    @blk.tensor
                    def _(te):
                        te.matmul(pg3[:], ONESF.ap()[0:1, 0:128], rsl,
                                  start=True, stop=True)
                with nc.Block() as blk:
                    @blk.vector
                    def _(v):
                        v.tensor_tensor(xb.ap(), xt.ap(), bc_ap(pg3, NCH, TT),
                                        ALU.mult)
                MM0, EV0 = C["mm"], C["ev"]
                qkocs = [("q", oc) for oc in range(4)] + \
                        [("k", oc) for oc in range(4)]
                with nc.Block() as blk:
                    @blk.tensor
                    def _(te):
                        for idx, (which, oc) in enumerate(qkocs):
                            pb = pgs[idx % 2]
                            if idx >= 2:
                                te.wait_ge(evS, EV0 + idx - 1)
                            wcol = (0 if which == "q" else 512) + oc * 128
                            gcol = ((hg * 512 if which == "q"
                                     else D + hg * 512) + oc * 128)
                            for c in range(NCH):
                                te.matmul(pb[:], wbuf.ap()[:, c, wcol:wcol + 128],
                                          xb.ap()[:, c, :], start=(c == 0),
                                          stop=False)
                            te.matmul(pb[:], bqkv_sb.ap()[:, gcol:gcol + 128],
                                      ONESB.ap()[0:1, :], start=False, stop=True
                                      ).then_inc(mmS, 1)
                    @blk.scalar
                    def _(s):
                        for idx, (which, oc) in enumerate(qkocs):
                            pb = pgs[idx % 2]
                            s.wait_ge(mmS, MM0 + idx + 1)
                            dst = qh if which == "q" else kn
                            scl = 1.0 if which == "q" else -1.0
                            s.activation(dst.ap()[:, oc, it * TT:(it + 1) * TT],
                                         pb[:], AF.Copy, scale=scl
                                         ).then_inc(evS, 1)
                C["mm"] += len(qkocs)
                C["ev"] += len(qkocs)
            with nc.Block() as blk:
                @blk.gpsimd
                def _(g):
                    wload(g, wbuf.ap()[:, :, 0:512],
                          wqkv[:, 2 * D + hg * 512:2 * D + (hg + 1) * 512]
                          .rearrange("(c p) o -> p c o", p=128))
                    g.wait_ge(wldsem, C["wld"])
            for it in range(4):
                t0 = tb0 + it * TT
                with nc.Block() as blk:
                    @blk.sync
                    def _(sy):
                        sy.dma_start(out=xt.ap(), in_=x_dram_tile(xin, t0, TT)
                                     ).then_inc(ld, 16)
                        C["ld"] += 16
                        sy.wait_ge(ld, C["ld"])
                with nc.Block() as blk:
                    @blk.tensor
                    def _(te):
                        te.matmul(pg3[:], ONESF.ap()[0:1, 0:128],
                                  rrowB.ap()[:, it * TT:(it + 1) * TT],
                                  start=True, stop=True)
                with nc.Block() as blk:
                    @blk.vector
                    def _(v):
                        v.tensor_tensor(xb.ap(), xt.ap(), bc_ap(pg3, NCH, TT),
                                        ALU.mult)
                MM0, EV0 = C["mm"], C["ev"]
                with nc.Block() as blk:
                    @blk.tensor
                    def _(te):
                        for tb in range(4):
                            pb = pgs[tb % 2]
                            if tb >= 2:
                                te.wait_ge(evS, EV0 + tb - 1)
                            for c in range(NCH):
                                te.matmul(pb[:],
                                          xb.ap()[:, c, tb * 128:(tb + 1) * 128],
                                          wbuf.ap()[:, c, 0:512],
                                          start=(c == 0), stop=False)
                            te.matmul(pb[:], ONESB.ap()[0:1, 0:128],
                                      bqkv_sb.ap()[:, 2 * D + hg * 512:
                                                   2 * D + (hg + 1) * 512],
                                      start=False, stop=True).then_inc(mmS, 1)
                    @blk.scalar
                    def _(s):
                        for tb in range(4):
                            pb = pgs[tb % 2]
                            s.wait_ge(mmS, MM0 + tb + 1)
                            s.activation(vtm.ap()[:, it * 4 + tb, :], pb[:],
                                         AF.Copy).then_inc(evS, 1)
                C["mm"] += 4
                C["ev"] += 4
            for src, dstrows, epsb in ((qh, qsqs, 0.0), (kn, ksqs, 1e-6)):
                for it in range(4):
                    MM0, EV0 = C["mm"], C["ev"]
                    with nc.Block() as blk:
                        @blk.scalar
                        def _(s):
                            for oc in range(4):
                                sqd = (sqs0, sqs1)[oc % 2]
                                if oc >= 2:
                                    s.wait_ge(mmS, MM0 + oc - 1)
                                s.activation(sqd.ap(), src.ap()[:, oc,
                                             it * TT:(it + 1) * TT],
                                             AF.Square).then_inc(evS, 1)
                        @blk.tensor
                        def _(te):
                            for oc in range(4):
                                sqd = (sqs0, sqs1)[oc % 2]
                                te.wait_ge(evS, EV0 + oc + 1)
                                te.matmul(pq8[:], E8.ap()[:, oc, :], sqd.ap(),
                                          start=(oc == 0), stop=(oc == 3)
                                          ).then_inc(mmS, 1)
                    C["mm"] += 4
                    C["ev"] += 4
                    with nc.Block() as blk:
                        @blk.scalar
                        def _(s):
                            s.activation(dstrows.ap()[:, it * TT:(it + 1) * TT],
                                         pq8[:], AF.Copy, bias=epsb)
            if A1_DEBUG and b == 0 and hg == 0:
                with nc.Block() as blk:
                    @blk.sync
                    def _(sy):
                        for ap_, h_ in ((dbg["d_qh"], qh), (dbg["d_kn"], kn),
                                        (dbg["d_vtm"], vtm), (dbg["d_qsqs"], qsqs),
                                        (dbg["d_ksqs"], ksqs)):
                            sy.dma_start(out=ap_, in_=h_.ap()).then_inc(st, 16)
                            C["st"] += 16
                        sy.wait_ge(st, C["st"])
            for hl in range(8):
                c_l, half = hl // 2, hl % 2
                gch = (hg * 8 + hl) // 2
                MM0, EV0, TT0 = C["mm"], C["ev"], C["tt"]
                MX0, EX0, PN0, TR0 = C["mx"], C["ex"], C["pn"], C["tr"]
                AV0, OE0 = C["av"], C["oe"]
                groups = [(qi, kk0, kw) for qi in range(16)
                          for (kk0, kw) in a1_kgroups(qi)]
                with nc.Block() as blk:
                    @blk.tensor
                    def _(te):
                        gidx = 0
                        for qi in range(16):
                            nk = (qi + 1) * 128
                            lhs = qh.ap()[half * 64:(half + 1) * 64, c_l,
                                          qi * 128:(qi + 1) * 128]
                            for (kk0, kw) in a1_kgroups(qi):
                                bank = pgs[gidx % 3]
                                if gidx >= 3:
                                    te.wait_ge(ttS, TT0 + gidx - 2)
                                rhs = kn.ap()[half * 64:(half + 1) * 64, c_l,
                                              kk0:kk0 + kw]
                                te.matmul(bank[:, 0:kw], lhs, rhs,
                                          start=True, stop=False
                                          ).then_inc(mmS, 1)
                                te.wait_ge(evS, EV0 + gidx + 1)
                                te.matmul(bank[:, 0:kw], lhs, rhs,
                                          start=False, stop=False,
                                          skip_group_check=True)
                                te.matmul(bank[:, 0:kw], ind8(hl, 128),
                                          ksqs.ap()[:, kk0:kk0 + kw],
                                          start=False, stop=False,
                                          skip_group_check=True)
                                te.matmul(bank[:, 0:kw],
                                          qsqs.ap()[:, qi * 128:(qi + 1) * 128],
                                          ind8(hl, kw),
                                          start=False, stop=True,
                                          skip_group_check=True).then_inc(mmS, 1)
                                gidx += 1
                            te.wait_ge(trS, TR0 + 16 * (qi + 1))
                            te.wait_ge(oeS, OE0 + qi)
                            for kc in range(qi + 1):
                                ins = te.matmul(pav[:],
                                                vtm.ap()[:, kc,
                                                         hl * 64:(hl + 1) * 64],
                                                PT.ap()[:, kc, :],
                                                start=(kc == 0), stop=(kc == qi))
                            ins.then_inc(avS, 1)
                    @blk.scalar
                    def _(s):
                        gidx = 0
                        for qi in range(16):
                            nk = (qi + 1) * 128
                            for (kk0, kw) in a1_kgroups(qi):
                                bank = pgs[gidx % 3]
                                sqd = (sqf0, sqf1)[gidx % 2]
                                s.wait_ge(mmS, MM0 + 2 * gidx + 1)
                                if gidx >= 2:
                                    s.wait_ge(ttS, TT0 + gidx - 1)
                                s.activation(sqd.ap()[:, 0:kw], bank[:, 0:kw],
                                             AF.Square).then_inc(evS, 1)
                                gidx += 1
                            s.wait_ge(mxS, MX0 + qi + 1)
                            if qi >= 1:
                                s.wait_ge(pnS, PN0 + qi)
                            s.activation(Pb.ap()[:, 0:nk], sS.ap()[:, 0:nk],
                                         AF.Exp, bias=negm.ap())
                            s.activation(junk1.ap(), negm.ap(), AF.Copy
                                         ).then_inc(exS, 1)
                            s.wait_ge(avS, AV0 + qi + 1)
                            s.activation(Ob.ap()[half * 64:(half + 1) * 64, gch,
                                                 qi * 128:(qi + 1) * 128],
                                         pav[:], AF.Copy).then_inc(oeS, 1)
                    @blk.vector
                    def _(v):
                        gidx = 0
                        for qi in range(16):
                            nk = (qi + 1) * 128
                            for (kk0, kw) in a1_kgroups(qi):
                                bank = pgs[gidx % 3]
                                sqd = (sqf0, sqf1)[gidx % 2]
                                rcd = (rec0, rec1)[gidx % 2]
                                v.wait_ge(mmS, MM0 + 2 * gidx + 2)
                                v.reciprocal(rcd.ap()[:, 0:kw], bank[:, 0:kw])
                                v.tensor_tensor(sS.ap()[:, kk0:kk0 + kw],
                                                sqd.ap()[:, 0:kw],
                                                rcd.ap()[:, 0:kw],
                                                ALU.mult).then_inc(ttS, 1)
                                gidx += 1
                            v.tensor_tensor(sS.ap()[:, qi * 128:nk],
                                            sS.ap()[:, qi * 128:nk],
                                            MASKTRI.ap(), ALU.add)
                            v.reduce_max(negm.ap(), sS.ap()[:, 0:nk], axis=AX.X,
                                         negate=True)
                            v.tensor_copy(junk1.ap(), negm.ap()).then_inc(mxS, 1)
                            v.wait_ge(exS, EX0 + qi + 1)
                            v.reduce_sum(zrow.ap(), Pb.ap()[:, 0:nk], axis=AX.X)
                            v.drain()
                            v.reciprocal(invzq.ap(), zrow.ap())
                            v.drain()
                            v.tensor_scalar(PbN.ap()[:, 0:nk], Pb.ap()[:, 0:nk],
                                            invzq.ap(), None, ALU.mult)
                            v.tensor_copy(junk1.ap(), invzq.ap()).then_inc(pnS, 1)
                    @blk.sync
                    def _(sy):
                        for qi in range(16):
                            nk = (qi + 1) * 128
                            sy.wait_ge(pnS, PN0 + qi + 1)
                            sy.dma_start_transpose(PT.ap()[:, 0:qi + 1, :],
                                                   PbN.ap()[:, 0:nk]
                                                   ).then_inc(trS, 16)
                if A1_DEBUG and b == 0 and hg == 0 and hl == 0:
                    with nc.Block() as blk:
                        @blk.sync
                        def _(sy):
                            sy.dma_start(out=dbg["d_s"], in_=sS.ap()).then_inc(st, 16)
                            sy.dma_start(out=dbg["d_P"], in_=Pb.ap()).then_inc(st, 16)
                            C["st"] += 32
                            sy.wait_ge(st, C["st"])
                ngroups = len(groups)
                C["mm"] += 2 * ngroups
                C["ev"] += ngroups
                C["tt"] += ngroups
                C["mx"] += 16
                C["ex"] += 16
                C["pn"] += 16
                C["tr"] += 16 * 16
                C["av"] += 16
                C["oe"] += 16
        if A1_DEBUG and b == 0:
            with nc.Block() as blk:
                @blk.sync
                def _(sy):
                    sy.dma_start(out=dbg["d_Ob"], in_=Ob.ap()).then_inc(st, 16)
                    C["st"] += 16
                    sy.wait_ge(st, C["st"])
        with nc.Block() as blk:
            @blk.gpsimd
            def _(g):
                wload(g, wbuf.ap()[:, :, 0:1024],
                      wout.rearrange("(c p) o -> p c o", p=128))
                g.wait_ge(wldsem, C["wld"])
        for it in range(4):
            t0 = tb0 + it * TT
            with nc.Block() as blk:
                @blk.sync
                def _(sy):
                    sy.dma_start(out=xt.ap(), in_=x_dram_tile(xin, t0, TT)
                                 ).then_inc(ld, 16)
                    C["ld"] += 16
                    sy.wait_ge(ld, C["ld"])
            MM0, EV0 = C["mm"], C["ev"]
            with nc.Block() as blk:
                @blk.tensor
                def _(te):
                    for oc in range(NCH):
                        pb = pgs[oc % 2]
                        if oc >= 2:
                            te.wait_ge(evS, EV0 + oc - 1)
                        for c in range(NCH):
                            te.matmul(pb[:], wbuf.ap()[:, c, oc * 128:(oc + 1) * 128],
                                      Ob.ap()[:, c, it * TT:(it + 1) * TT],
                                      start=(c == 0), stop=False)
                        te.matmul(pb[:], bout_sb.ap()[:, oc * 128:(oc + 1) * 128],
                                  ONESB.ap()[0:1, :], start=False, stop=True
                                  ).then_inc(mmS, 1)
                @blk.scalar
                def _(s):
                    for oc in range(NCH):
                        pb = pgs[oc % 2]
                        s.wait_ge(mmS, MM0 + oc + 1)
                        s.activation(yb.ap()[:, oc, :], pb[:], AF.Copy
                                     ).then_inc(evS, 1)
            C["mm"] += NCH
            C["ev"] += NCH
            emit_lnwrap(nc, csb, yb, xt, xb, rowA, rowB, rowC,
                        pl1[:], pl2[:], pg0, pg1, tmpb)
            with nc.Block() as blk:
                @blk.sync
                def _(sy):
                    sy.dma_start(out=x_dram_tile(xout, t0, TT), in_=xt.ap()
                                 ).then_inc(st, 16)
                    C["st"] += 16
                    sy.wait_ge(st, C["st"])
    _finish()







def emit_hprep(nc, csb, P, io):
    stk, SB, PS, SEM, _finish = _stage_allocs(nc, P)
    xin = io["xin"]
    xiout = io["xiout"]
    ONESB, ONESF = csb["ONESB"], csb["ONESF"]
    xt = SB("xt", [128, NCH, TT], F32)
    sqb = SB("sqb", [128, NCH, TT], BF16)
    rrow = SB("rrow", [1, TT], F32)
    xo = SB("xo", [128, NCH, TT], F32)
    pl1 = PS("pl1", [1, TT], F32)
    pbc = PS("pbc", [128, TT], F32)
    ld = SEM("ld")
    st = SEM("st")
    for i in range(NTILE):
        t0 = i * TT
        with nc.Block() as blk:
            @blk.sync
            def _(sy):
                sy.dma_start(out=xt.ap(), in_=x_dram_tile(xin, t0, TT)).then_inc(ld, 16)
                sy.wait_ge(ld, 16 * (i + 1))
        with nc.Block() as blk:
            @blk.scalar
            def _(s):
                s.activation(sqb.ap(), xt.ap(), AF.Square)
        with nc.Block() as blk:
            @blk.tensor
            def _(te):
                for c in range(NCH):
                    te.matmul(pl1[:], ONESB.ap()[:, 0:1], sqb.ap()[:, c, :],
                              start=(c == 0), stop=(c == NCH - 1))
        with nc.Block() as blk:
            @blk.scalar
            def _(s):
                s.activation(rrow.ap(), pl1[:], AF.Ln,
                             bias=csb["EPS5"].ap()[0:1, :], scale=1.0 / D)
                s.activation(rrow.ap(), rrow.ap(), AF.Exp, scale=-0.5)
        with nc.Block() as blk:
            @blk.tensor
            def _(te):
                te.matmul(pbc[:], ONESF.ap()[0:1, 0:128], rrow.ap(),
                          start=True, stop=True)
        with nc.Block() as blk:
            @blk.vector
            def _(v):
                v.tensor_tensor(xo.ap(), xt.ap(), bc_ap(pbc, NCH, TT), ALU.mult)
        with nc.Block() as blk:
            @blk.sync
            def _(sy):
                sy.dma_start(out=x_dram_tile(xiout, t0, TT), in_=xo.ap()
                             ).then_inc(st, 16)
                sy.wait_ge(st, 16 * (i + 1))
    _finish()



HID = 2730
NHP = 22
SUP = 1024
NSUP = TOK // SUP


def hp_width(hp):
    return 42 if hp == NHP - 1 else 128


def emit_iffn(nc, csb, P, io):
    stk, SB, PS, SEM, _finish = _stage_allocs(nc, P)
    xin = io["xin"]
    xout = io["xout"]
    w1a = io["w1a"]
    w1g = io["w1g"]
    b1a = io["b1a"]
    b1g = io["b1g"]
    w2p = io["w2p"]
    b2 = io["b2"]
    lng = io["lng"]
    lnb = io["lnb"]
    ONESB, ONESF = csb["ONESB"], csb["ONESF"]

    xt = SB("xt", [128, NCH, SUP + 1], F32)
    sqb = SB("sqb", [128, NCH, SUP], BF16)
    tln = SB("tln", [128, NCH, SUP], F32)
    xb = SB("xb", [128, NCH, SUP], BF16)
    th = SB("th", [128, NHP, SUP], BF16)
    gsb = SB("gsb", [128, TT], BF16)
    w1ab = SB("w1ab", [128, NCH, 128], BF16)
    w1gb = SB("w1gb", [128, NCH, 128], BF16)
    w2b = SB("w2b", [128, NHP, 128], BF16)
    b1as = SB("b1as", [128, NHP], F32)
    b1gs = SB("b1gs", [128, NHP], F32)
    b2s = SB("b2s", [128, NCH], F32)
    lngs = SB("lngs", [128, NCH], F32)
    lnbs = SB("lnbs", [128, NCH], F32)
    rowM = SB("rowM", [1, SUP], F32)
    rowR = SB("rowR", [1, SUP], F32)
    rowC = SB("rowC", [1, SUP], F32)

    pg0 = PS("pg0", [128, TT], F32)
    pg1 = PS("pg1", [128, TT], F32)
    pg2 = PS("pg2", [128, TT], F32)
    pg3 = PS("pg3", [128, TT], F32)
    pbc0 = PS("pbc0", [128, TT], F32)
    pbc1 = PS("pbc1", [128, TT], F32)
    pl1 = PS("pl1", [1, TT], F32)
    pl2 = PS("pl2", [1, TT], F32)

    sems = {}
    for nm in ["ld", "st", "mm", "ev", "wld"]:
        sems[nm] = SEM(nm)
    C = {k: 0 for k in sems}
    ld, st, mmS, evS, wldsem = (sems[k] for k in ["ld", "st", "mm", "ev", "wld"])

    def wload(g, dst_ap, src_ap, slow=False):
        g.dma_start(out=dst_ap, in_=src_ap,
                    allow_slow_non_contiguous=slow).then_inc(wldsem, 16)
        C["wld"] += 16

    with nc.Block() as blk:
        @blk.gpsimd
        def _(g):
            wload(g, b1as.ap(), b1a)
            wload(g, b1gs.ap(), b1g)
            wload(g, b2s.ap(), b2)
            wload(g, lngs.ap(), lng)
            wload(g, lnbs.ap(), lnb)
            g.wait_ge(wldsem, C["wld"])

    for sup in range(NSUP):
        t0 = sup * SUP
        bstart = (t0 % T) == 0
        with nc.Block() as blk:
            @blk.sync
            def _(sy):
                if bstart:
                    sy.dma_start(out=xt.ap()[:, :, 1:SUP + 1],
                                 in_=x_dram_tile(xin, t0, SUP)).then_inc(ld, 16)
                else:
                    sy.dma_start(out=xt.ap(),
                                 in_=xin[:, :, t0 - 1:t0 + SUP]
                                 .rearrange("c p t -> p c t")).then_inc(ld, 16)
                C["ld"] += 16
                sy.wait_ge(ld, C["ld"])
            @blk.vector
            def _(v):
                if bstart:
                    v.memset(xt.ap()[:, :, 0:1], 0.0)
        xsh_lo = xt.ap()[:, 0:4, 1:SUP + 1]
        xsh_hi = xt.ap()[:, 4:8, 0:SUP]
        with nc.Block() as blk:
            @blk.scalar
            def _(s):
                s.activation(sqb.ap()[:, 0:4, :], xsh_lo, AF.Square)
                s.activation(sqb.ap()[:, 4:8, :], xsh_hi, AF.Square)
        for half in range(2):
            hsl = slice(half * TT, (half + 1) * TT)
            with nc.Block() as blk:
                @blk.tensor
                def _(te):
                    for c in range(NCH):
                        off = 1 if c < 4 else 0
                        te.matmul(pl1[:], ONESF.ap()[:, 0:1],
                                  xt.ap()[:, c, off + half * TT:off + (half + 1) * TT],
                                  start=(c == 0), stop=(c == NCH - 1))
                    for c in range(NCH):
                        te.matmul(pl2[:], ONESB.ap()[:, 0:1],
                                  sqb.ap()[:, c, hsl],
                                  start=(c == 0), stop=(c == NCH - 1))
            with nc.Block() as blk:
                @blk.scalar
                def _(s):
                    s.activation(rowM.ap()[:, hsl], pl1[:], AF.Copy, scale=1.0 / D)
                    s.activation(rowR.ap()[:, hsl], pl1[:], AF.Square,
                                 scale=1.0 / D)
                    s.activation(rowC.ap()[:, hsl], pl2[:], AF.Copy, scale=1.0 / D)
            with nc.Block() as blk:
                @blk.vector
                def _(v):
                    v.tensor_sub(rowC.ap()[:, hsl], rowC.ap()[:, hsl],
                                 rowR.ap()[:, hsl])
            with nc.Block() as blk:
                @blk.scalar
                def _(s):
                    s.activation(rowR.ap()[:, hsl], rowC.ap()[:, hsl], AF.Ln,
                                 bias=csb["EPS5"].ap()[0:1, :])
                    s.activation(rowR.ap()[:, hsl], rowR.ap()[:, hsl],
                                 AF.Exp, scale=-0.5)
            with nc.Block() as blk:
                @blk.vector
                def _(v):
                    v.tensor_mul(rowC.ap()[:, hsl], rowM.ap()[:, hsl],
                                 rowR.ap()[:, hsl])
                    v.drain()
                    v.tensor_scalar_mul(rowC.ap()[:, hsl], rowC.ap()[:, hsl],
                                        -1.0)
            with nc.Block() as blk:
                @blk.tensor
                def _(te):
                    te.matmul(pbc0[:], ONESF.ap()[0:1, 0:128], rowR.ap()[:, hsl],
                              start=True, stop=True)
                    te.matmul(pbc1[:], ONESF.ap()[0:1, 0:128], rowC.ap()[:, hsl],
                              start=True, stop=True)
            with nc.Block() as blk:
                @blk.vector
                def _(v):
                    v.tensor_tensor(tln.ap()[:, 0:4, hsl],
                                    xt.ap()[:, 0:4, 1 + half * TT:1 + (half + 1) * TT],
                                    bc_ap(pbc0, 4, TT), ALU.mult)
                    v.tensor_tensor(tln.ap()[:, 4:8, hsl],
                                    xt.ap()[:, 4:8, half * TT:(half + 1) * TT],
                                    bc_ap(pbc0, 4, TT), ALU.mult)
            with nc.Block() as blk:
                @blk.vector
                def _(v):
                    v.tensor_tensor(tln.ap()[:, :, hsl], tln.ap()[:, :, hsl],
                                    bc_ap(pbc1, NCH, TT), ALU.add)
        with nc.Block() as blk:
            @blk.vector
            def _(v):
                for c in range(NCH):
                    v.tensor_scalar(xb.ap()[:, c, :], tln.ap()[:, c, :],
                                    lngs.ap()[:, c:c + 1], lnbs.ap()[:, c:c + 1],
                                    ALU.mult, ALU.add)
        with nc.Block() as blk:
            @blk.vector
            def _(v):
                v.memset(th.ap()[:, NHP - 1, :], 0.0)
        for hp in range(NHP):
            wdt = hp_width(hp)
            with nc.Block() as blk:
                @blk.gpsimd
                def _(g):
                    wload(g, w1ab.ap()[:, :, 0:wdt],
                          w1a[:, hp * 128:hp * 128 + wdt]
                          .rearrange("(c p) o -> p c o", p=128))
                    wload(g, w1gb.ap()[:, :, 0:wdt],
                          w1g[:, hp * 128:hp * 128 + wdt]
                          .rearrange("(c p) o -> p c o", p=128))
                    g.wait_ge(wldsem, C["wld"])
            MM0, EV0 = C["mm"], C["ev"]
            with nc.Block() as blk:
                @blk.tensor
                def _(te):
                    for tt2 in range(2):
                        hsl = slice(tt2 * TT, (tt2 + 1) * TT)
                        if tt2 >= 1:
                            te.wait_ge(evS, EV0 + 2 * tt2 - 1)
                        for c in range(NCH):
                            te.matmul(pg0[0:wdt, :] if tt2 == 0 else pg2[0:wdt, :],
                                      w1ab.ap()[:, c, 0:wdt], xb.ap()[:, c, hsl],
                                      start=(c == 0), stop=(c == NCH - 1)
                                      ).then_inc(mmS, 1) if c == NCH - 1 else \
                            te.matmul(pg0[0:wdt, :] if tt2 == 0 else pg2[0:wdt, :],
                                      w1ab.ap()[:, c, 0:wdt], xb.ap()[:, c, hsl],
                                      start=(c == 0), stop=False)
                        for c in range(NCH):
                            te.matmul(pg1[0:wdt, :] if tt2 == 0 else pg3[0:wdt, :],
                                      w1gb.ap()[:, c, 0:wdt], xb.ap()[:, c, hsl],
                                      start=(c == 0), stop=(c == NCH - 1)
                                      ).then_inc(mmS, 1) if c == NCH - 1 else \
                            te.matmul(pg1[0:wdt, :] if tt2 == 0 else pg3[0:wdt, :],
                                      w1gb.ap()[:, c, 0:wdt], xb.ap()[:, c, hsl],
                                      start=(c == 0), stop=False)
                @blk.scalar
                def _(s):
                    for tt2 in range(2):
                        pB = pg1 if tt2 == 0 else pg3
                        s.wait_ge(mmS, MM0 + 2 * tt2 + 2)
                        if tt2 == 1:
                            s.wait_ge(evS, EV0 + 2)
                        s.activation(gsb.ap()[0:wdt, :], pB[0:wdt, :], AF.Gelu,
                                     bias=b1gs.ap()[0:wdt, hp:hp + 1]
                                     ).then_inc(evS, 1)
                @blk.vector
                def _(v):
                    for tt2 in range(2):
                        pA = pg0 if tt2 == 0 else pg2
                        hsl = slice(tt2 * TT, (tt2 + 1) * TT)
                        v.wait_ge(evS, EV0 + 2 * tt2 + 1)
                        v.scalar_tensor_tensor(th.ap()[0:wdt, hp, hsl],
                                               pA[0:wdt, :],
                                               b1as.ap()[0:wdt, hp:hp + 1],
                                               gsb.ap()[0:wdt, :],
                                               ALU.add, ALU.mult).then_inc(evS, 1)
            C["mm"] += 4
            C["ev"] += 4
        for oc in range(NCH):
            with nc.Block() as blk:
                @blk.gpsimd
                def _(g):
                    wload(g, w2b.ap(),
                          w2p[:, oc * 128:(oc + 1) * 128]
                          .rearrange("(c p) o -> p c o", p=128))
                    g.wait_ge(wldsem, C["wld"])
            MM0 = C["mm"]
            with nc.Block() as blk:
                @blk.tensor
                def _(te):
                    for tt2 in range(2):
                        hsl = slice(tt2 * TT, (tt2 + 1) * TT)
                        pb = (pg0, pg1)[tt2]
                        for hp in range(NHP):
                            ins = te.matmul(pb[:], w2b.ap()[:, hp, :],
                                            th.ap()[:, hp, hsl],
                                            start=(hp == 0), stop=(hp == NHP - 1))
                        ins.then_inc(mmS, 1)
                @blk.vector
                def _(v):
                    for tt2 in range(2):
                        hsl1 = slice(1 + tt2 * TT, 1 + (tt2 + 1) * TT)
                        pb = (pg0, pg1)[tt2]
                        v.wait_ge(mmS, MM0 + tt2 + 1)
                        v.scalar_tensor_tensor(xt.ap()[:, oc, hsl1], pb[:],
                                               b2s.ap()[:, oc:oc + 1],
                                               xt.ap()[:, oc, hsl1],
                                               ALU.add, ALU.add)
            C["mm"] += 2
        with nc.Block() as blk:
            @blk.sync
            def _(sy):
                sy.dma_start(out=x_dram_tile(xout, t0, SUP),
                             in_=xt.ap()[:, :, 1:SUP + 1]).then_inc(st, 16)
                C["st"] += 16
                sy.wait_ge(st, C["st"])
    _finish()





def iattn_windows():
    out = []
    for w in range(8):
        for sub in range(2):
            qt0 = w * 256 + sub * 128
            if w == 0:
                kw0, kwid = 0, 128 if sub == 0 else 256
                mn = "MSK_W0S0" if sub == 0 else "MSK_W0S1"
            else:
                kw0, kwid = (w - 1) * 256 + sub * 128, 384
                mn = "MSK_SUB0" if sub == 0 else "MSK_SUB1"
            out.append((qt0, kw0, kwid, mn))
    return out


IATTN_PH = 99
IATTN_NOE8 = False
IATTN_DEBUG = False


def emit_iattn(nc, csb, P, io):
    stk, SB, PS, SEM, _finish = _stage_allocs(nc, P)
    PH = IATTN_PH
    dbg = {}
    assert not IATTN_DEBUG, "debug dumps removed in emitter form"
    rotaps = io
    xin = io["xin"]
    xout = io["xout"]
    wqkv = io["wqkv"]
    wo = io["wo"]
    qs2 = io["qs2"]
    ks2 = io["ks2"]
    I128B, R128T, E8, E8B = csb["I128B"], csb["R128T"], csb["E8SEL"], csb["E8B"]

    xs = SB("xs", [128, NCH, 2048], BF16)
    Ob = SB("Ob", [128, NCH, 2048], BF16)
    qf = SB("qf", [128, 4, 2048], BF16)
    kf = SB("kf", [128, 4, 2048], BF16)
    vtm = SB("vtm", [128, 16, 512], BF16)
    wbuf = SB("wbuf", [128, NCH, 1024], BF16)
    tabq1 = SB("tabq1", [128, 2048], BF16)
    tabq2 = SB("tabq2", [128, 2048], BF16)
    tabk1 = SB("tabk1", [128, 2048], BF16)
    tabk2 = SB("tabk2", [128, 2048], BF16)
    qraw = SB("qraw", [128, 4, TT], BF16)
    kraw = SB("kraw", [128, 4, TT], BF16)
    tqb = SB("tqb", [128, TT], BF16)
    csqb = SB("csqb", [128, TT], BF16)
    usqb = SB("usqb", [128, TT], BF16)
    sqs0 = SB("sqs0", [128, TT], BF16)
    sqs1 = SB("sqs1", [128, TT], BF16)
    nqf = SB("nqf", [8, TT], F32)
    nq8 = SB("nq8", [8, TT], BF16)
    nk8 = SB("nk8", [8, TT], BF16)
    sS0 = SB("sS0", [128, TT], F32)
    sS1 = SB("sS1", [128, TT], F32)
    Pw = SB("Pw", [128, 8, 384], BF16)
    PwN = SB("PwN", [128, 8, 384], BF16)
    PTw = SB("PTw", [128, 24, 128], BF16)
    negm0 = SB("negm0", [128, 1], F32)
    negm1 = SB("negm1", [128, 1], F32)
    zrow = SB("zrow", [128, 1], F32)
    invzq = SB("invzq", [128, 1], F32)
    junk1 = SB("junk1", [128, 1], F32)
    qs2s = SB("qs2s", [128, 1], F32)
    ks2s = SB("ks2s", [128, 1], F32)
    xt = SB("xt", [128, NCH, TT], F32)

    pg0 = PS("pg0", [128, TT], F32)
    pg1 = PS("pg1", [128, TT], F32)
    pg2 = PS("pg2", [128, TT], F32)
    pg3 = PS("pg3", [128, TT], F32)
    pq8 = PS("pq8", [8, TT], F32)
    pk8 = PS("pk8", [8, TT], F32)
    pav = PS("pav", [64, 128], F32)

    sems = {}
    for nm in ["ld", "st", "mm", "ev", "tt", "ex", "pn", "tr", "av", "oe", "wld"]:
        sems[nm] = SEM(nm)
    C = {k: 0 for k in sems}
    ld, st, mmS, evS, ttS = (sems[k] for k in ["ld", "st", "mm", "ev", "tt"])
    exS, pnS, trS, avS, oeS, wldsem = (
        sems[k] for k in ["ex", "pn", "tr", "av", "oe", "wld"])

    def wload(g, dst_ap, src_ap, slow=False):
        g.dma_start(out=dst_ap, in_=src_ap,
                    allow_slow_non_contiguous=slow).then_inc(wldsem, 16)
        C["wld"] += 16

    with nc.Block() as blk:
        @blk.gpsimd
        def _(g):
            wload(g, qs2s.ap(), qs2)
            wload(g, ks2s.ap(), ks2)
            g.wait_ge(wldsem, C["wld"])
    for tab, rot, col in ((tabq1, "ROT_CSQ", qs2s), (tabq2, "ROT_UQ", qs2s),
                          (tabk1, "ROT_CSK", ks2s), (tabk2, "ROT_UK", ks2s)):
        with nc.Block() as blk:
            @blk.sync
            def _(sy):
                sy.dma_start(out=xs.ap()[:, 0, :], in_=rotaps[rot]
                             ).then_inc(ld, 16)
                C["ld"] += 16
                sy.wait_ge(ld, C["ld"])
        with nc.Block() as blk:
            @blk.vector
            def _(v):
                v.tensor_scalar(tab.ap(), xs.ap()[:, 0, :], col.ap(), None,
                                ALU.mult)

    for b in range(B):
        tb0 = b * T
        with nc.Block() as blk:
            @blk.gpsimd
            def _(g):
                wload(g, xs.ap()[:, 0:4, :],
                      xin[0:4, :, tb0:tb0 + T].rearrange("c p t -> p c t"))
                wload(g, xs.ap()[:, 4:8, 1:T],
                      xin[4:8, :, tb0:tb0 + T - 1].rearrange("c p t -> p c t"))
                g.wait_ge(wldsem, C["wld"])
            @blk.vector
            def _(v):
                pass
        with nc.Block() as blk:
            @blk.vector
            def _(v):
                v.memset(xs.ap()[:, 4:8, 0:1], 0.0)
        for hg in range(2 if PH >= 9 else 1):
            if b > 0 and PH < 9:
                break
            with nc.Block() as blk:
                @blk.gpsimd
                def _(g):
                    wload(g, wbuf.ap()[:, :, 0:512],
                          wqkv[:, hg * 512:(hg + 1) * 512]
                          .rearrange("(c p) o -> p c o", p=128))
                    wload(g, wbuf.ap()[:, :, 512:1024],
                          wqkv[:, D + hg * 512:D + (hg + 1) * 512]
                          .rearrange("(c p) o -> p c o", p=128))
                    g.wait_ge(wldsem, C["wld"])
            for it in range(4 if PH >= 3 else 0):
                tsl = slice(it * TT, (it + 1) * TT)
                MM0, EV0 = C["mm"], C["ev"]
                qkocs = [("q", oc) for oc in range(4)] + \
                        [("k", oc) for oc in range(4)]
                MM0, EV0 = C["mm"], C["ev"]
                with nc.Block() as blk:
                    @blk.tensor
                    def _(te):
                        for idx, (which, oc) in enumerate(qkocs):
                            pb = (pg0, pg1)[idx % 2]
                            if idx >= 2:
                                te.wait_ge(evS, EV0 + idx - 1)
                            wcol = (0 if which == "q" else 512) + oc * 128
                            for c in range(NCH):
                                ins = te.matmul(pb[:],
                                                wbuf.ap()[:, c, wcol:wcol + 128],
                                                xs.ap()[:, c, tsl],
                                                start=(c == 0), stop=(c == NCH - 1))
                            ins.then_inc(mmS, 1)
                    @blk.scalar
                    def _(s):
                        for idx, (which, oc) in enumerate(qkocs):
                            pb = (pg0, pg1)[idx % 2]
                            dst = qraw if which == "q" else kraw
                            s.wait_ge(mmS, MM0 + idx + 1)
                            s.activation(dst.ap()[:, oc, :], pb[:], AF.Copy
                                         ).then_inc(evS, 1)
                C["mm"] += 8
                C["ev"] += 8
                for which, src, pq in (("q", qraw, pq8), ("k", kraw, pk8)):
                    MM0, EV0 = C["mm"], C["ev"]
                    with nc.Block() as blk:
                        @blk.scalar
                        def _(s):
                            for oc in range(4):
                                sqd = (sqs0, sqs1)[oc % 2]
                                if oc >= 2:
                                    s.wait_ge(mmS, MM0 + oc - 1)
                                s.activation(sqd.ap(), src.ap()[:, oc, :],
                                             AF.Square).then_inc(evS, 1)
                        @blk.tensor
                        def _(te):
                            for oc in range(4):
                                sqd = (sqs0, sqs1)[oc % 2]
                                te.wait_ge(evS, EV0 + oc + 1)
                                te.matmul(pq[:], E8.ap()[:, oc, :], sqd.ap(),
                                          start=(oc == 0), stop=(oc == 3)
                                          ).then_inc(mmS, 1)
                    C["mm"] += 4
                    C["ev"] += 4
                if PH < 4:
                    continue
                with nc.Block() as blk:
                    @blk.scalar
                    def _(s):
                        s.activation(nqf.ap(), pq8[:], AF.Sqrt)
                with nc.Block() as blk:
                    @blk.vector
                    def _(v):
                        v.tensor_scalar_max(nqf.ap(), nqf.ap(), 1e-12)
                        v.drain()
                        with nc.allow_low_precision(
                                reason="l2norm scale in bf16 is within bar"):
                            v.reciprocal(nq8.ap(), nqf.ap())
                with nc.Block() as blk:
                    @blk.scalar
                    def _(s):
                        s.activation(nqf.ap(), pk8[:], AF.Sqrt)
                with nc.Block() as blk:
                    @blk.vector
                    def _(v):
                        v.tensor_scalar_max(nqf.ap(), nqf.ap(), 1e-12)
                        v.drain()
                        with nc.allow_low_precision(
                                reason="l2norm scale in bf16 is within bar"):
                            v.reciprocal(nk8.ap(), nqf.ap())
                MM0, EV0, TT0 = C["mm"], C["ev"], C["tt"]
                rots = ([("q", cl) for cl in range(4)] +
                        [("k", cl) for cl in range(4)]) if PH >= 5 else []
                with nc.Block() as blk:
                    @blk.tensor
                    def _(te):
                        for idx, (which, cl) in enumerate(rots):
                            n8 = nq8 if which == "q" else nk8
                            if idx > 0:
                                te.wait_ge(ttS, TT0 + idx)
                            te.matmul(pg3[:], E8B.ap()[:, cl, :], n8.ap(),
                                      start=True, stop=True).then_inc(mmS, 1)
                            te.wait_ge(ttS, TT0 + idx + 1)
                            if idx > 0:
                                te.wait_ge(evS, EV0 + idx)
                            te.matmul(pg2[:], I128B.ap(), csqb.ap(),
                                      start=True, stop=False)
                            te.matmul(pg2[:], R128T.ap(), usqb.ap(),
                                      start=False, stop=True).then_inc(mmS, 1)
                    @blk.vector
                    def _(v):
                        for idx, (which, cl) in enumerate(rots):
                            src = qraw if which == "q" else kraw
                            tb1 = tabq1 if which == "q" else tabk1
                            tb2 = tabq2 if which == "q" else tabk2
                            v.wait_ge(mmS, MM0 + 2 * idx + 1)
                            v.tensor_tensor(tqb.ap(), src.ap()[:, cl, :], pg3[:],
                                            ALU.mult)
                            v.tensor_tensor(csqb.ap(), tqb.ap(),
                                            tb1.ap()[:, tsl], ALU.mult)
                            v.tensor_tensor(usqb.ap(), tqb.ap(),
                                            tb2.ap()[:, tsl], ALU.mult
                                            ).then_inc(ttS, 1)
                    @blk.scalar
                    def _(s):
                        for idx, (which, cl) in enumerate(rots):
                            dst = qf if which == "q" else kf
                            s.wait_ge(mmS, MM0 + 2 * idx + 2)
                            s.activation(dst.ap()[:, cl, tsl], pg2[:], AF.Copy
                                         ).then_inc(evS, 1)
                C["mm"] += 16
                C["ev"] += 8
                C["tt"] += 8
            if PH < 6:
                continue
            with nc.Block() as blk:
                @blk.gpsimd
                def _(g):
                    wload(g, wbuf.ap()[:, :, 0:512],
                          wqkv[:, 2 * D + hg * 512:2 * D + (hg + 1) * 512]
                          .rearrange("(c p) o -> p c o", p=128))
                    g.wait_ge(wldsem, C["wld"])
            for it in range(4):
                MM0, EV0 = C["mm"], C["ev"]
                with nc.Block() as blk:
                    @blk.tensor
                    def _(te):
                        for tb in range(4):
                            pb = (pg0, pg1)[tb % 2]
                            if tb >= 2:
                                te.wait_ge(evS, EV0 + tb - 1)
                            for c in range(NCH):
                                ins = te.matmul(
                                    pb[:],
                                    xs.ap()[:, c, it * TT + tb * 128:
                                            it * TT + (tb + 1) * 128],
                                    wbuf.ap()[:, c, 0:512],
                                    start=(c == 0), stop=(c == NCH - 1))
                            ins.then_inc(mmS, 1)
                    @blk.scalar
                    def _(s):
                        for tb in range(4):
                            pb = (pg0, pg1)[tb % 2]
                            s.wait_ge(mmS, MM0 + tb + 1)
                            s.activation(vtm.ap()[:, it * 4 + tb, :], pb[:],
                                         AF.Copy).then_inc(evS, 1)
                C["mm"] += 4
                C["ev"] += 4
            if IATTN_DEBUG and b == 0 and hg == 0:
                with nc.Block() as blk:
                    @blk.sync
                    def _(sy):
                        for nm, h in (("d_qf", qf), ("d_kf", kf), ("d_vtm", vtm),
                                      ("d_xs", xs), ("d_nq", nq8)):
                            sy.dma_start(out=dbg[nm], in_=h.ap()).then_inc(st, 16)
                            C["st"] += 16
                        sy.wait_ge(st, C["st"])
            for (qt0, kw0, kwid, mn) in (iattn_windows() if PH >= 7 else []):
                mask = csb[mn]
                nkc = kwid // 128
                MM0, TT0, EX0, PN0 = C["mm"], C["tt"], C["ex"], C["pn"]
                with nc.Block() as blk:
                    @blk.tensor
                    def _(te):
                        for hl in range(8):
                            cl, half = hl // 2, hl % 2
                            pb = (pg0, pg1)[hl % 2]
                            if hl >= 2:
                                te.wait_ge(ttS, TT0 + hl - 1)
                            te.matmul(pb[0:128, 0:kwid],
                                      qf.ap()[half * 64:(half + 1) * 64, cl,
                                              qt0:qt0 + 128],
                                      kf.ap()[half * 64:(half + 1) * 64, cl,
                                              kw0:kw0 + kwid],
                                      start=True, stop=True).then_inc(mmS, 1)
                    @blk.vector
                    def _(v):
                        for hl in range(8):
                            pb = (pg0, pg1)[hl % 2]
                            ss = (sS0, sS1)[hl % 2]
                            ng = (negm0, negm1)[hl % 2]
                            v.wait_ge(mmS, MM0 + hl + 1)
                            v.scalar_tensor_tensor(ss.ap()[:, 0:kwid],
                                                   pb[0:128, 0:kwid], 8.0,
                                                   mask.ap()[:, 0:kwid],
                                                   ALU.mult, ALU.add)
                            v.reduce_max(ng.ap(), ss.ap()[:, 0:kwid],
                                         axis=AX.X, negate=True)
                            v.tensor_copy(junk1.ap(), ng.ap()).then_inc(ttS, 1)
                            if hl > 0:
                                v.wait_ge(exS, EX0 + hl)
                                v.reduce_sum(zrow.ap(),
                                             Pw.ap()[:, hl - 1, 0:kwid], axis=AX.X)
                                v.drain()
                                v.reciprocal(invzq.ap(), zrow.ap())
                                v.drain()
                                v.tensor_scalar(PwN.ap()[:, hl - 1, 0:kwid],
                                                Pw.ap()[:, hl - 1, 0:kwid],
                                                invzq.ap(), None, ALU.mult
                                                ).then_inc(pnS, 1)
                        v.wait_ge(exS, EX0 + 8)
                        v.reduce_sum(zrow.ap(), Pw.ap()[:, 7, 0:kwid], axis=AX.X)
                        v.drain()
                        v.reciprocal(invzq.ap(), zrow.ap())
                        v.drain()
                        v.tensor_scalar(PwN.ap()[:, 7, 0:kwid],
                                        Pw.ap()[:, 7, 0:kwid],
                                        invzq.ap(), None, ALU.mult
                                        ).then_inc(pnS, 1)
                    @blk.scalar
                    def _(s):
                        for hl in range(8):
                            ss = (sS0, sS1)[hl % 2]
                            ng = (negm0, negm1)[hl % 2]
                            s.wait_ge(ttS, TT0 + hl + 1)
                            s.activation(Pw.ap()[:, hl, 0:kwid],
                                         ss.ap()[:, 0:kwid], AF.Exp,
                                         bias=ng.ap()).then_inc(exS, 1)
                C["mm"] += 8
                C["tt"] += 8
                C["ex"] += 8
                C["pn"] += 8
                if PH < 8:
                    continue
                with nc.Block() as blk:
                    @blk.sync
                    def _(sy):
                        if kwid == 384:
                            sy.dma_start_transpose(
                                PTw.ap()[:, 0:8 * nkc, :],
                                PwN.ap().rearrange("p h k -> p (h k)")
                                ).then_inc(trS, 16)
                            C["tr"] += 16
                        else:
                            for hl in range(8):
                                sy.dma_start_transpose(
                                    PTw.ap()[:, hl * nkc:(hl + 1) * nkc, :],
                                    PwN.ap()[:, hl, 0:kwid]).then_inc(trS, 16)
                                C["tr"] += 16
                        sy.wait_ge(trS, C["tr"])
                if PH < 9:
                    continue
                AV0, OE0 = C["av"], C["oe"]
                with nc.Block() as blk:
                    @blk.tensor
                    def _(te):
                        for hl in range(8):
                            cl, half = hl // 2, hl % 2
                            if hl > 0:
                                te.wait_ge(oeS, OE0 + hl)
                            for kc in range(nkc):
                                ins = te.matmul(
                                    pav[:],
                                    vtm.ap()[:, kw0 // 128 + kc,
                                             hl * 64:(hl + 1) * 64],
                                    PTw.ap()[:, hl * nkc + kc, :],
                                    start=(kc == 0), stop=(kc == nkc - 1))
                            ins.then_inc(avS, 1)
                    @blk.scalar
                    def _(s):
                        for hl in range(8):
                            gch = (hg * 8 + hl) // 2
                            half = hl % 2
                            s.wait_ge(avS, AV0 + hl + 1)
                            s.activation(Ob.ap()[half * 64:(half + 1) * 64, gch,
                                                 qt0:qt0 + 128],
                                         pav[:], AF.Copy).then_inc(oeS, 1)
                C["av"] += 8
                C["oe"] += 8
        if IATTN_DEBUG and b == 0:
            with nc.Block() as blk:
                @blk.sync
                def _(sy):
                    sy.dma_start(out=dbg["d_Ob"], in_=Ob.ap()).then_inc(st, 16)
                    C["st"] += 16
                    sy.wait_ge(st, C["st"])
        if PH < 9:
            continue
        with nc.Block() as blk:
            @blk.gpsimd
            def _(g):
                wload(g, wbuf.ap()[:, :, 0:1024],
                      wo.rearrange("(c p) o -> p c o", p=128))
                g.wait_ge(wldsem, C["wld"])
        for it in range(4):
            t0 = tb0 + it * TT
            with nc.Block() as blk:
                @blk.sync
                def _(sy):
                    sy.dma_start(out=xt.ap(), in_=x_dram_tile(xin, t0, TT)
                                 ).then_inc(ld, 16)
                    C["ld"] += 16
                    sy.wait_ge(ld, C["ld"])
            MM0, EV0 = C["mm"], C["ev"]
            with nc.Block() as blk:
                @blk.tensor
                def _(te):
                    for oc in range(NCH):
                        pb = (pg0, pg1)[oc % 2]
                        if oc >= 2:
                            te.wait_ge(evS, EV0 + oc - 1)
                        for c in range(NCH):
                            ins = te.matmul(pb[:],
                                            wbuf.ap()[:, c, oc * 128:(oc + 1) * 128],
                                            Ob.ap()[:, c, it * TT:(it + 1) * TT],
                                            start=(c == 0), stop=(c == NCH - 1))
                        ins.then_inc(mmS, 1)
                @blk.vector
                def _(v):
                    for oc in range(NCH):
                        pb = (pg0, pg1)[oc % 2]
                        v.wait_ge(mmS, MM0 + oc + 1)
                        v.tensor_tensor(xt.ap()[:, oc, :], pb[:],
                                        xt.ap()[:, oc, :], ALU.add
                                        ).then_inc(evS, 1)
            C["mm"] += NCH
            C["ev"] += NCH
            with nc.Block() as blk:
                @blk.sync
                def _(sy):
                    sy.dma_start(out=x_dram_tile(xout, t0, TT), in_=xt.ap()
                                 ).then_inc(st, 16)
                    C["st"] += 16
                    sy.wait_ge(st, C["st"])
    _finish()





def emit_final(nc, csb, P, io):
    stk, SB, PS, SEM, _finish = _stage_allocs(nc, P)
    xin = io["xin"]
    xiin = io["xiin"]
    m0in = io["m0in"]
    xout = io["xout"]
    xt = SB("xt", [128, NCH, TT], F32)
    fit = SB("fit", [128, NCH, TT], F32)
    yb = SB("yb", [128, NCH, TT], BF16)
    sqb = SB("sqb", [128, NCH, TT], BF16)
    tmpb = SB("tmpb", [128, NCH, TT], F32)
    m0sb = SB("m0sb", [1, TOK], F32)
    rowA = SB("rowA", [1, TT], F32)
    rowB = SB("rowB", [1, TT], F32)
    rowC = SB("rowC", [1, TT], F32)
    pg0 = PS("pg0", [128, TT], F32)
    pg1 = PS("pg1", [128, TT], F32)
    pl1 = PS("pl1", [1, TT], F32)
    pl2 = PS("pl2", [1, TT], F32)
    ld = SEM("ld")
    st = SEM("st")
    with nc.Block() as blk:
        @blk.sync
        def _(sy):
            sy.dma_start(out=m0sb.ap(), in_=m0in).then_inc(ld, 16)
            sy.wait_ge(ld, 16)
    for i in range(NTILE):
        t0 = i * TT
        with nc.Block() as blk:
            @blk.sync
            def _(sy):
                sy.dma_start(out=xt.ap(), in_=x_dram_tile(xin, t0, TT)
                             ).then_inc(ld, 16)
                sy.dma_start(out=fit.ap(), in_=x_dram_tile(xiin, t0, TT)
                             ).then_inc(ld, 16)
                sy.wait_ge(ld, 16 + 32 * (i + 1))
        with nc.Block() as blk:
            @blk.vector
            def _(v):
                v.tensor_copy(yb.ap(), fit.ap())
        emit_lnwrap(nc, csb, yb, xt, sqb, rowA, rowB, rowC,
                    pl1[:], pl2[:], pg0, pg1, tmpb,
                    extra_row=m0sb.ap()[:, t0:t0 + TT])
        with nc.Block() as blk:
            @blk.sync
            def _(sy):
                sy.dma_start(out=x_dram_tile(xout, t0, TT), in_=xt.ap()
                             ).then_inc(st, 16)
                sy.wait_ge(st, 16 * (i + 1))
    _finish()



def fm_shards(x):
    shards = []
    for c in range(NCORES):
        xc = np.asarray(x[c * B:(c + 1) * B]).reshape(TOK, D).T
        shards.append(np.ascontiguousarray(xc).reshape(NCH, 128, TOK)
                      .astype(np.float32))
    return shards


def unshard_out(shards):
    out = np.empty((NCORES * B, T, D), dtype=np.float32)
    for c in range(NCORES):
        out[c * B:(c + 1) * B] = shards[c].reshape(D, TOK).T.reshape(B, T, D)
    return out


_STAGE_CACHE = {}

STAGE_CONSTS = {
    "prep": ["ONESF"],
    "A0": ["ONESB", "ONESF"],
    "F0": ["ONESB", "ONESF"],
    "A1": ["ONESB", "ONESF", "I16F", "MASKTRI", "E8SEL"],
    "hprep": ["ONESB", "ONESF"],
    "iffn": ["ONESB", "ONESF"],
    "final": ["ONESB", "ONESF"],
    "iattn": ["I128B", "R128T", "E8SEL", "E8B",
              "ROT_CSQ", "ROT_UQ", "ROT_CSK", "ROT_UK",
              "MSK_SUB0", "MSK_SUB1", "MSK_W0S0", "MSK_W0S1"],
}


def get_stage(name):
    if name not in _STAGE_CACHE:
        _STAGE_CACHE[name] = globals()[f"build_{name}"]()
    return _STAGE_CACHE[name]


def run_stage(name, per_core_maps):
    nc = get_stage(name)
    cmaps = []
    for m in per_core_maps:
        mm = dict(m)
        for k in STAGE_CONSTS[name]:
            mm[k] = CONSTS[k]
        cmaps.append(mm)
    res = run_bass_kernel_spmd(nc, cmaps, core_ids=list(range(NCORES)))
    return res.results





XSHP = [NCH, 128, TOK]

STAGE_IO = {
    "prep": {"in": [("xin", XSHP)], "out": [("xout", XSHP), ("m0out", [1, TOK])]},
    "A0": {"in": [("xin", XSHP)] + [(w, [D, D]) for w in ["Wq", "Wk", "Wv", "Wo"]]
           + [(b, [1, D]) for b in ["bq", "bk", "bv", "bo"]],
           "out": [("xout", XSHP)]},
    "F0": {"in": [("xin", XSHP)] + [(w, [D, D]) for w in ["WrT", "WkT", "WvT"]]
           + [("mur", [1, D]), ("muk", [1, D])],
           "out": [("xout", XSHP)]},
    "A1": {"in": [("xin", XSHP), ("wqkv", [D, 3 * D]), ("bqkv", [1, 3 * D]),
                  ("wout", [D, D]), ("bout", [1, D])],
           "out": [("xout", XSHP)]},
    "hprep": {"in": [("xin", XSHP)], "out": [("xiout", XSHP)]},
    "iattn": {"in": [("xin", XSHP), ("wqkv", [D, 3 * D]), ("wo", [D, D]),
                     ("qs2", [128, 1]), ("ks2", [128, 1])],
              "out": [("xout", XSHP)]},
    "iffn": {"in": [("xin", XSHP), ("w1a", [D, HID]), ("w1g", [D, HID]),
                    ("b1a", [128, NHP]), ("b1g", [128, NHP]),
                    ("w2p", [NHP * 128, D]), ("b2", [128, NCH]),
                    ("lng", [128, NCH]), ("lnb", [128, NCH])],
              "out": [("xout", XSHP)]},
    "final": {"in": [("xin", XSHP), ("xiin", XSHP), ("m0in", [1, TOK])],
              "out": [("xout", XSHP)]},
}

ROTNAMES = ["ROT_CSQ", "ROT_UQ", "ROT_CSK", "ROT_UK"]


def _mk_build(stage):
    def build():
        nc = new_nc()
        caps = declare_consts(nc, [k for k in STAGE_CONSTS[stage]
                                   if k not in ROTNAMES])
        csb = load_consts(nc, caps)
        io = {}
        for nm, shp in STAGE_IO[stage]["in"]:
            io[nm] = nc.dram_tensor(nm, shp, F32, kind="ExternalInput").ap()
        for nm, shp in STAGE_IO[stage]["out"]:
            io[nm] = nc.dram_tensor(nm, shp, F32, kind="ExternalOutput").ap()
        if stage == "iattn":
            io.update(declare_consts(nc, ROTNAMES))
        globals()[f"emit_{stage}"](nc, csb, "s_", io)
        return nc
    return build


for _st in STAGE_IO:
    globals()[f"build_{_st}"] = _mk_build(_st)


ALLCONSTS = ["ONESB", "ONESF", "I16F", "MASKTRI", "E8SEL", "I128B", "R128T",
             "E8B", "MSK_SUB0", "MSK_SUB1", "MSK_W0S0", "MSK_W0S1"]


GROUPS = {
    "G1": ["prep", "A0", "F0"],
    "G2": ["A1", "hprep"],
    "GB": ["iattn", "iffn"],
}
KEEP_XOUT = {"G2": {"A1"}}


def group_const_names(g):
    names = []
    for st in GROUPS[g]:
        for k in STAGE_CONSTS[st]:
            if k not in ROTNAMES and k not in names:
                names.append(k)
    return names


def _mk_group_build(gname):
    stages = GROUPS[gname]

    def build():
        nc = new_nc()
        caps = declare_consts(nc, group_const_names(gname))
        csb = load_consts(nc, caps)
        rotaps = (declare_consts(nc, ROTNAMES)
                  if "iattn" in stages else {})
        chain = nc.dram_tensor("gxin", XSHP, F32, kind="ExternalInput").ap()
        for si, st in enumerate(stages):
            last = (si == len(stages) - 1)
            io = {}
            for nm, shp in STAGE_IO[st]["in"]:
                if nm == "xin":
                    io[nm] = chain
                else:
                    io[nm] = nc.dram_tensor(nm, shp, F32,
                                            kind="ExternalInput").ap()
            for nm, shp in STAGE_IO[st]["out"]:
                if nm in ("xout", "xiout") and not last:
                    if st in KEEP_XOUT.get(gname, ()):
                        nxt = nc.dram_tensor(nm, shp, F32,
                                             kind="ExternalOutput").ap()
                    else:
                        nxt = nc.dram_tensor(f"gchain{si}", shp, F32,
                                             kind="Internal").ap()
                    io[nm] = nxt
                    chain = nxt
                else:
                    io[nm] = nc.dram_tensor(nm, shp, F32,
                                            kind="ExternalOutput").ap()
            if st == "iattn":
                io.update(rotaps)
            globals()[f"emit_{st}"](nc, csb, f"{st}_", io)
        return nc
    return build


_GROUP_CACHE = {}


def run_group(gname, per_core_maps):
    if gname not in _GROUP_CACHE:
        _GROUP_CACHE[gname] = _mk_group_build(gname)()
    nc = _GROUP_CACHE[gname]
    cmaps = []
    cns = group_const_names(gname)
    if "iattn" in GROUPS[gname]:
        cns = cns + ROTNAMES
    for m in per_core_maps:
        mm = dict(m)
        for k in cns:
            mm[k] = CONSTS[k]
        cmaps.append(mm)
    res = run_bass_kernel_spmd(nc, cmaps, core_ids=list(range(NCORES)))
    return res.results


def build_fused():
    nc = new_nc()
    caps = declare_consts(nc, ALLCONSTS)
    csb = load_consts(nc, caps)
    rotaps = declare_consts(nc, ROTNAMES)
    x0 = nc.dram_tensor("x0", XSHP, F32, kind="ExternalInput").ap()
    out = nc.dram_tensor("out", XSHP, F32, kind="ExternalOutput").ap()
    XA = nc.dram_tensor("XresA", XSHP, F32, kind="Internal").ap()
    XB = nc.dram_tensor("XresB", XSHP, F32, kind="Internal").ap()
    M0 = nc.dram_tensor("M0res", [1, TOK], F32, kind="Internal").ap()

    def w(name, shp):
        return nc.dram_tensor(name, shp, F32, kind="ExternalInput").ap()

    emit_prep(nc, csb, "p_", {"xin": x0, "xout": XA, "m0out": M0})
    emit_A0(nc, csb, "a0_", {"xin": XA, "xout": XB,
            **{n: w("a0" + n, [D, D]) for n in ["Wq", "Wk", "Wv", "Wo"]},
            **{n: w("a0" + n, [1, D]) for n in ["bq", "bk", "bv", "bo"]}})
    emit_F0(nc, csb, "f0_", {"xin": XB, "xout": XA,
            **{n: w("f0" + n, [D, D]) for n in ["WrT", "WkT", "WvT"]},
            **{n: w("f0" + n, [1, D]) for n in ["mur", "muk"]}})
    emit_A1(nc, csb, "a1_", {"xin": XA, "xout": XB,
            "wqkv": w("a1wqkv", [D, 3 * D]), "bqkv": w("a1bqkv", [1, 3 * D]),
            "wout": w("a1wout", [D, D]), "bout": w("a1bout", [1, D])})
    XC = nc.dram_tensor("XresC", XSHP, F32, kind="Internal").ap()
    emit_hprep(nc, csb, "h_", {"xin": XB, "xiout": XA})
    cur, nxt = XA, XC
    for lyr in range(6):
        emit_iattn(nc, csb, f"ia{lyr}_", {"xin": cur, "xout": nxt,
                   "wqkv": w(f"ia{lyr}wqkv", [D, 3 * D]),
                   "wo": w(f"ia{lyr}wo", [D, D]),
                   "qs2": w(f"ia{lyr}qs2", [128, 1]),
                   "ks2": w(f"ia{lyr}ks2", [128, 1]), **rotaps})
        cur, nxt = nxt, cur
        emit_iffn(nc, csb, f"if{lyr}_", {"xin": cur, "xout": nxt,
                  "w1a": w(f"if{lyr}w1a", [D, HID]),
                  "w1g": w(f"if{lyr}w1g", [D, HID]),
                  "b1a": w(f"if{lyr}b1a", [128, NHP]),
                  "b1g": w(f"if{lyr}b1g", [128, NHP]),
                  "w2p": w(f"if{lyr}w2p", [NHP * 128, D]),
                  "b2": w(f"if{lyr}b2", [128, NCH]),
                  "lng": w(f"if{lyr}lng", [128, NCH]),
                  "lnb": w(f"if{lyr}lnb", [128, NCH])})
        cur, nxt = nxt, cur
    emit_final(nc, csb, "fi_", {"xin": XB, "xiin": cur, "m0in": M0, "xout": out})
    return nc


FUSED = False
_FUSED_CACHE = []


def kernel_fused(inputs):
    if not _FUSED_CACHE:
        _FUSED_CACHE.append(build_fused())
    nc = _FUSED_CACHE[0]
    X = fm_shards(inputs["L0_attn_X_embed"])
    base = {}
    for k in ALLCONSTS + ROTNAMES:
        base[k] = CONSTS[k]
    for n in ["Wq", "Wk", "Wv", "Wo"]:
        base["a0" + n] = inputs["L0_attn_" + n]
    for n in ["bq", "bk", "bv", "bo"]:
        base["a0" + n] = inputs["L0_attn_" + n].reshape(1, -1)
    base["f0WrT"] = np.ascontiguousarray(inputs["L0_ffn_W_r"].T)
    base["f0WkT"] = np.ascontiguousarray(inputs["L0_ffn_W_k"].T)
    base["f0WvT"] = np.ascontiguousarray(inputs["L0_ffn_W_v"].T)
    base["f0mur"] = inputs["L0_ffn_mu_r"].reshape(1, -1)
    base["f0muk"] = inputs["L0_ffn_mu_k"].reshape(1, -1)
    base["a1wqkv"] = inputs["L1_attn_w_qkv"]
    base["a1bqkv"] = inputs["L1_attn_b_qkv"].reshape(1, -1)
    base["a1wout"] = inputs["L1_attn_w_out"]
    base["a1bout"] = inputs["L1_attn_b_out"].reshape(1, -1)
    for lyr in range(6):
        am = _iattn_maps(inputs, lyr)
        base[f"ia{lyr}wqkv"] = am["wqkv"]
        base[f"ia{lyr}wo"] = am["wo"]
        base[f"ia{lyr}qs2"] = am["qs2"]
        base[f"ia{lyr}ks2"] = am["ks2"]
        fmp = _iffn_maps(inputs, lyr)
        for k2, v2 in fmp.items():
            base[f"if{lyr}{k2}"] = v2
    maps = [{**base, "x0": X[c]} for c in range(NCORES)]
    res = run_bass_kernel_spmd(nc, maps, core_ids=list(range(NCORES)))
    return unshard_out([res.results[c]["out"] for c in range(NCORES)])


def _iattn_maps(inputs, lyr):
    return {
        "wqkv": np.ascontiguousarray(inputs["L1_ffn_Wqkv"][lyr]),
        "wo": np.ascontiguousarray(inputs["L1_ffn_Wo"][lyr]),
        "qs2": np.ascontiguousarray(
            np.tile(np.asarray(inputs["L1_ffn_q_scale"][lyr]), 2).reshape(128, 1)),
        "ks2": np.ascontiguousarray(
            np.tile(np.asarray(inputs["L1_ffn_k_scale"][lyr]), 2).reshape(128, 1)),
    }


def _iffn_maps(inputs, lyr):
    W1 = np.asarray(inputs["L1_ffn_W1"][lyr])
    W2 = np.asarray(inputs["L1_ffn_W2"][lyr])
    b1 = np.asarray(inputs["L1_ffn_b1"][lyr])

    def padbias(v):
        vp = np.zeros(NHP * 128, np.float32)
        vp[:HID] = v
        return np.ascontiguousarray(vp.reshape(NHP, 128).T)
    w2p = np.zeros((NHP * 128, D), np.float32)
    w2p[:HID] = W2
    col = lambda v: np.ascontiguousarray(np.asarray(v).reshape(NCH, 128).T)
    return {
        "w1a": np.ascontiguousarray(W1[:, :HID]),
        "w1g": np.ascontiguousarray(W1[:, HID:]),
        "b1a": padbias(b1[:HID]), "b1g": padbias(b1[HID:]),
        "w2p": w2p,
        "b2": col(inputs["L1_ffn_b2"][lyr]),
        "lng": col(inputs["L1_ffn_ln_g"][lyr]),
        "lnb": col(inputs["L1_ffn_ln_b"][lyr]),
    }


def kernel(**inputs):
    inputs = {k: np.asarray(v) for k, v in inputs.items()}
    if FUSED:
        return kernel_fused(inputs)
    X = fm_shards(inputs["L0_attn_X_embed"])

    a0m = {"Wq": inputs["L0_attn_Wq"], "Wk": inputs["L0_attn_Wk"],
           "Wv": inputs["L0_attn_Wv"], "Wo": inputs["L0_attn_Wo"],
           "bq": inputs["L0_attn_bq"].reshape(1, -1),
           "bk": inputs["L0_attn_bk"].reshape(1, -1),
           "bv": inputs["L0_attn_bv"].reshape(1, -1),
           "bo": inputs["L0_attn_bo"].reshape(1, -1)}
    f0m = {"WrT": np.ascontiguousarray(inputs["L0_ffn_W_r"].T),
           "WkT": np.ascontiguousarray(inputs["L0_ffn_W_k"].T),
           "WvT": np.ascontiguousarray(inputs["L0_ffn_W_v"].T),
           "mur": inputs["L0_ffn_mu_r"].reshape(1, -1),
           "muk": inputs["L0_ffn_mu_k"].reshape(1, -1)}
    res = run_group("G1", [{"gxin": X[c], **a0m, **f0m}
                           for c in range(NCORES)])
    X = [res[c]["xout"] for c in range(NCORES)]
    m0 = [res[c]["m0out"] for c in range(NCORES)]

    a1m = {"wqkv": inputs["L1_attn_w_qkv"],
           "bqkv": inputs["L1_attn_b_qkv"].reshape(1, -1),
           "wout": inputs["L1_attn_w_out"],
           "bout": inputs["L1_attn_b_out"].reshape(1, -1)}
    res = run_group("G2", [{"gxin": X[c], **a1m} for c in range(NCORES)])
    X = [res[c]["xout"] for c in range(NCORES)]
    XI = [res[c]["xiout"] for c in range(NCORES)]

    for lyr in range(6):
        am = _iattn_maps(inputs, lyr)
        fm = _iffn_maps(inputs, lyr)
        res = run_group("GB", [{"gxin": XI[c], **am, **fm}
                               for c in range(NCORES)])
        XI = [res[c]["xout"] for c in range(NCORES)]

    res = run_stage("final", [{"xin": X[c], "xiin": XI[c], "m0in": m0[c]}
                              for c in range(NCORES)])
    out = [res[c]["xout"] for c in range(NCORES)]
    return unshard_out(out)
